# version 02
# baseline (speedup 1.0000x reference)
import numpy as np
import ml_dtypes
import concourse.bass as bass
import concourse.mybir as mybir
from concourse.bass_utils import run_bass_kernel_spmd
from concourse.bass_core_rust import IndirectOffsetOnAxis



F32 = mybir.dt.float32
BF16 = mybir.dt.bfloat16
U32 = mybir.dt.uint32
U8 = mybir.dt.uint8
AF = mybir.ActivationFunctionType
ALU = mybir.AluOpType
AX = mybir.AxisListType

D = 1024
TOK = 2048
DC = 8
HID = 4096
HC = 32
QT = 4


class Track:
    def __init__(self, nc, name):
        self.sem = nc.alloc_semaphore(name)
        self.n = 0

    def inc(self, inst, k=1):
        inst.then_inc(self.sem, k)
        self.n += k
        return inst

    def wait(self, eng, n=None):
        eng.wait_ge(self.sem, self.n if n is None else n)


def make_tracks(nc):
    return {k: Track(nc, k) for k in ["ld", "ld2", "wl", "wl2", "tn", "sc", "ve", "st", "st2", "gp", "gp2"]}


def blk_lhsT(w_in_out):
    i, o = w_in_out.shape
    return np.ascontiguousarray(
        w_in_out.reshape(i // 128, 128, o // 128, 128).transpose(2, 1, 0, 3))


def blk_scales(s):
    return np.ascontiguousarray(s.reshape(-1, 128).T)



def emit_proj(nc, block, T, w_dram, n_oc, n_ic, rhs_tile, wbuf, psums, psum_free,
              evac_fn, evac_eng, evac_track, n_tt=QT, nbuf=2, extra_mm_wait=(),
              flush_fn=None, after_evac=None, evac_prewait=None):
    tn = T["tn"]
    wlt = [T["wl"], T["wl2"]]
    ev = evac_track
    blk_w = n_ic * 128
    wtile = lambda par, ic: wbuf[:, par * blk_w + ic * 128: par * blk_w + (ic + 1) * 128]
    tn_after = {}
    wl_after = {}
    ev_after = {}

    tn_at_entry = tn.n

    @block.gpsimd
    def _(gpsimd):
        if tn_at_entry:
            tn.wait(gpsimd, tn_at_entry)
        for oc in range(min(nbuf, n_oc)):
            wlt[oc % 2].inc(gpsimd.dma_start(
                out=wbuf[:, (oc % nbuf) * blk_w:((oc % nbuf) + 1) * blk_w],
                in_=w_dram[oc].rearrange("p ic o -> p (ic o)")), 16)
            wl_after[oc] = wlt[oc % 2].n

    for oc in range(n_oc):
        par = oc % nbuf

        @block.tensor
        def _(tensor, oc=oc, par=par):
            wlt[oc % 2].wait(tensor, wl_after[oc])
            if oc == 0:
                for t in extra_mm_wait:
                    t.wait(tensor)
            for tt in range(n_tt):
                slot = (oc * n_tt + tt) % len(psums)
                if slot in psum_free:
                    ftrk, fn = psum_free[slot]
                    ftrk.wait(tensor, fn)
                ps = psums[slot]
                for ic in range(n_ic):
                    mm = tensor.matmul(ps[:, :], wtile(par, ic),
                                       rhs_tile(ic)[:, tt * 512:(tt + 1) * 512],
                                       start=(ic == 0), stop=(ic == n_ic - 1))
                tn.inc(mm)
                tn_after[(oc, tt)] = tn.n

        eng_sel = getattr(block, evac_eng)

        @eng_sel
        def _(eng, oc=oc):
            for tt in range(n_tt):
                slot = (oc * n_tt + tt) % len(psums)
                tn.wait(eng, tn_after[(oc, tt)])
                if evac_prewait is not None:
                    evac_prewait(eng, oc, tt)
                last = evac_fn(eng, oc, tt, psums[slot])
                ev.inc(last)
                psum_free[slot] = (ev, ev.n)
                ev_after[(oc, tt)] = ev.n
                if after_evac is not None:
                    after_evac(oc, tt, ev.n)

        if flush_fn is not None:
            @block.gpsimd
            def _(gpsimd, oc=oc):
                ev.wait(gpsimd, ev_after[(oc, n_tt - 1)])
                flush_fn(gpsimd, oc)

        nxt = oc + nbuf
        if nxt < n_oc:
            @block.gpsimd
            def _(gpsimd, oc=oc, nxt=nxt):
                tn.wait(gpsimd, tn_after[(oc, n_tt - 1)])
                wlt[nxt % 2].inc(gpsimd.dma_start(
                    out=wbuf[:, (nxt % nbuf) * blk_w:((nxt % nbuf) + 1) * blk_w],
                    in_=w_dram[nxt].rearrange("p ic o -> p (ic o)")), 16)
            wl_after[nxt] = wlt[nxt % 2].n
    return tn_after, ev_after


def run_cores(nc, in_maps, **kw):
    return run_bass_kernel_spmd(nc, in_maps, core_ids=list(range(len(in_maps))), **kw)




import ml_dtypes

SC_S1, SC_S2, SC_H1, SC_H2, SC_S3 = 0, HC, 2 * HC, 3 * HC, 4 * HC


def emit_ln(nc, block, T, x_tile, xn_tile, misc, onesrow, rowbuf, sq_sb, bcast,
            psr, psB, pre_wait=(), ntok=TOK):
    tn, sc, ve = T["tn"], T["sc"], T["ve"]
    nq = ntok // 512
    psr_free = [0]
    psB_free = [0]
    for q in range(nq):
        qs = slice(q * 512, (q + 1) * 512)
        R = lambda i: rowbuf[0:1, i * 512:(i + 1) * 512]
        m_row, msq_row, var_row, sd_row, r_row, mr_row = (R(i) for i in range(6))

        @block.scalar
        def _(scalar, q=q, qs=qs):
            if q == 0:
                for t in pre_wait:
                    t.wait(scalar)
            else:
                tn.wait(scalar)
                ve.wait(scalar)
            for c in range(DC):
                inst = scalar.activation(out=sq_sb[:, c * 512:(c + 1) * 512],
                                         in_=x_tile(c)[:, qs], func=AF.Square)
            sc.inc(inst)

        @block.tensor
        def _(tensor, q=q, qs=qs):
            sc.wait(tensor)
            if psr_free[0]:
                ve.wait(tensor, psr_free[0])
            for c in range(DC):
                tensor.matmul(psr[0:1, 0:512], misc[:, 1:2], x_tile(c)[:, qs],
                              start=(c == 0), stop=(c == DC - 1))
            for c in range(DC):
                mm = tensor.matmul(psr[0:1, 512:1024], misc[:, 1:2],
                                   sq_sb[:, c * 512:(c + 1) * 512],
                                   start=(c == 0), stop=(c == DC - 1))
            tn.inc(mm)

        @block.vector
        def _(vector, q=q):
            tn.wait(vector)
            ve.inc(vector.tensor_copy(m_row, psr[0:1, 0:512]))
            ve.wait(vector)
            ve.inc(vector.tensor_tensor(msq_row, m_row, m_row, ALU.mult))
            ve.wait(vector)
            ve.inc(vector.tensor_tensor(var_row, psr[0:1, 512:1024], msq_row, ALU.subtract))
            ve.wait(vector)
            ve.inc(vector.tensor_scalar_add(var_row, var_row, 1e-5))
        psr_free[0] = ve.n

        @block.scalar
        def _(scalar, q=q):
            ve.wait(scalar)
            sc.inc(scalar.sqrt(sd_row, var_row))

        @block.vector
        def _(vector, q=q):
            sc.wait(vector)
            ve.inc(vector.reciprocal(r_row, sd_row))
            ve.wait(vector)
            ve.inc(vector.tensor_tensor(mr_row, m_row, r_row, ALU.mult))

        @block.tensor
        def _(tensor, q=q):
            ve.wait(tensor)
            if psB_free[0]:
                ve.wait(tensor, psB_free[0])
            tensor.matmul(psB[:, 0:512], onesrow[0:1, :], r_row, start=True, stop=True)
            tn.inc(tensor.matmul(psB[:, 512:1024], onesrow[0:1, :], mr_row,
                                 start=True, stop=True))

        @block.vector
        def _(vector, q=q):
            tn.wait(vector)
            vector.tensor_copy(bcast[:, 0:512], psB[:, 0:512])
            ve.inc(vector.tensor_copy(bcast[:, 512:1024], psB[:, 512:1024]))

        @block.vector
        def _(vector, q=q, qs=qs):
            for c in range(DC):
                vector.tensor_tensor(sq_sb[:, c * 512:(c + 1) * 512], x_tile(c)[:, qs],
                                     bcast[:, 0:512], ALU.mult)
                inst = vector.tensor_tensor(xn_tile(c)[:, qs],
                                            sq_sb[:, c * 512:(c + 1) * 512],
                                            bcast[:, 512:1024], ALU.subtract)
            ve.inc(inst)
        psB_free[0] = ve.n
        if q + 1 < nq:
            pass

    return


def emit_stageA(nc, t, block, T):
    ld, wl, tn, sc, ve, st, gp = (T[k] for k in ["ld", "wl", "tn", "sc", "ve", "st", "gp"])

    xn_sb = nc.alloc_sbuf_tensor("xn_sb", [128, DC * TOK], BF16)
    misc = nc.alloc_sbuf_tensor("misc", [128, 8], F32)
    scal_sb = nc.alloc_sbuf_tensor("scal_sb", [128, 4 * HC + DC], F32)
    onesrow = nc.alloc_sbuf_tensor("onesrow", [1, 128], F32)
    ident = nc.alloc_sbuf_tensor("ident", [128, 128], BF16)

    psA = nc.alloc_psum_tensor("psA", [128, 1024], F32)
    psB = nc.alloc_psum_tensor("psB", [128, 1024], F32)
    psC = nc.alloc_psum_tensor("psC", [128, 1024], F32)
    pst = nc.alloc_psum_tensor("pst", [128, 512], BF16)
    psr = psA[0:1, :]

    xn_t = lambda c: xn_sb[:, c * TOK:(c + 1) * TOK]
    psums = [psA[:, 0:512], psA[:, 512:1024], psC[:, 0:512], psC[:, 512:1024]]
    psum_free = {}

    @block.sync
    def _(sync):
        ld.inc(sync.dma_start(out=misc[:, 0:2], in_=t["consts"]), 16)
        ld.inc(sync.dma_start(out=scal_sb[:, :], in_=t["scls"]), 16)
        ld.inc(sync.dma_start(out=ident[:, :], in_=t["identbf"]), 16)

    @block.scalar
    def _(scalar):
        ld.wait(scalar)
        sc.inc(scalar.copy(out=onesrow[0:1, :], in_=misc[0:1, 0:1].broadcast_to([1, 128])))

    with (
        nc.sbuf_tensor("x_sb", [128, DC * TOK], F32) as x_sb,
        nc.sbuf_tensor("sq_sb", [128, DC * 512], F32) as sq_sb,
        nc.sbuf_tensor("rowbuf", [1, 6 * 512], F32) as rowbuf,
        nc.sbuf_tensor("bcast", [128, 1024], F32) as bcast,
    ):
        x_t = lambda c: x_sb[:, c * TOK:(c + 1) * TOK]

        @block.sync
        def _(sync):
            for c in range(DC):
                ld.inc(sync.dma_start(out=x_t(c), in_=t["xT"][c * 128:(c + 1) * 128, :]), 16)

        emit_ln(nc, block, T, x_t, xn_t, misc, onesrow, rowbuf, sq_sb, bcast,
                psr, psB, pre_wait=(ld, sc))

        @block.gpsimd
        def _(gpsimd):
            ve.wait(gpsimd)
            for c in range(DC):
                st.inc(gpsimd.dma_start(out=t["xnT_o"][c * 128:(c + 1) * 128, :], in_=xn_t(c)), 16)

    with (
        nc.sbuf_tensor("q_sb", [128, DC * TOK], BF16) as q_sb,
        nc.sbuf_tensor("k_sb", [128, DC * TOK], BF16) as k_sb,
        nc.sbuf_tensor("vT_sb", [128, DC * TOK], BF16) as vT_sb,
        nc.sbuf_tensor("vv_sb", [128, 16 * D], BF16) as vv_sb,
        nc.sbuf_tensor("wqkv", [128, 2 * DC * 128], BF16) as wqkv,
    ):
        for wname, dst in (("wq", q_sb), ("wk", k_sb), ("wv", vT_sb)):
            def evac(eng, oc, tt, ps, dst=dst):
                return eng.tensor_copy(
                    dst[:, oc * TOK + tt * 512: oc * TOK + (tt + 1) * 512], ps)

            emit_proj(nc, block, T, t[wname], DC, DC, xn_t, wqkv, psums, psum_free,
                      evac, "vector", ve, extra_mm_wait=(ve,) if wname == "wq" else ())

        @block.gpsimd
        def _(gpsimd):
            ve.wait(gpsimd)
            for c in range(DC):
                st.inc(gpsimd.dma_start(out=t["qT_o"][c * 128:(c + 1) * 128, :],
                                        in_=q_sb[:, c * TOK:(c + 1) * TOK]), 16)
                st.inc(gpsimd.dma_start(out=t["kT_o"][c * 128:(c + 1) * 128, :],
                                        in_=k_sb[:, c * TOK:(c + 1) * TOK]), 16)

        vv_t = lambda s: vv_sb[:, s * D:(s + 1) * D]
        tp_tn = {}
        tp_sc = {}
        for u in range(16 * DC):
            s_i, oc = divmod(u, DC)

            @block.tensor
            def _(tensor, u=u, s_i=s_i, oc=oc):
                if u == 0:
                    ve.wait(tensor)
                if u >= 2:
                    sc.wait(tensor, tp_sc[u - 2])
                mm = tensor.transpose(
                    pst[:, (u % 2) * 128:(u % 2) * 128 + 128],
                    vT_sb[:, oc * TOK + s_i * 128: oc * TOK + (s_i + 1) * 128],
                    ident[:, :])
                tn.inc(mm)
                tp_tn[u] = tn.n

            @block.scalar
            def _(scalar, u=u, s_i=s_i, oc=oc):
                tn.wait(scalar, tp_tn[u])
                sc.inc(scalar.copy(out=vv_t(s_i)[:, oc * 128:(oc + 1) * 128],
                                   in_=pst[:, (u % 2) * 128:(u % 2) * 128 + 128]))
                tp_sc[u] = sc.n

        psum_free[2] = (sc, sc.n)
        psum_free[3] = (sc, sc.n)

        @block.gpsimd
        def _(gpsimd):
            sc.wait(gpsimd)
            for s in range(16):
                st.inc(gpsimd.dma_start(out=t["vv_o"][s * 128:(s + 1) * 128, :], in_=vv_t(s)), 16)

    with (
        nc.sbuf_tensor("hbuf", [128, HC * TOK], BF16) as hbuf,
        nc.sbuf_tensor("wffn", [128, 2 * HC * 128], BF16) as wffn,
        nc.sbuf_tensor("stg", [128, 4 * 512], BF16) as stg,
        nc.sbuf_tensor("ostg", [128, 2 * TOK], BF16) as ostg,
        nc.sbuf_tensor("gstg", [128, 2 * TOK], F32) as gstg,
    ):
        h_t = lambda j: hbuf[:, j * TOK:(j + 1) * TOK]

        @block.gpsimd
        def _(gpsimd):
            st.wait(gpsimd)

        def evac_a(eng, oc, tt, ps):
            return eng.activation(out=h_t(oc)[:, tt * 512:(tt + 1) * 512], in_=ps,
                                  func=AF.Silu, scale=scal_sb[:, SC_S1 + oc:SC_S1 + oc + 1])

        emit_proj(nc, block, T, t["w1"], HC, DC, xn_t, wffn, psums, psum_free,
                  evac_a, "scalar", sc, extra_mm_wait=(st,))

        sc_after = {}

        def evac_b(eng, oc, tt, ps):
            u = oc * QT + tt
            if u >= 4:
                ve.wait(eng, sc_after.get(("vmul", u - 4), 0))
            return eng.activation(out=stg[:, tt * 512:(tt + 1) * 512], in_=ps,
                                  func=AF.Identity, scale=scal_sb[:, SC_S2 + oc:SC_S2 + oc + 1])

        def rec_b(oc, tt, n):
            sc_after[(oc, tt)] = n

        tnA, evA = emit_proj(nc, block, T, t["w2"], HC, DC, xn_t, wffn, psums, psum_free,
                             evac_b, "scalar", sc, after_evac=rec_b)

        for oc in range(HC):
            @block.vector
            def _(vector, oc=oc):
                for tt in range(QT):
                    sc.wait(vector, sc_after[(oc, tt)])
                    inst = vector.tensor_tensor(
                        h_t(oc)[:, tt * 512:(tt + 1) * 512],
                        h_t(oc)[:, tt * 512:(tt + 1) * 512],
                        stg[:, tt * 512:(tt + 1) * 512], ALU.mult)
                    ve.inc(inst)
                    sc_after[("vmul", oc * QT + tt)] = ve.n

        st_after = {}

        def evac_h2(eng, oc, tt, ps):
            u = oc % 2
            if oc >= 2:
                st.wait(eng, st_after.get(oc - 2, 0))
            return eng.activation(out=ostg[:, u * TOK + tt * 512:u * TOK + (tt + 1) * 512],
                                  in_=ps, func=AF.Gelu,
                                  scale=scal_sb[:, SC_H1 + oc:SC_H1 + oc + 1])

        def flush_h2(gpsimd, oc):
            u = oc % 2
            st.inc(gpsimd.dma_start(
                out=t["h2_dram"][oc * 128:(oc + 1) * 128, :],
                in_=ostg[:, u * TOK:(u + 1) * TOK]), 16)
            st_after[oc] = st.n

        emit_proj(nc, block, T, t["h1w"], HC, HC, h_t, wffn, psums, psum_free,
                  evac_h2, "scalar", sc, extra_mm_wait=(ve,), flush_fn=flush_h2)

        @block.sync
        def _(sync):
            tn.wait(sync)
            st.wait(sync)
            ld.wait(sync)
            for j in range(HC):
                ld.inc(sync.dma_start(out=h_t(j), in_=t["h2_dram"][j * 128:(j + 1) * 128, :]), 16)

        def evac_h3(eng, oc, tt, ps):
            u = oc % 2
            if oc >= 2:
                st.wait(eng, st_after.get(("h3", oc - 2), 0))
            return eng.activation(out=ostg[:, u * TOK + tt * 512:u * TOK + (tt + 1) * 512],
                                  in_=ps, func=AF.Gelu,
                                  scale=scal_sb[:, SC_H2 + oc:SC_H2 + oc + 1])

        def flush_h3(gpsimd, oc):
            u = oc % 2
            st.inc(gpsimd.dma_start(
                out=t["h3_dram"][oc * 128:(oc + 1) * 128, :],
                in_=ostg[:, u * TOK:(u + 1) * TOK]), 16)
            st_after[("h3", oc)] = st.n

        emit_proj(nc, block, T, t["h2w"], HC, HC, h_t, wffn, psums, psum_free,
                  evac_h3, "scalar", sc, extra_mm_wait=(ld,), flush_fn=flush_h3)

        @block.sync
        def _(sync):
            tn.wait(sync)
            st.wait(sync)
            ld.wait(sync)
            for j in range(HC):
                ld.inc(sync.dma_start(out=h_t(j), in_=t["h3_dram"][j * 128:(j + 1) * 128, :]), 16)

        def evac_g(eng, oc, tt, ps):
            u = oc % 2
            if oc >= 2:
                st.wait(eng, st_after.get(("g", oc - 2), 0))
            return eng.activation(out=gstg[:, u * TOK + tt * 512:u * TOK + (tt + 1) * 512],
                                  in_=ps, func=AF.Silu,
                                  scale=scal_sb[:, SC_S3 + oc:SC_S3 + oc + 1])

        def flush_g(gpsimd, oc):
            u = oc % 2
            st.inc(gpsimd.dma_start(
                out=t["g0T"][oc * 128:(oc + 1) * 128, :],
                in_=gstg[:, u * TOK:(u + 1) * TOK]), 16)
            st_after[("g", oc)] = st.n

        emit_proj(nc, block, T, t["w3"], DC, HC, h_t, wffn, psums, psum_free,
                  evac_g, "scalar", sc, extra_mm_wait=(ld,), flush_fn=flush_g)

        @block.gpsimd
        def _(gpsimd):
            st.wait(gpsimd)


def build_stageA():
    nc = bass.Bass("TRN2", target_bir_lowering=False, debug=False)
    t = {}
    t["xT"] = nc.dram_tensor("xT", [D, TOK], F32, kind="ExternalInput").ap()
    for n, shp in (("wq", [DC, 128, DC, 128]), ("wk", [DC, 128, DC, 128]),
                   ("wv", [DC, 128, DC, 128]), ("w1", [HC, 128, DC, 128]),
                   ("w2", [HC, 128, DC, 128]), ("h1w", [HC, 128, HC, 128]),
                   ("h2w", [HC, 128, HC, 128]), ("w3", [DC, 128, HC, 128])):
        t[n] = nc.dram_tensor(n, shp, F32, kind="ExternalInput").ap()
    t["scls"] = nc.dram_tensor("scls", [128, 4 * HC + DC], F32, kind="ExternalInput").ap()
    t["consts"] = nc.dram_tensor("consts", [128, 2], F32, kind="ExternalInput").ap()
    t["identbf"] = nc.dram_tensor("identbf", [128, 128], BF16, kind="ExternalInput").ap()
    t["g0T"] = nc.dram_tensor("g0T", [D, TOK], F32, kind="ExternalOutput").ap()
    t["qT_o"] = nc.dram_tensor("qT_o", [D, TOK], BF16, kind="ExternalOutput").ap()
    t["kT_o"] = nc.dram_tensor("kT_o", [D, TOK], BF16, kind="ExternalOutput").ap()
    t["vv_o"] = nc.dram_tensor("vv_o", [TOK, D], BF16, kind="ExternalOutput").ap()
    t["xnT_o"] = nc.dram_tensor("xnT_o", [D, TOK], BF16, kind="ExternalOutput").ap()
    t["h2_dram"] = nc.dram_tensor("h2_spill", [HC * 128, TOK], BF16).ap()
    t["h3_dram"] = nc.dram_tensor("h3_spill", [HC * 128, TOK], BF16).ap()

    with nc.Block() as block:
        T = make_tracks(nc)
        emit_stageA(nc, t, block, T)
    return nc


def host_inputs_stageA(d, core):
    x = np.asarray(d["L0_attn_x"]).reshape(16384, 1024)
    sl = slice(core * TOK, (core + 1) * TOK)
    im = {
        "xT": np.ascontiguousarray(x[sl].T),
        "wq": blk_lhsT(np.asarray(d["L0_attn_W_q"])),
        "wk": blk_lhsT(np.asarray(d["L0_attn_W_k"])),
        "wv": blk_lhsT(np.asarray(d["L0_attn_W_v"])),
        "w1": blk_lhsT(np.asarray(d["L0_ffn_w1_w"]).T),
        "w2": blk_lhsT(np.asarray(d["L0_ffn_w2_w"]).T),
        "h1w": blk_lhsT(np.asarray(d["L0_ffn_h1_w"]).T),
        "h2w": blk_lhsT(np.asarray(d["L0_ffn_h2_w"]).T),
        "w3": blk_lhsT(np.asarray(d["L0_ffn_w3_w"]).T),
        "scls": np.concatenate([blk_scales(np.asarray(d[k])) for k in
                                ("L0_ffn_w1_s", "L0_ffn_w2_s", "L0_ffn_h1_s",
                                 "L0_ffn_h2_s", "L0_ffn_w3_s")], axis=1),
        "consts": np.stack([np.ones(128, np.float32),
                            np.full(128, 1.0 / 1024, np.float32)], axis=1),
        "identbf": np.eye(128, dtype=np.float32).astype(ml_dtypes.bfloat16),
    }
    return im







NB = 2731
NPC = 22
NQT = 6
QW = [512, 512, 512, 512, 512, 171]
PADB = 2816
SCALE = 0.125


def emit_stageB(nc, t, block, T):
    ld, tn, sc, ve, st = (T[k] for k in ["ld", "tn", "sc", "ve", "st"])

    qh_sb = nc.alloc_sbuf_tensor("qh_sb", [128, 2 * 8448], BF16)
    kh_sb = nc.alloc_sbuf_tensor("kh_sb", [128, 2 * 8448], BF16)
    vaug_h = nc.alloc_sbuf_tensor("vaug", [128, NPC * 512], BF16)
    e_sb = nc.alloc_sbuf_tensor("e_sb", [128, 2 * 512], BF16)
    s0_sb = nc.alloc_sbuf_tensor("s0_sb", [128, 2 * 8192], BF16)
    bm_sb = nc.alloc_sbuf_tensor("bm_sb", [128, 6 * 512], BF16)
    tm_sb = nc.alloc_sbuf_tensor("tm_sb", [128, 1], F32)
    cvec_sb = nc.alloc_sbuf_tensor("cvec_sb", [1, PADB], F32)
    onesrow = nc.alloc_sbuf_tensor("onesrowB", [1, 64], F32)
    misc = nc.alloc_sbuf_tensor("miscB", [128, 2], F32)
    rows = nc.alloc_sbuf_tensor("rowsB", [1, 4 * 512], F32)
    rb_sb = nc.alloc_sbuf_tensor("rb_sb", [64, 2 * 512], F32)

    psS = nc.alloc_psum_tensor("psS", [128, 1024], F32)
    psO = nc.alloc_psum_tensor("psO", [128, 1024], F32)
    psb2 = nc.alloc_psum_tensor("psb2", [64, 1024], F32)

    @block.sync
    def _(sync):
        for i in range(2):
            ld.inc(sync.dma_start(out=qh_sb[:, i * 8448:(i + 1) * 8448], in_=t["qh"][i]), 16)
            ld.inc(sync.dma_start(out=kh_sb[:, i * 8448:(i + 1) * 8448], in_=t["kh"][i]), 16)
        ld.inc(sync.dma_start(out=bm_sb[:, :], in_=t["bmasks"]), 16)
        ld.inc(sync.dma_start(out=tm_sb[:, :], in_=t["tmask"]), 16)
        ld.inc(sync.dma_start(out=cvec_sb[:, :], in_=t["cvec"]), 16)
        ld.inc(sync.dma_start(out=misc[:, :], in_=t["consts"]), 16)

    @block.scalar
    def _(scalar):
        ld.wait(scalar)
        sc.inc(scalar.copy(out=onesrow[0:1, :], in_=misc[0:1, 0:1].broadcast_to([1, 64])))

    def vaug_lhsT(pc, h):
        return vaug_h[:, pc * 512 + 128 * h: pc * 512 + 128 * h + 128]

    e_free = {}
    ps_free = {}
    po_free = {}
    rb_free = {}
    psb_free = {}
    v_ld = [0]

    uq = 0
    for s in range(3):
        @block.gpsimd
        def _(gpsimd, s=s):
            if s > 0:
                tn.wait(gpsimd)
            src = t["vh"].rearrange("(r three) d -> r three d", three=3)
            for pc in range(NPC):
                st.inc(gpsimd.dma_start(
                    out=vaug_h[:, pc * 512: (pc + 1) * 512],
                    in_=src[pc * 128:(pc + 1) * 128, s:s + 1, :].squeeze(1)), 16)
        v_ld[0] = st.n

        for h in range(4):
            hp, hr = divmod(h, 2)
            qoff = hp * 8448
            hrow = hr * 64
            kt = lambda a, n, hrow=hrow, qoff=qoff: kh_sb[hrow:hrow + 64, qoff + a: qoff + a + 3 * (n - 1) + 1: 3]
            qt_ = lambda a, n, hrow=hrow, qoff=qoff: qh_sb[hrow:hrow + 64, qoff + a: qoff + a + 3 * (n - 1) + 1: 3]

            for qt in range(NQT):
                W = QW[qt]
                oslot = uq % 2
                exp_done = {}
                mask_done = {}
                av_done = {}

                for pc in range(NPC):
                    eslot = pc % 2
                    sslot = pc % 2

                    @block.tensor
                    def _(tensor, s=s, h=h, qt=qt, pc=pc, W=W, sslot=sslot,
                          oslot=oslot, uq=uq):
                        if uq == 0 and pc == 0:
                            ld.wait(tensor)
                        if pc == 0:
                            st.wait(tensor, v_ld[0])
                            if oslot in po_free:
                                ve.wait(tensor, po_free[oslot])
                        if sslot in ps_free:
                            sc.wait(tensor, ps_free[sslot])
                        tn.inc(tensor.matmul(
                            psS[:, sslot * 512: sslot * 512 + W],
                            kt(s + 384 * pc, 128),
                            qt_(s + 1536 * qt, W),
                            start=True, stop=True))
                        exp_src_ready = tn.n
                        if pc >= 1:
                            pe = pc - 1
                            if pe in mask_done:
                                ve.wait(tensor, mask_done[pe])
                            else:
                                sc.wait(tensor, exp_done[pe])
                            tn.inc(tensor.matmul(
                                psO[:, oslot * 512: oslot * 512 + W],
                                vaug_lhsT(pe, h),
                                e_sb[:, (pe % 2) * 512:(pe % 2) * 512 + W],
                                start=(pe == 0), stop=False))
                            av_done[pe] = tn.n
                            e_free[pe % 2] = tn.n
                        exp_done[("tn", pc)] = exp_src_ready

                    @block.scalar
                    def _(scalar, pc=pc, W=W, sslot=sslot, eslot=eslot):
                        tn.wait(scalar, exp_done[("tn", pc)])
                        if eslot in e_free:
                            tn.wait(scalar, e_free[eslot])
                        sc.inc(scalar.activation(
                            out=e_sb[:, eslot * 512: eslot * 512 + W],
                            in_=psS[:, sslot * 512: sslot * 512 + W],
                            func=AF.Exp, scale=SCALE))
                        exp_done[pc] = sc.n
                        ps_free[sslot] = sc.n

                    j = pc - 4 * qt + 1
                    need_band = 0 <= j < 6
                    need_tail = pc == NPC - 1
                    if need_band or need_tail:
                        @block.vector
                        def _(vector, pc=pc, W=W, eslot=eslot, j=j,
                              need_band=need_band, need_tail=need_tail):
                            sc.wait(vector, exp_done[pc])
                            esl = e_sb[:, eslot * 512: eslot * 512 + W]
                            if need_band:
                                inst = vector.tensor_tensor(
                                    esl, esl, bm_sb[:, j * 512: j * 512 + W], ALU.mult)
                            if need_tail:
                                inst = vector.tensor_scalar_mul(esl, esl, tm_sb[:, 0:1])
                            ve.inc(inst)
                            mask_done[pc] = ve.n

                @block.tensor
                def _(tensor, h=h, W=W, oslot=oslot):
                    pe = NPC - 1
                    if pe in mask_done:
                        ve.wait(tensor, mask_done[pe])
                    else:
                        sc.wait(tensor, exp_done[pe])
                    tn.inc(tensor.matmul(
                        psO[:, oslot * 512: oslot * 512 + W],
                        vaug_lhsT(pe, h),
                        e_sb[:, (pe % 2) * 512:(pe % 2) * 512 + W],
                        start=False, stop=True))
                    av_done[pe] = tn.n
                    e_free[pe % 2] = tn.n

                Wr = W if not (s == 2 and qt == NQT - 1) else W - 1
                dr = rows[0:1, oslot * 1024: oslot * 1024 + 512]
                dr2 = rows[0:1, oslot * 1024 + 512: oslot * 1024 + 1024]
                rb = rb_sb[:, oslot * 512: oslot * 512 + 512]

                @block.vector
                def _(vector, qt=qt, W=W, oslot=oslot):
                    tn.wait(vector, av_done[NPC - 1])
                    if oslot in rb_free:
                        ve.wait(vector, rb_free[oslot])
                    ve.inc(vector.tensor_tensor(dr[0:1, 0:W], psO[64:65, oslot * 512: oslot * 512 + W],
                                         cvec_sb[0:1, qt * 512: qt * 512 + W], ALU.add))
                    ve.wait(vector)
                    ve.inc(vector.reciprocal(dr2[0:1, 0:W], dr[0:1, 0:W]))
                recip_done = ve.n

                @block.tensor
                def _(tensor, W=W, oslot=oslot):
                    ve.wait(tensor, recip_done)
                    if oslot in psb_free:
                        sc.wait(tensor, psb_free[oslot])
                    tn.inc(tensor.matmul(psb2[:, oslot * 512: oslot * 512 + W],
                                         onesrow[0:1, :], dr2[0:1, 0:W],
                                         start=True, stop=True))
                bcast_done = tn.n

                @block.scalar
                def _(scalar, W=W, oslot=oslot):
                    tn.wait(scalar, bcast_done)
                    sc.inc(scalar.copy(out=rb[:, 0:W],
                                       in_=psb2[:, oslot * 512: oslot * 512 + W]))
                    psb_free[oslot] = sc.n
                rbcopy_done = sc.n

                @block.vector
                def _(vector, s=s, qt=qt, Wr=Wr, oslot=oslot, hp=hp, hrow=hrow):
                    sc.wait(vector, rbcopy_done)
                    base = hp * 8192 + s + 1536 * qt
                    out_ap = s0_sb[hrow:hrow + 64, base: base + 3 * (Wr - 1) + 1: 3]
                    ve.inc(vector.tensor_tensor(
                        out_ap, psO[0:64, oslot * 512: oslot * 512 + Wr],
                        rb[:, 0:Wr], ALU.mult))
                    po_free[oslot] = ve.n
                    rb_free[oslot] = ve.n
                uq += 1

    @block.gpsimd
    def _(gpsimd):
        ve.wait(gpsimd)
        for i in range(2):
            st.inc(gpsimd.dma_start(out=t["s0h"][i], in_=s0_sb[:, i * 8192:(i + 1) * 8192]), 16)
        st.wait(gpsimd)


def build_stageB():
    nc = bass.Bass("TRN2", target_bir_lowering=False, debug=False)
    t = {}
    t["qh"] = nc.dram_tensor("qh", [2, 128, 8448], BF16, kind="ExternalInput").ap()
    t["kh"] = nc.dram_tensor("kh", [2, 128, 8448], BF16, kind="ExternalInput").ap()
    t["vh"] = nc.dram_tensor("vh", [8448, 512], BF16, kind="ExternalInput").ap()
    t["cvec"] = nc.dram_tensor("cvec", [1, PADB], F32, kind="ExternalInput").ap()
    t["bmasks"] = nc.dram_tensor("bmasks", [128, 6 * 512], BF16, kind="ExternalInput").ap()
    t["tmask"] = nc.dram_tensor("tmask", [128, 1], F32, kind="ExternalInput").ap()
    t["consts"] = nc.dram_tensor("consts", [128, 2], F32, kind="ExternalInput").ap()
    t["s0h"] = nc.dram_tensor("s0h", [2, 128, 8192], BF16, kind="ExternalOutput").ap()
    with nc.Block() as block:
        T = make_tracks(nc)
        emit_stageB(nc, t, block, T)
    return nc


def host_consts_B():
    import ml_dtypes
    cvec = np.zeros((1, PADB), np.float32)
    cvec[0, 0] = 2.0
    cvec[0, 1] = 1.0
    cvec[0, NB - 2] += 1.0
    cvec[0, NB - 1] += 2.0
    bm = np.ones((128, 6 * 512), np.float32)
    for j in range(6):
        d0 = (j - 1) * 128
        pl = np.arange(128)[:, None]
        ql = np.arange(512)[None, :]
        bm[:, j * 512:(j + 1) * 512] += (np.abs(d0 + pl - ql) <= 2).astype(np.float32)
    tm = (np.arange(128) < (NB - NPC * 128 + 128)).astype(np.float32).reshape(128, 1)
    consts = np.stack([np.ones(128, np.float32), np.full(128, 1 / 1024, np.float32)], 1)
    return (cvec, bm.astype(ml_dtypes.bfloat16), tm, consts)


def host_inputs_stageB(core, qT_full, kT_full, vv_full):
    import ml_dtypes
    b = core // 4
    h0 = 4 * (core % 4)
    cvec, bm, tm, consts = host_consts_B()
    qh = np.zeros((2, 128, 8448), dtype=ml_dtypes.bfloat16)
    kh = np.zeros((2, 128, 8448), dtype=ml_dtypes.bfloat16)
    qh[:, :, 0:8192] = qT_full[b][h0 * 64:(h0 + 4) * 64].reshape(2, 128, 8192)
    kh[:, :, 0:8192] = kT_full[b][h0 * 64:(h0 + 4) * 64].reshape(2, 128, 8192)
    vh = np.zeros((8448, 512), dtype=ml_dtypes.bfloat16)
    for hh in range(4):
        vh[0:8192, 128 * hh:128 * hh + 64] = vv_full[b][:, (h0 + hh) * 64:(h0 + hh + 1) * 64]
        vh[:, 128 * hh + 64:128 * hh + 128] = 1.0
    return {"qh": qh, "kh": kh, "vh": vh, "cvec": cvec, "bmasks": bm,
            "tmask": tm, "consts": consts}








def emit_stageC(nc, t, block, T):
    ld, wl, tn, sc, ve, st = (T[k] for k in ["ld", "wl", "tn", "sc", "ve", "st"])

    x_sb = nc.alloc_sbuf_tensor("x_sb", [128, DC * TOK], F32)
    xn_sb = nc.alloc_sbuf_tensor("xn_sb", [128, DC * TOK], BF16)
    misc = nc.alloc_sbuf_tensor("misc", [128, 8], F32)
    onesrow = nc.alloc_sbuf_tensor("onesrow", [1, 128], F32)
    ident = nc.alloc_sbuf_tensor("ident_sb", [128, 128], BF16)
    iota16 = nc.alloc_sbuf_tensor("iota16_sb", [128, 16], F32)
    wg_sb = nc.alloc_sbuf_tensor("wg_sb", [128, DC * 16], BF16)
    pe_sb = nc.alloc_sbuf_tensor("pe_sb", [128, 32], F32)

    psA = nc.alloc_psum_tensor("psA", [128, 1024], F32)
    psB = nc.alloc_psum_tensor("psB", [128, 1024], F32)
    psC = nc.alloc_psum_tensor("psC", [128, 1024], F32)
    pst = nc.alloc_psum_tensor("pst", [128, 512], BF16)
    psr = psA[0:1, :]

    x_t = lambda c: x_sb[:, c * TOK:(c + 1) * TOK]
    xn_t = lambda c: xn_sb[:, c * TOK:(c + 1) * TOK]
    psums = [psA[:, 0:512], psA[:, 512:1024], psC[:, 0:512], psC[:, 512:1024]]
    psum_free = {}

    @block.sync
    def _(sync):
        ld.inc(sync.dma_start(out=misc[:, 0:2], in_=t["consts"]), 16)
        ld.inc(sync.dma_start(out=ident[:, :], in_=t["identbf"]), 16)
        ld.inc(sync.dma_start(out=iota16[:, :], in_=t["iota16"]), 16)
        for c in range(DC):
            ld.inc(sync.dma_start(out=x_t(c), in_=t["xT"][c * 128:(c + 1) * 128, :]), 16)

    @block.gpsimd
    def _(gpsimd):
        wl.inc(gpsimd.dma_start(out=wg_sb.rearrange("p (c n) -> p c n", n=16), in_=t["wg"].transpose([1, 0, 2])), 16)

    @block.scalar
    def _(scalar):
        ld.wait(scalar)
        sc.inc(scalar.copy(out=onesrow[0:1, :], in_=misc[0:1, 0:1].broadcast_to([1, 128])))

    with (
        nc.sbuf_tensor("cstg", [128, 2 * (TOK + 2 * TOK)], F32) as cstg,
        nc.sbuf_tensor("s0bf", [128, 2 * TOK], BF16) as s0bf,
    ):
        ld_after = {}

        ldt = [T["ld"], T["ld2"]]

        @block.sync
        def _(sync):
            ldt[0].wait(sync)
            ldt[1].wait(sync)
            for c in range(2):
                ldt[c % 2].inc(sync.dma_start(out=s0bf[:, (c % 2) * TOK:((c % 2) + 1) * TOK],
                                              in_=t["s0T"][c * 128:(c + 1) * 128, :]), 16)
                ldt[c % 2].inc(sync.dma_start(out=cstg[:, (c % 2) * 3 * TOK:(c % 2) * 3 * TOK + TOK],
                                              in_=t["g0T"][c * 128:(c + 1) * 128, :]), 16)
                ld_after[c] = ldt[c % 2].n

        ve_after = {}
        for c in range(DC):
            par = c % 2
            g0v = cstg[:, par * 3 * TOK: par * 3 * TOK + TOK]
            s0f = cstg[:, par * 3 * TOK + TOK: par * 3 * TOK + 2 * TOK]
            tmpv = cstg[:, par * 3 * TOK + 2 * TOK: par * 3 * TOK + 3 * TOK]

            @block.scalar
            def _(scalar, c=c, par=par, s0f=s0f):
                ldt[c % 2].wait(scalar, ld_after[c])
                sc.inc(scalar.copy(out=s0f, in_=s0bf[:, par * TOK:(par + 1) * TOK]))

            @block.vector
            def _(vector, c=c, g0v=g0v, s0f=s0f, tmpv=tmpv):
                sc.wait(vector)
                vector.tensor_tensor(tmpv, s0f, g0v, ALU.mult)
                ve.inc(vector.tensor_tensor(x_t(c), x_t(c), tmpv, ALU.add))
                ve_after[c] = ve.n

            nxt = c + 2
            if nxt < DC:
                @block.sync
                def _(sync, nxt=nxt, par=par, c=c):
                    ve.wait(sync, ve_after[c])
                    ldt[par].inc(sync.dma_start(out=s0bf[:, par * TOK:(par + 1) * TOK],
                                                in_=t["s0T"][nxt * 128:(nxt + 1) * 128, :]), 16)
                    ldt[par].inc(sync.dma_start(out=cstg[:, par * 3 * TOK: par * 3 * TOK + TOK],
                                                in_=t["g0T"][nxt * 128:(nxt + 1) * 128, :]), 16)
                    ld_after[nxt] = ldt[par].n

        @block.gpsimd
        def _(gpsimd):
            ve.wait(gpsimd)
            for c in range(DC):
                st.inc(gpsimd.dma_start(out=t["x1T"][c * 128:(c + 1) * 128, :], in_=x_t(c)), 16)

    with (
        nc.sbuf_tensor("sq_sb", [128, DC * 512], F32) as sq_sb,
        nc.sbuf_tensor("rowbuf", [1, 6 * 512], F32) as rowbuf,
        nc.sbuf_tensor("bcast", [128, 1024], F32) as bcast,
    ):
        emit_ln(nc, block, T, x_t, xn_t, misc, onesrow, rowbuf, sq_sb, bcast,
                psr, psB, pre_wait=(ve,))

    pslog = psB
    log_sb = nc.alloc_sbuf_tensor("log_sb", [128, 2 * 16], F32)
    rt_sb = nc.alloc_sbuf_tensor("rt_sb", [128, 8], F32)
    msk_sb = nc.alloc_sbuf_tensor("msk_sb", [128, 16], U8)
    sel_sb = nc.alloc_sbuf_tensor("sel_sb", [128, 16], F32)
    big_sb = nc.alloc_sbuf_tensor("big_sb", [128, 16], F32)

    @block.vector
    def _(vector):
        ve.inc(vector.memset(big_sb[:, :], 1e9))

    lg_tn = {}
    lg_done = {}
    se_prev = [0]
    for j in range(16):
        slot = j % 2

        @block.tensor
        def _(tensor, j=j, slot=slot):
            if j == 0:
                ve.wait(tensor)
                wl.wait(tensor)
            if j >= 2:
                ve.wait(tensor, lg_done[j - 2])
            for c in range(DC):
                mm = tensor.matmul(pslog[:, slot * 512: slot * 512 + 16],
                                   xn_t(c)[:, j * 128:(j + 1) * 128],
                                   wg_sb[:, c * 16:(c + 1) * 16],
                                   start=(c == 0), stop=(c == DC - 1))
            tn.inc(mm)
            lg_tn[j] = tn.n

        @block.vector
        def _(vector, j=j, slot=slot):
            tn.wait(vector, lg_tn[j])
            if j > 0:
                sc.wait(vector, se_prev[0])
            lg = log_sb[:, slot * 16:(slot + 1) * 16]
            ve.inc(vector.tensor_copy(lg, pslog[:, slot * 512: slot * 512 + 16]))
            ve.wait(vector)
            ve.inc(vector.reduce_max(rt_sb[:, 0:1], lg, axis=AX.X))
            ve.wait(vector)
            ve.inc(vector.tensor_scalar_mul(rt_sb[:, 1:2], rt_sb[:, 0:1], -1.0))
        vm = ve.n

        @block.scalar
        def _(scalar, j=j, slot=slot):
            ve.wait(scalar, vm)
            lg = log_sb[:, slot * 16:(slot + 1) * 16]
            sc.inc(scalar.activation(out=sel_sb[:, :], in_=lg,
                                     func=AF.Exp, bias=rt_sb[:, 1:2]))
        se = sc.n
        se_prev[0] = se

        @block.vector
        def _(vector, j=j, slot=slot):
            sc.wait(vector, se)
            lg = log_sb[:, slot * 16:(slot + 1) * 16]
            ve.inc(vector.reduce_sum(rt_sb[:, 2:3], sel_sb[:, :], axis=AX.X))
            ve.wait(vector)
            vector.reciprocal(pe_sb[:, j:j + 1], rt_sb[:, 2:3])
            ve.inc(vector.tensor_scalar(out=msk_sb[:, :], in0=lg, scalar1=rt_sb[:, 0:1],
                                        scalar2=None, op0=ALU.is_ge))
            ve.wait(vector)
            ve.inc(vector.select(sel_sb[:, :], msk_sb[:, :], iota16[:, :], big_sb[:, :]))
            ve.wait(vector)
            ve.inc(vector.tensor_reduce(pe_sb[:, 16 + j:17 + j], sel_sb[:, :],
                                        axis=AX.X, op=ALU.min))
        lg_done[j] = ve.n

    @block.gpsimd
    def _(gpsimd):
        ve.wait(gpsimd)
        st.inc(gpsimd.dma_start(out=t["pe_o"], in_=pe_sb[:, :]), 16)

    with (
        nc.sbuf_tensor("qkstg", [128, 4 * TOK], BF16) as qkstg,
        nc.sbuf_tensor("v1T_sb", [128, DC * TOK], BF16) as v1T_sb,
        nc.sbuf_tensor("wqkv_buf", [128, 2 * DC * 128], BF16) as wqkv_buf,
        nc.sbuf_tensor("tstg", [128, 2 * D], BF16) as tstg,
    ):
        st_after = {}

        def evac(eng, oc, tt, ps):
            if oc < 16:
                par = oc % 2
                if oc >= 2:
                    st.wait(eng, st_after.get(oc - 2, 0))
                return eng.tensor_copy(
                    qkstg[:, (par * QT + tt) * 512:(par * QT + tt) * 512 + 512], ps)
            return eng.tensor_copy(
                v1T_sb[:, (oc - 16) * TOK + tt * 512:(oc - 16) * TOK + (tt + 1) * 512], ps)

        def flush(gpsimd, oc):
            if oc >= 16:
                return
            par = oc % 2
            dst = t["q1T"] if oc < 8 else t["k1T"]
            row = (oc % 8) * 128
            st.inc(gpsimd.dma_start(out=dst[row:row + 128, :],
                                    in_=qkstg[:, par * QT * 512:(par * QT + QT) * 512]), 16)
            st_after[oc] = st.n

        emit_proj(nc, block, T, t["wqkv1"], 24, DC, xn_t, wqkv_buf, psums, psum_free,
                  evac, "vector", ve, flush_fn=flush, extra_mm_wait=(ve,))

        tp_tn = {}
        tp_sc = {}
        u = 0
        for which, src_t, outdram in (("v", lambda c: v1T_sb[:, c * TOK:(c + 1) * TOK], t["vv1"]),
                                      ("xn", xn_t, t["xn1tok"])):
            for s_i in range(16):
                for oc in range(DC):
                    @block.tensor
                    def _(tensor, u=u, s_i=s_i, oc=oc, src_t=src_t):
                        if u == 0:
                            ve.wait(tensor)
                        if u >= 2:
                            sc.wait(tensor, tp_sc[u - 2])
                        mm = tensor.transpose(
                            pst[:, (u % 2) * 128:(u % 2) * 128 + 128],
                            src_t(oc)[:, s_i * 128:(s_i + 1) * 128], ident[:, :])
                        tn.inc(mm)
                        tp_tn[u] = tn.n

                    @block.scalar
                    def _(scalar, u=u, s_i=s_i, oc=oc):
                        tn.wait(scalar, tp_tn[u])
                        par = (s_i % 2)
                        sc.inc(scalar.copy(
                            out=tstg[:, par * D + oc * 128: par * D + (oc + 1) * 128],
                            in_=pst[:, (u % 2) * 128:(u % 2) * 128 + 128]))
                        tp_sc[u] = sc.n
                    u += 1

                @block.gpsimd
                def _(gpsimd, s_i=s_i, outdram=outdram, u=u):
                    sc.wait(gpsimd, tp_sc[u - 1])
                    st.inc(gpsimd.dma_start(
                        out=outdram[s_i * 128:(s_i + 1) * 128, :],
                        in_=tstg[:, (s_i % 2) * D:((s_i % 2) + 1) * D]), 16)

                @block.scalar
                def _(scalar):
                    st.wait(scalar)
                    sc.inc(scalar.copy(out=misc[:, 4:5], in_=misc[:, 0:1]))
                tp_sc[u - 1] = sc.n

    @block.gpsimd
    def _(gpsimd):
        st.wait(gpsimd)


def build_stageC():
    nc = bass.Bass("TRN2", target_bir_lowering=False, debug=False)
    t = {}
    t["xT"] = nc.dram_tensor("xT", [D, TOK], F32, kind="ExternalInput").ap()
    t["s0T"] = nc.dram_tensor("s0T", [D, TOK], BF16, kind="ExternalInput").ap()
    t["g0T"] = nc.dram_tensor("g0T", [D, TOK], F32, kind="ExternalInput").ap()
    t["wqkv1"] = nc.dram_tensor("wqkv1", [24, 128, DC, 128], F32, kind="ExternalInput").ap()
    t["wg"] = nc.dram_tensor("wg", [DC, 128, 16], F32, kind="ExternalInput").ap()
    t["identbf"] = nc.dram_tensor("identbf", [128, 128], BF16, kind="ExternalInput").ap()
    t["iota16"] = nc.dram_tensor("iota16", [128, 16], F32, kind="ExternalInput").ap()
    t["consts"] = nc.dram_tensor("consts", [128, 2], F32, kind="ExternalInput").ap()
    t["x1T"] = nc.dram_tensor("x1T", [D, TOK], F32, kind="ExternalOutput").ap()
    t["q1T"] = nc.dram_tensor("q1T", [D, TOK], BF16, kind="ExternalOutput").ap()
    t["k1T"] = nc.dram_tensor("k1T", [D, TOK], BF16, kind="ExternalOutput").ap()
    t["vv1"] = nc.dram_tensor("vv1", [TOK, D], BF16, kind="ExternalOutput").ap()
    t["xn1tok"] = nc.dram_tensor("xn1tok", [TOK, D], BF16, kind="ExternalOutput").ap()
    t["pe_o"] = nc.dram_tensor("pe_o", [128, 32], F32, kind="ExternalOutput").ap()
    with nc.Block() as block:
        T = make_tracks(nc)
        emit_stageC(nc, t, block, T)
    return nc


def host_inputs_stageC(d, core, s0_full, g0_full, x0_full):
    import ml_dtypes
    sl = slice(core * TOK, (core + 1) * TOK)
    iota = np.broadcast_to(np.arange(16, dtype=np.float32), (128, 16)).copy()
    return {
        "xT": np.ascontiguousarray(x0_full[sl].T).astype(np.float32),
        "s0T": np.ascontiguousarray(s0_full[sl].T.astype(ml_dtypes.bfloat16)),
        "g0T": np.ascontiguousarray(g0_full[sl].T).astype(np.float32),
        "wqkv1": blk_lhsT(np.asarray(d["L1_attn_w_qkv"]).T),
        "wg": np.ascontiguousarray(np.asarray(d["L1_ffn_Wg"]).reshape(DC, 128, 16)),
        "identbf": np.eye(128, dtype=np.float32).astype(ml_dtypes.bfloat16),
        "iota16": iota,
        "consts": np.stack([np.ones(128, np.float32),
                            np.full(128, 1 / 1024, np.float32)], 1),
    }







D1_NQT = 16
D1_SCALE = 0.125
LSEQ = 8192


def emit_stageD1(nc, t, block, T):
    ld, tn, sc, ve, st = (T[k] for k in ["ld", "tn", "sc", "ve", "st"])

    qh_sb = nc.alloc_sbuf_tensor("qh1_sb", [128, 2 * LSEQ], BF16)
    kh_sb = nc.alloc_sbuf_tensor("kh1_sb", [128, 2 * LSEQ], BF16)
    vaug_h = nc.alloc_sbuf_tensor("vaug1", [128, 64 * 512], BF16)
    e_sb = nc.alloc_sbuf_tensor("e1_sb", [128, 2 * 512], BF16)
    s1_sb = nc.alloc_sbuf_tensor("s1_sb", [128, 2 * LSEQ], BF16)
    cm_sb = nc.alloc_sbuf_tensor("cm_sb", [128, 4 * 512], BF16)
    onesrow = nc.alloc_sbuf_tensor("onesrowD1", [1, 64], F32)
    misc = nc.alloc_sbuf_tensor("miscD1", [128, 2], F32)
    rows = nc.alloc_sbuf_tensor("rowsD1", [1, 4 * 512], F32)
    rb_sb = nc.alloc_sbuf_tensor("rb1_sb", [64, 2 * 512], F32)

    psS = nc.alloc_psum_tensor("psS1", [128, 1024], F32)
    psO = nc.alloc_psum_tensor("psO1", [128, 1024], F32)
    psb2 = nc.alloc_psum_tensor("psb21", [64, 1024], F32)

    @block.sync
    def _(sync):
        for i in range(2):
            ld.inc(sync.dma_start(out=qh_sb[:, i * LSEQ:(i + 1) * LSEQ], in_=t["q1h"][i]), 16)
            ld.inc(sync.dma_start(out=kh_sb[:, i * LSEQ:(i + 1) * LSEQ], in_=t["k1h"][i]), 16)
        ld.inc(sync.dma_start(out=cm_sb[:, :], in_=t["cmasks"]), 16)
        ld.inc(sync.dma_start(out=misc[:, :], in_=t["consts"]), 16)
        for pc in range(64):
            ld.inc(sync.dma_start(out=vaug_h[:, pc * 512:(pc + 1) * 512],
                                  in_=t["vh1"][pc * 128:(pc + 1) * 128, :]), 16)

    @block.scalar
    def _(scalar):
        ld.wait(scalar)
        sc.inc(scalar.copy(out=onesrow[0:1, :], in_=misc[0:1, 0:1].broadcast_to([1, 64])))

    vaug_lhsT = lambda pc, h: vaug_h[:, pc * 512 + 128 * h: pc * 512 + 128 * h + 128]

    e_free = {}
    ps_free = {}
    po_free = {}
    rb_free = {}
    psb_free = {}

    uq = 0
    for h in range(4):
        hp, hr = divmod(h, 2)
        qoff = hp * LSEQ
        hrow = hr * 64
        kt = lambda pc, hrow=hrow, qoff=qoff: kh_sb[hrow:hrow + 64, qoff + pc * 128: qoff + (pc + 1) * 128]
        qt_ = lambda qt, hrow=hrow, qoff=qoff: qh_sb[hrow:hrow + 64, qoff + qt * 512: qoff + (qt + 1) * 512]

        for qt in range(D1_NQT):
            npc = 4 * qt + 4
            oslot = uq % 2
            exp_done = {}
            mask_done = {}
            av_done = {}

            for pc in range(npc + 1):
                if pc < npc:
                    eslot = pc % 2
                    sslot = pc % 2

                    @block.tensor
                    def _(tensor, h=h, qt=qt, pc=pc, sslot=sslot, oslot=oslot,
                          uq=uq, kt=kt, qt_=qt_):
                        if uq == 0 and pc == 0:
                            ld.wait(tensor)
                            sc.wait(tensor)
                        if pc == 0 and oslot in po_free:
                            ve.wait(tensor, po_free[oslot])
                        if sslot in ps_free:
                            sc.wait(tensor, ps_free[sslot])
                        tn.inc(tensor.matmul(
                            psS[:, sslot * 512:(sslot + 1) * 512],
                            kt(pc), qt_(qt), start=True, stop=True))
                        exp_done[("tn", pc)] = tn.n
                        if pc >= 1:
                            pe = pc - 1
                            if pe in mask_done:
                                ve.wait(tensor, mask_done[pe])
                            else:
                                sc.wait(tensor, exp_done[pe])
                            tn.inc(tensor.matmul(
                                psO[:, oslot * 512:(oslot + 1) * 512],
                                vaug_lhsT(pe, h),
                                e_sb[:, (pe % 2) * 512:(pe % 2 + 1) * 512],
                                start=(pe == 0), stop=False))
                            av_done[pe] = tn.n
                            e_free[pe % 2] = tn.n

                    @block.scalar
                    def _(scalar, pc=pc, sslot=sslot, eslot=eslot):
                        tn.wait(scalar, exp_done[("tn", pc)])
                        if eslot in e_free:
                            tn.wait(scalar, e_free[eslot])
                        sc.inc(scalar.activation(
                            out=e_sb[:, eslot * 512:(eslot + 1) * 512],
                            in_=psS[:, sslot * 512:(sslot + 1) * 512],
                            func=AF.Exp, scale=D1_SCALE))
                        exp_done[pc] = sc.n
                        ps_free[sslot] = sc.n

                    j = pc - 4 * qt
                    if 0 <= j < 4:
                        @block.vector
                        def _(vector, pc=pc, eslot=eslot, j=j):
                            sc.wait(vector, exp_done[pc])
                            esl = e_sb[:, eslot * 512:(eslot + 1) * 512]
                            ve.inc(vector.tensor_tensor(
                                esl, esl, cm_sb[:, j * 512:(j + 1) * 512], ALU.mult))
                            mask_done[pc] = ve.n
                else:
                    @block.tensor
                    def _(tensor, h=h, qt=qt, oslot=oslot, npc=npc):
                        pe = npc - 1
                        if pe in mask_done:
                            ve.wait(tensor, mask_done[pe])
                        else:
                            sc.wait(tensor, exp_done[pe])
                        tn.inc(tensor.matmul(
                            psO[:, oslot * 512:(oslot + 1) * 512],
                            vaug_lhsT(pe, h),
                            e_sb[:, (pe % 2) * 512:(pe % 2 + 1) * 512],
                            start=False, stop=True))
                        av_done[pe] = tn.n
                        e_free[pe % 2] = tn.n

            dr2 = rows[0:1, oslot * 1024 + 512: oslot * 1024 + 1024]
            rb = rb_sb[:, oslot * 512:(oslot + 1) * 512]

            @block.vector
            def _(vector, qt=qt, oslot=oslot, npc=npc):
                tn.wait(vector, av_done[npc - 1])
                if oslot in rb_free:
                    ve.wait(vector, rb_free[oslot])
                ve.inc(vector.reciprocal(dr2, psO[64:65, oslot * 512:(oslot + 1) * 512]))
            recip_done = ve.n

            @block.tensor
            def _(tensor, oslot=oslot):
                ve.wait(tensor, recip_done)
                if oslot in psb_free:
                    sc.wait(tensor, psb_free[oslot])
                tn.inc(tensor.matmul(psb2[:, oslot * 512:(oslot + 1) * 512],
                                     onesrow[0:1, :], dr2, start=True, stop=True))
            bcast_done = tn.n

            @block.scalar
            def _(scalar, oslot=oslot):
                tn.wait(scalar, bcast_done)
                sc.inc(scalar.copy(out=rb, in_=psb2[:, oslot * 512:(oslot + 1) * 512]))
                psb_free[oslot] = sc.n
            rbcopy_done = sc.n

            @block.vector
            def _(vector, qt=qt, oslot=oslot, hp=hp, hrow=hrow):
                sc.wait(vector, rbcopy_done)
                out_ap = s1_sb[hrow:hrow + 64, hp * LSEQ + qt * 512: hp * LSEQ + (qt + 1) * 512]
                ve.inc(vector.tensor_tensor(
                    out_ap, psO[0:64, oslot * 512:(oslot + 1) * 512], rb, ALU.mult))
                po_free[oslot] = ve.n
                rb_free[oslot] = ve.n
            uq += 1

    @block.gpsimd
    def _(gpsimd):
        ve.wait(gpsimd)
        for i in range(2):
            st.inc(gpsimd.dma_start(out=t["s1h"][i], in_=s1_sb[:, i * LSEQ:(i + 1) * LSEQ]), 16)
        st.wait(gpsimd)


def build_stageD1():
    nc = bass.Bass("TRN2", target_bir_lowering=False, debug=False)
    t = {}
    t["q1h"] = nc.dram_tensor("q1h", [2, 128, LSEQ], BF16, kind="ExternalInput").ap()
    t["k1h"] = nc.dram_tensor("k1h", [2, 128, LSEQ], BF16, kind="ExternalInput").ap()
    t["vh1"] = nc.dram_tensor("vh1", [LSEQ, 512], BF16, kind="ExternalInput").ap()
    t["cmasks"] = nc.dram_tensor("cmasks", [128, 4 * 512], BF16, kind="ExternalInput").ap()
    t["consts"] = nc.dram_tensor("consts", [128, 2], F32, kind="ExternalInput").ap()
    t["s1h"] = nc.dram_tensor("s1h", [2, 128, LSEQ], BF16, kind="ExternalOutput").ap()
    with nc.Block() as block:
        T = make_tracks(nc)
        emit_stageD1(nc, t, block, T)
    return nc


def host_consts_D1():
    import ml_dtypes
    cm = np.zeros((128, 4 * 512), np.float32)
    pl = np.arange(128)[:, None]
    ql = np.arange(512)[None, :]
    for j in range(4):
        cm[:, j * 512:(j + 1) * 512] = (128 * j + pl <= ql).astype(np.float32)
    return cm.astype(ml_dtypes.bfloat16)


def host_inputs_stageD1(core, qT_full, kT_full, vv_full):
    import ml_dtypes
    b = core // 4
    h0 = 4 * (core % 4)
    qh = qT_full[b][h0 * 64:(h0 + 4) * 64].reshape(2, 128, LSEQ).astype(ml_dtypes.bfloat16)
    kh = kT_full[b][h0 * 64:(h0 + 4) * 64].reshape(2, 128, LSEQ).astype(ml_dtypes.bfloat16)
    vh = np.ones((LSEQ, 512), dtype=ml_dtypes.bfloat16)
    for hh in range(4):
        vh[:, 128 * hh:128 * hh + 64] = vv_full[b][:, (h0 + hh) * 64:(h0 + hh + 1) * 64]
    return {"q1h": qh, "k1h": kh, "vh1": vh, "cmasks": host_consts_D1(),
            "consts": np.stack([np.ones(128, np.float32),
                                np.full(128, 1 / 1024, np.float32)], 1)}







CAP = 1280
NEXP = 2


def emit_stageD2(nc, t, block, T, upto=3):
    ld, wl, tn, sc, ve, st = (T[k] for k in ["ld", "wl", "tn", "sc", "ve", "st"])

    pe_sb = nc.alloc_sbuf_tensor("pe_sb", [128, 256], F32)
    expid = nc.alloc_sbuf_tensor("expid_sb", [128, 2], F32)
    iotat = nc.alloc_sbuf_tensor("iotat", [128, 128], F32)
    iotaf = nc.alloc_sbuf_tensor("iotaf_sb", [128, 128], F32)
    iotau = nc.alloc_sbuf_tensor("iotau", [128, 128 * 8], U32)
    sl_sb = nc.alloc_sbuf_tensor("sl_sb", [128, 128], F32)
    misc = nc.alloc_sbuf_tensor("miscD", [128, 4], F32)
    onesrow = nc.alloc_sbuf_tensor("onesrowD", [1, 128], F32)
    ident = nc.alloc_sbuf_tensor("identD", [128, 128], BF16)
    ones128 = nc.alloc_sbuf_tensor("ones128", [128, 128], F32)

    ind = nc.alloc_sbuf_tensor("ind", [128, 128], F32)
    pm = nc.alloc_sbuf_tensor("pm", [128, 128], F32)
    ge = nc.alloc_sbuf_tensor("ge", [128, 128], F32)
    sc128 = nc.alloc_sbuf_tensor("sc128", [128, 16], F32)
    scu8 = nc.alloc_sbuf_tensor("scu8", [128, 1], U8)
    tie = nc.alloc_sbuf_tensor("tie", [128, 128], F32)
    kept = nc.alloc_sbuf_tensor("kept", [128, 128], F32)
    slot = nc.alloc_sbuf_tensor("slot", [128, 128], F32)
    scan = nc.alloc_sbuf_tensor("scan", [128, 128], F32)
    zero128 = nc.alloc_sbuf_tensor("zero128", [128, 128], F32)
    offu = nc.alloc_sbuf_tensor("offu", [128, 128], U32)
    gidx_sb = nc.alloc_sbuf_tensor("gidx_sb", [128, 80], U32)
    initu = nc.alloc_sbuf_tensor("initu", [128, 80], U32)

    psq = nc.alloc_psum_tensor("psq", [128, 1024], F32)
    psw = nc.alloc_psum_tensor("psw", [128, 1024], F32)
    pst = nc.alloc_psum_tensor("pstD", [128, 512], BF16)

    gidx_dram = nc.dram_tensor("gidx_dram", [CAP + 128, 8], U32).ap()

    LO, HI, MID, CNT, SEL, MM, OFFS, T1 = range(8)
    col = lambda i: sc128[:, i:i + 1]

    @block.sync
    def _(sync):
        ld.inc(sync.dma_start(out=pe_sb[:, :], in_=t["pe_all"]), 16)
        ld.inc(sync.dma_start(out=expid[:, :], in_=t["expid"]), 16)
        ld.inc(sync.dma_start(out=iotat[:, :], in_=t["iotatok"]), 16)
        ld.inc(sync.dma_start(out=iotaf[:, :], in_=t["iotaf"]), 16)
        ld.inc(sync.dma_start(out=sl_sb[:, :], in_=t["slmat"]), 16)
        ld.inc(sync.dma_start(out=misc[:, 0:2], in_=t["consts"]), 16)
        ld.inc(sync.dma_start(out=ident[:, :], in_=t["identbf"]), 16)

    @block.scalar
    def _(scalar):
        ld.wait(scalar)
        sc.inc(scalar.copy(out=onesrow[0:1, :], in_=misc[0:1, 0:1].broadcast_to([1, 128])))

    @block.vector
    def _(vector):
        ld.wait(vector)
        vector.memset(zero128[:, :], 0.0)
        vector.memset(initu[:, :], 16383)
        vector.memset(ones128[:, :], 1.0)
        ve.inc(vector.tensor_copy(
            iotau[:, :].rearrange("p (f k) -> p f k", k=8),
            iotat[:, :, None].broadcast_to([128, 128, 8])))

    def vsync(vector, inst):
        ve.inc(inst)
        ve.wait(vector)

    p_all = pe_sb[:, 0:128]
    e_all = pe_sb[:, 128:256]

    for e_i in range(NEXP):
        @block.vector
        def _(vector, e_i=e_i):
            if e_i == 0:
                sc.wait(vector)
            else:
                st.wait(vector)
            vsync(vector, vector.tensor_scalar(out=ind[:, :], in0=e_all,
                                               scalar1=expid[:, e_i:e_i + 1],
                                               scalar2=None, op0=ALU.is_equal))
            vsync(vector, vector.tensor_tensor(pm[:, :], p_all, ind[:, :], ALU.mult))
            vsync(vector, vector.memset(col(LO), 0.0))
            vsync(vector, vector.memset(col(HI), 1.0))

        for it in range(40):
            @block.vector
            def _(vector, it=it):
                vsync(vector, vector.tensor_tensor(col(MID), col(LO), col(HI), ALU.add))
                vsync(vector, vector.tensor_scalar_mul(col(MID), col(MID), 0.5))
                vsync(vector, vector.tensor_scalar(out=ge[:, :], in0=pm[:, :],
                                                   scalar1=col(MID), scalar2=None,
                                                   op0=ALU.is_gt))
                vsync(vector, vector.reduce_sum(scan[:, 0:1], ge[:, :], axis=AX.X))
            rdy = ve.n

            @block.tensor
            def _(tensor, it=it):
                ve.wait(tensor, rdy)
                tn.inc(tensor.matmul(psq[:, 512:513], ones128[:, :], scan[:, 0:1],
                                     start=True, stop=True))
            bc = tn.n

            @block.vector
            def _(vector, it=it):
                tn.wait(vector, bc)
                vsync(vector, vector.tensor_copy(col(CNT), psq[:, 512:513]))
                vsync(vector, vector.tensor_scalar(out=col(SEL), in0=col(CNT),
                                                   scalar1=float(CAP), scalar2=None,
                                                   op0=ALU.is_ge))
                vsync(vector, vector.tensor_tensor(col(T1), col(MID), col(LO), ALU.subtract))
                vsync(vector, vector.tensor_tensor(col(T1), col(T1), col(SEL), ALU.mult))
                vsync(vector, vector.tensor_tensor(col(LO), col(LO), col(T1), ALU.add))
                vsync(vector, vector.tensor_tensor(col(T1), col(HI), col(MID), ALU.subtract))
                vsync(vector, vector.tensor_tensor(col(T1), col(T1), col(SEL), ALU.mult))
                vsync(vector, vector.tensor_tensor(col(HI), col(MID), col(T1), ALU.add))

        @block.vector
        def _(vector, e_i=e_i):
            vsync(vector, vector.tensor_scalar(out=kept[:, :], in0=pm[:, :],
                                               scalar1=col(HI), scalar2=None,
                                               op0=ALU.is_gt))
            vsync(vector, vector.tensor_scalar(out=tie[:, :], in0=pm[:, :],
                                               scalar1=col(LO), scalar2=None,
                                               op0=ALU.is_gt))
            vsync(vector, vector.tensor_tensor(tie[:, :], tie[:, :], kept[:, :],
                                               ALU.subtract))
            vsync(vector, vector.reduce_sum(scan[:, 0:1], kept[:, :], axis=AX.X))
        rdy2 = ve.n

        @block.tensor
        def _(tensor, e_i=e_i):
            ve.wait(tensor, rdy2)
            tn.inc(tensor.matmul(psq[:, 512:513], ones128[:, :], scan[:, 0:1],
                                 start=True, stop=True))
        bcm = tn.n

        @block.vector
        def _(vector, e_i=e_i):
            tn.wait(vector, bcm)
            vsync(vector, vector.tensor_copy(col(MM), psq[:, 512:513]))
            vsync(vector, vector.tensor_tensor_scan(scan[:, :], tie[:, :],
                                                    zero128[:, :], 0.0,
                                                    ALU.add, ALU.add))
            vsync(vector, vector.reduce_sum(sc128[:, T1:T1 + 1], tie[:, :], axis=AX.X))
        rdy3 = ve.n

        @block.tensor
        def _(tensor, e_i=e_i):
            ve.wait(tensor, rdy3)
            tn.inc(tensor.matmul(psq[:, 512:513], sl_sb[:, :], sc128[:, T1:T1 + 1],
                                 start=True, stop=True))
        bco = tn.n

        @block.vector
        def _(vector, e_i=e_i):
            tn.wait(vector, bco)
            vsync(vector, vector.tensor_copy(col(OFFS), psq[:, 512:513]))
            vsync(vector, vector.tensor_scalar(out=scan[:, :], in0=scan[:, :],
                                               scalar1=col(OFFS), scalar2=None,
                                               op0=ALU.add))
            vsync(vector, vector.tensor_tensor(scan[:, :], scan[:, :], tie[:, :],
                                               ALU.subtract))
            vsync(vector, vector.tensor_scalar(out=scan[:, :], in0=scan[:, :],
                                               scalar1=col(MM), scalar2=None,
                                               op0=ALU.add))
            vsync(vector, vector.tensor_scalar(out=scan[:, :], in0=scan[:, :],
                                               scalar1=float(CAP), scalar2=None,
                                               op0=ALU.is_lt))
            vsync(vector, vector.tensor_tensor(scan[:, :], scan[:, :], tie[:, :],
                                               ALU.mult))
            vsync(vector, vector.tensor_tensor(kept[:, :], kept[:, :], scan[:, :],
                                               ALU.add))
            vsync(vector, vector.tensor_tensor_scan(slot[:, :], kept[:, :],
                                                    zero128[:, :], 0.0,
                                                    ALU.add, ALU.add))
            vsync(vector, vector.reduce_sum(sc128[:, T1:T1 + 1], kept[:, :], axis=AX.X))
        rdy4 = ve.n

        @block.tensor
        def _(tensor, e_i=e_i):
            ve.wait(tensor, rdy4)
            tn.inc(tensor.matmul(psq[:, 512:513], sl_sb[:, :], sc128[:, T1:T1 + 1],
                                 start=True, stop=True))
        bco2 = tn.n

        @block.vector
        def _(vector, e_i=e_i):
            tn.wait(vector, bco2)
            vsync(vector, vector.tensor_copy(col(OFFS), psq[:, 512:513]))
            vsync(vector, vector.tensor_scalar(out=slot[:, :], in0=slot[:, :],
                                               scalar1=col(OFFS), scalar2=None,
                                               op0=ALU.add))
            vsync(vector, vector.tensor_tensor(slot[:, :], slot[:, :], kept[:, :],
                                               ALU.subtract))
            vsync(vector, vector.tensor_scalar(out=scan[:, :], in0=iotaf[:, :],
                                               scalar1=float(CAP), scalar2=None,
                                               op0=ALU.add))
            vsync(vector, vector.tensor_scalar_mul(ge[:, :], kept[:, :], -1.0))
            vsync(vector, vector.tensor_scalar_add(ge[:, :], ge[:, :], 1.0))
            vsync(vector, vector.tensor_tensor(scan[:, :], scan[:, :], ge[:, :],
                                               ALU.mult))
            vsync(vector, vector.tensor_tensor(ge[:, :], slot[:, :], kept[:, :],
                                               ALU.mult))
            vsync(vector, vector.tensor_tensor(scan[:, :], scan[:, :], ge[:, :],
                                               ALU.add))
            vsync(vector, vector.tensor_copy(offu[:, :], scan[:, :]))

        @block.gpsimd
        def _(gpsimd, e_i=e_i):
            ve.wait(gpsimd)
            st.inc(gpsimd.dma_start(out=t["keptslot"][e_i, 0], in_=kept[:, :]), 16)
            st.inc(gpsimd.dma_start(out=t["keptslot"][e_i, 1], in_=slot[:, :]), 16)
            st.inc(gpsimd.dma_start(
                out=gidx_dram[0:CAP, :].rearrange("(p g) k -> p (g k)", g=10),
                in_=initu[:, :]), 16)

        if upto < 2:
            continue
        @block.gpsimd
        def _(gpsimd, e_i=e_i):
            st.wait(gpsimd)
            for f in range(128):
                inst = gpsimd.indirect_dma_start(
                    out=gidx_dram, out_offset=IndirectOffsetOnAxis(ap=offu[:, f:f + 1], axis=0),
                    in_=iotau[:, f * 8:(f + 1) * 8], in_offset=None)
                st.inc(inst, 16)

        with (
            nc.sbuf_tensor("xe_sb%d" % e_i, [128, 10 * D], BF16) as xe_sb,
            nc.sbuf_tensor("xeT_sb%d" % e_i, [128, 8 * CAP], BF16) as xeT_sb,
            nc.sbuf_tensor("hT_sb%d" % e_i, [128, 16 * CAP], BF16) as hT_sb,
            nc.sbuf_tensor("w1_sb%d" % e_i, [128, 2 * 8 * 128], BF16) as w1_sb,
            nc.sbuf_tensor("w2_sb%d" % e_i, [128, 16 * 1024], BF16) as w2_sb,
            nc.sbuf_tensor("b1_sb%d" % e_i, [128, 16], F32) as b1_sb,
            nc.sbuf_tensor("b2b_sb%d" % e_i, [128, 1024], F32) as b2b_sb,
            nc.sbuf_tensor("b2r_sb%d" % e_i, [1, 1024], F32) as b2r_sb,
            nc.sbuf_tensor("mo_sb%d" % e_i, [128, 10 * D], BF16) as mo_sb,
        ):
            @block.sync
            def _(sync, e_i=e_i):
                st.wait(sync)
                ld.wait(sync)
                ld.inc(sync.dma_start(out=gidx_sb[:, :],
                                      in_=gidx_dram[0:CAP, :].rearrange("(p g) k -> p (g k)", g=10)), 16)
                ld.inc(sync.dma_start(out=b1_sb[:, :], in_=t["b1s"][e_i]), 16)
                ld.inc(sync.dma_start(out=b2r_sb[:, :], in_=t["b2r"][e_i]), 16)

            @block.gpsimd
            def _(gpsimd, e_i=e_i):
                ld.wait(gpsimd)
                for g in range(10):
                    inst = gpsimd.indirect_dma_start(
                        out=xe_sb[:, g * D:(g + 1) * D], out_offset=None,
                        in_=t["xn1tok"],
                        in_offset=IndirectOffsetOnAxis(ap=gidx_sb[:, g * 8:g * 8 + 1], axis=0))
                    st.inc(inst, 16)
                T["gp"].inc(gpsimd.dma_start(
                    out=w2_sb.rearrange("p (c n) -> p c n", n=1024),
                    in_=t["w2blk"][e_i].transpose([1, 0, 2])), 16)

            if upto < 3:
                @block.gpsimd
                def _(gpsimd, e_i=e_i, xe_sb=xe_sb):
                    st.wait(gpsimd)
                    mo3 = t["moe_out"][e_i].rearrange("(p g) d -> p g d", g=10)
                    for g in range(10):
                        st.inc(gpsimd.dma_start(
                            out=mo3[:, g:g + 1, :].squeeze(1),
                            in_=xe_sb[:, g * D:(g + 1) * D]), 16)
                continue
            tp_tn = {}
            tp_sc = {}
            for u in range(80):
                g, c = divmod(u, 8)

                @block.tensor
                def _(tensor, u=u, g=g, c=c, xe_sb=xe_sb):
                    if u == 0:
                        st.wait(tensor)
                    if u >= 2:
                        sc.wait(tensor, tp_sc[u - 2])
                    mm = tensor.transpose(pst[:, (u % 2) * 128:(u % 2) * 128 + 128],
                                          xe_sb[:, g * D + c * 128: g * D + (c + 1) * 128],
                                          ident[:, :])
                    tn.inc(mm)
                    tp_tn[u] = tn.n

                @block.scalar
                def _(scalar, u=u, g=g, c=c, xeT_sb=xeT_sb):
                    tn.wait(scalar, tp_tn[u])
                    sc.inc(scalar.copy(
                        out=xeT_sb[:, c * CAP + g * 128: c * CAP + (g + 1) * 128],
                        in_=pst[:, (u % 2) * 128:(u % 2) * 128 + 128]))
                    tp_sc[u] = sc.n

            if upto == 25:
                @block.gpsimd
                def _(gpsimd, e_i=e_i, xeT_sb=xeT_sb):
                    sc.wait(gpsimd)
                    mo3 = t["moe_out"][e_i].rearrange("(p g) d -> p g d", g=10)
                    for g in range(10):
                        st.inc(gpsimd.dma_start(
                            out=mo3[:, g:g + 1, :].squeeze(1),
                            in_=xe_sb[:, g * D:(g + 1) * D]), 16)
                continue
            xeT_t = lambda c: xeT_sb[:, c * CAP:(c + 1) * CAP]
            hT_t = lambda j: hT_sb[:, j * CAP:(j + 1) * CAP]
            NT3 = [(0, 512), (512, 512), (1024, 256)]
            wl_after = {}

            wlt = [T["wl"], T["wl2"]]

            @block.gpsimd
            def _(gpsimd, e_i=e_i):
                sc.wait(gpsimd)
                for oc in range(2):
                    wlt[oc % 2].inc(gpsimd.dma_start(
                        out=w1_sb[:, (oc % 2) * 1024:((oc % 2) + 1) * 1024],
                        in_=t["w1blk"][e_i, oc].rearrange("p ic o -> p (ic o)")), 16)
                    wl_after[oc] = wlt[oc % 2].n

            h_tn = {}
            h_sc = {}
            for oc in range(16):
                for ti, (t0, tw) in enumerate(NT3):
                    u = oc * 3 + ti

                    @block.tensor
                    def _(tensor, oc=oc, ti=ti, t0=t0, tw=tw, u=u, xeT_t=xeT_t):
                        if ti == 0:
                            wlt[oc % 2].wait(tensor, wl_after[oc])
                        slot_ps = u % 2
                        if u >= 2:
                            sc.wait(tensor, h_sc[u - 2])
                        for ic in range(8):
                            mm = tensor.matmul(
                                psw[:, slot_ps * 512: slot_ps * 512 + tw],
                                w1_sb[:, (oc % 2) * 1024 + ic * 128:(oc % 2) * 1024 + (ic + 1) * 128],
                                xeT_t(ic)[:, t0:t0 + tw],
                                start=(ic == 0), stop=(ic == 7))
                        tn.inc(mm)
                        h_tn[u] = tn.n

                    @block.scalar
                    def _(scalar, oc=oc, ti=ti, t0=t0, tw=tw, u=u, hT_t=hT_t):
                        if u == 0:
                            ld.wait(scalar)
                        slot_ps = u % 2
                        tn.wait(scalar, h_tn[u])
                        sc.inc(scalar.activation(
                            out=hT_t(oc)[:, t0:t0 + tw],
                            in_=psw[:, slot_ps * 512: slot_ps * 512 + tw],
                            func=AF.Gelu_apprx_tanh, bias=b1_sb[:, oc:oc + 1]))
                        h_sc[u] = sc.n

                nxt = oc + 2
                if nxt < 16:
                    @block.gpsimd
                    def _(gpsimd, oc=oc, nxt=nxt, e_i=e_i):
                        tn.wait(gpsimd, h_tn[oc * 3 + 2])
                        wlt[nxt % 2].inc(gpsimd.dma_start(
                            out=w1_sb[:, (nxt % 2) * 1024:((nxt % 2) + 1) * 1024],
                            in_=t["w1blk"][e_i, nxt].rearrange("p ic o -> p (ic o)")), 16)
                    wl_after[nxt] = wlt[nxt % 2].n

            @block.tensor
            def _(tensor):
                ld.wait(tensor)
                ve.wait(tensor)
                tensor.matmul(psq[:, 0:512], onesrow[0:1, :], b2r_sb[0:1, 0:512],
                              start=True, stop=True)
                tn.inc(tensor.matmul(psq[:, 512:1024], onesrow[0:1, :], b2r_sb[0:1, 512:1024],
                                     start=True, stop=True))
            b2bc = tn.n

            @block.vector
            def _(vector):
                tn.wait(vector, b2bc)
                ve.inc(vector.tensor_copy(b2b_sb[:, :], psq[:, 0:1024]))
            b2done = ve.n

            o_tn = {}
            o_ve = {}
            for mt in range(10):
                for nn in range(2):
                    u = mt * 2 + nn

                    @block.tensor
                    def _(tensor, mt=mt, nn=nn, u=u, hT_t=hT_t):
                        if u == 0:
                            T["gp"].wait(tensor)
                            sc.wait(tensor)
                        slot_ps = u % 2
                        if u >= 2:
                            ve.wait(tensor, o_ve[u - 2])
                        for hc in range(16):
                            mm = tensor.matmul(
                                psw[:, slot_ps * 512: slot_ps * 512 + 512],
                                hT_t(hc)[:, mt * 128:(mt + 1) * 128],
                                w2_sb[:, hc * 1024 + nn * 512: hc * 1024 + (nn + 1) * 512],
                                start=(hc == 0), stop=(hc == 15))
                        tn.inc(mm)
                        o_tn[u] = tn.n

                    @block.vector
                    def _(vector, mt=mt, nn=nn, u=u):
                        if u == 0:
                            ve.wait(vector, b2done)
                        slot_ps = u % 2
                        tn.wait(vector, o_tn[u])
                        ve.inc(vector.tensor_tensor(
                            mo_sb[:, mt * D + nn * 512: mt * D + (nn + 1) * 512],
                            psw[:, slot_ps * 512: slot_ps * 512 + 512],
                            b2b_sb[:, nn * 512:(nn + 1) * 512], ALU.add))
                        o_ve[u] = ve.n

            @block.gpsimd
            def _(gpsimd, e_i=e_i):
                ve.wait(gpsimd)
                mo3 = t["moe_out"][e_i].rearrange("(p g) d -> p g d", g=10)
                for g in range(10):
                    st.inc(gpsimd.dma_start(
                        out=mo3[:, g:g + 1, :].squeeze(1),
                        in_=mo_sb[:, g * D:(g + 1) * D]), 16)

    @block.gpsimd
    def _(gpsimd):
        st.wait(gpsimd)


def build_stageD2(upto=3):
    nc = bass.Bass("TRN2", target_bir_lowering=False, debug=False)
    t = {}
    t["pe_all"] = nc.dram_tensor("pe_all", [128, 256], F32, kind="ExternalInput").ap()
    t["xn1tok"] = nc.dram_tensor("xn1tok", [16384, D], BF16, kind="ExternalInput").ap()
    t["expid"] = nc.dram_tensor("expid", [128, 2], F32, kind="ExternalInput").ap()
    t["w1blk"] = nc.dram_tensor("w1blk", [2, 16, 128, 8, 128], F32, kind="ExternalInput").ap()
    t["w2blk"] = nc.dram_tensor("w2blk", [2, 16, 128, 1024], F32, kind="ExternalInput").ap()
    t["b1s"] = nc.dram_tensor("b1s", [2, 128, 16], F32, kind="ExternalInput").ap()
    t["b2r"] = nc.dram_tensor("b2r", [2, 1, 1024], F32, kind="ExternalInput").ap()
    t["iotatok"] = nc.dram_tensor("iotatok", [128, 128], F32, kind="ExternalInput").ap()
    t["slmat"] = nc.dram_tensor("slmat", [128, 128], F32, kind="ExternalInput").ap()
    t["iotaf"] = nc.dram_tensor("iotaf", [128, 128], F32, kind="ExternalInput").ap()
    t["consts"] = nc.dram_tensor("consts", [128, 2], F32, kind="ExternalInput").ap()
    t["identbf"] = nc.dram_tensor("identbf", [128, 128], BF16, kind="ExternalInput").ap()
    t["moe_out"] = nc.dram_tensor("moe_out", [2, CAP, D], BF16, kind="ExternalOutput").ap()
    t["keptslot"] = nc.dram_tensor("keptslot", [2, 2, 128, 128], F32, kind="ExternalOutput").ap()
    with nc.Block() as block:
        T = make_tracks(nc)
        emit_stageD2(nc, t, block, T, upto=upto)
    return nc


def host_inputs_stageD2(d, core, p_vec, e_vec, xn1tok_full):
    import ml_dtypes
    n0 = 2 * core
    W1 = np.asarray(d["L1_ffn_W1"])
    W2 = np.asarray(d["L1_ffn_W2"])
    b1 = np.asarray(d["L1_ffn_b1"])
    b2 = np.asarray(d["L1_ffn_b2"])
    pe = np.concatenate([p_vec.reshape(128, 128), e_vec.reshape(128, 128).astype(np.float32)], 1)
    return {
        "pe_all": pe.astype(np.float32),
        "xn1tok": xn1tok_full.astype(ml_dtypes.bfloat16),
        "expid": np.stack([np.full(128, n0, np.float32), np.full(128, n0 + 1, np.float32)], 1),
        "w1blk": np.stack([blk_lhsT(W1[n0]), blk_lhsT(W1[n0 + 1])]),
        "w2blk": np.stack([W2[n0].reshape(16, 128, 1024), W2[n0 + 1].reshape(16, 128, 1024)]),
        "b1s": np.stack([blk_scales(b1[n0]), blk_scales(b1[n0 + 1])]),
        "b2r": np.stack([b2[n0].reshape(1, 1024), b2[n0 + 1].reshape(1, 1024)]),
        "iotatok": (np.arange(128)[:, None] * 128 + np.arange(128)[None, :]).astype(np.float32),
        "slmat": (np.arange(128)[:, None] < np.arange(128)[None, :]).astype(np.float32),
        "iotaf": np.broadcast_to(np.arange(128, dtype=np.float32), (128, 128)).copy(),
        "consts": np.stack([np.ones(128, np.float32), np.full(128, 1 / 1024, np.float32)], 1),
        "identbf": np.eye(128, dtype=np.float32).astype(ml_dtypes.bfloat16),
    }








def emit_stageE(nc, t, block, T, upto=4):
    ld, tn, sc, ve, st = (T[k] for k in ["ld", "tn", "sc", "ve", "st"])

    o1_sb = nc.alloc_sbuf_tensor("o1_sb", [128, DC * TOK], BF16)
    s1_sb = nc.alloc_sbuf_tensor("s1p_sb", [128, DC * TOK], BF16)
    g1T_sb = nc.alloc_sbuf_tensor("g1T_sb", [128, DC * TOK], BF16)
    xn_tok = nc.alloc_sbuf_tensor("xntok_sb", [128, 16 * D], BF16)
    ks_sb = nc.alloc_sbuf_tensor("ks_sb", [128, 2 * 16 * 16], F32)
    io_sb = nc.alloc_sbuf_tensor("io1280_sb", [128, 16], F32)
    ident = nc.alloc_sbuf_tensor("identE", [128, 128], BF16)
    misc = nc.alloc_sbuf_tensor("miscE", [128, 8], F32)
    offu = nc.alloc_sbuf_tensor("offuE", [128, 16], U32)
    gtok = nc.alloc_sbuf_tensor("gtok", [128, 2 * D], F32)
    gbf = nc.alloc_sbuf_tensor("gbf", [128, 2 * D], BF16)
    g1tok = nc.alloc_sbuf_tensor("g1tok", [128, 2 * D], BF16)
    wbufE = nc.alloc_sbuf_tensor("wbufE", [128, 2 * DC * 128], BF16)
    rt = nc.alloc_sbuf_tensor("rtE", [128, 8], F32)

    psA = nc.alloc_psum_tensor("psAE", [128, 1024], F32)
    psC = nc.alloc_psum_tensor("psCE", [128, 1024], F32)
    pst = nc.alloc_psum_tensor("pstE", [128, 512], BF16)

    psums = [psA[:, 0:512], psA[:, 512:1024], psC[:, 0:512], psC[:, 512:1024]]
    psum_free = {}

    o1_t = lambda c: o1_sb[:, c * TOK:(c + 1) * TOK]
    kept_my = ks_sb[:, 0:256]
    slot_my = ks_sb[:, 256:512]

    @block.sync
    def _(sync):
        for c in range(DC):
            ld.inc(sync.dma_start(out=o1_t(c), in_=t["o1T"][c * 128:(c + 1) * 128, :]), 16)
        ld.inc(sync.dma_start(out=ks_sb[:, 0:256],
                              in_=t["kept_my"].rearrange("p a e -> p (a e)")), 16)
        ld.inc(sync.dma_start(out=ks_sb[:, 256:512],
                              in_=t["slot_my"].rearrange("p a e -> p (a e)")), 16)
        ld.inc(sync.dma_start(out=io_sb[:, :], in_=t["iota1280"]), 16)
        ld.inc(sync.dma_start(out=ident[:, :], in_=t["identbf"]), 16)
        ld.inc(sync.dma_start(out=misc[:, 0:2], in_=t["consts"]), 16)
        for s in range(16):
            ld.inc(sync.dma_start(out=xn_tok[:, s * D:(s + 1) * D],
                                  in_=t["xn1tok"][s * 128:(s + 1) * 128, :]), 16)

    def evac(eng, oc, tt, ps):
        return eng.tensor_copy(s1_sb[:, oc * TOK + tt * 512: oc * TOK + (tt + 1) * 512], ps)

    emit_proj(nc, block, T, t["wproj"], DC, DC, o1_t, wbufE, psums, psum_free,
              evac, "vector", ve, extra_mm_wait=(ld,))

    if upto < 2:
        @block.gpsimd
        def _(gpsimd):
            ve.wait(gpsimd)
            for c in range(DC):
                st.inc(gpsimd.dma_start(out=t["yT"][c * 128:(c + 1) * 128, :],
                                        in_=s1_sb[:, c * TOK:(c + 1) * TOK]), 16)
            st.wait(gpsimd)
        return

    gath_st = {}
    sel_ve = {}
    silu_sc = {}
    for s in range(16):
        par = s % 2

        @block.vector
        def _(vector, s=s, par=par):
            if s == 0:
                ld.wait(vector)
            if s >= 2:
                sc.wait(vector, silu_sc[s - 2])
            km = kept_my[:, s * 16:(s + 1) * 16]
            sm = slot_my[:, s * 16:(s + 1) * 16]
            ve.inc(vector.tensor_tensor(gtok[:, par * D: par * D + 16], sm, io_sb[:, :], ALU.add))
            ve.wait(vector)
            ve.inc(vector.tensor_tensor(gtok[:, par * D: par * D + 16],
                                        gtok[:, par * D: par * D + 16], km, ALU.mult))
            ve.wait(vector)
            ve.inc(vector.reduce_sum(rt[:, 0:1], gtok[:, par * D: par * D + 16], axis=AX.X))
            ve.wait(vector)
            ve.inc(vector.reduce_sum(rt[:, 1:2], km, axis=AX.X))
            ve.wait(vector)
            ve.inc(vector.tensor_copy(offu[:, s:s + 1], rt[:, 0:1]))
            sel_ve[("off", s)] = ve.n

        gpt = [T["gp"], T["gp2"]]

        @block.gpsimd
        def _(gpsimd, s=s, par=par):
            ve.wait(gpsimd, sel_ve[("off", s)])
            gpt[par].inc(gpsimd.indirect_dma_start(
                out=gbf[:, par * D:(par + 1) * D], out_offset=None,
                in_=t["moe_all"],
                in_offset=IndirectOffsetOnAxis(ap=offu[:, s:s + 1], axis=0)), 16)
            gath_st[s] = gpt[par].n

        @block.vector
        def _(vector, s=s, par=par):
            gpt[par].wait(vector, gath_st[s])
            ve.inc(vector.tensor_tensor(gtok[:, par * D:(par + 1) * D],
                                        gbf[:, par * D:(par + 1) * D],
                                        xn_tok[:, s * D:(s + 1) * D], ALU.subtract))
            ve.wait(vector)
            ve.inc(vector.tensor_scalar(out=gtok[:, par * D:(par + 1) * D],
                                        in0=gtok[:, par * D:(par + 1) * D],
                                        scalar1=rt[:, 1:2], scalar2=None, op0=ALU.mult))
            ve.wait(vector)
            ve.inc(vector.tensor_tensor(gtok[:, par * D:(par + 1) * D],
                                        gtok[:, par * D:(par + 1) * D],
                                        xn_tok[:, s * D:(s + 1) * D], ALU.add))
            sel_ve[s] = ve.n

        @block.scalar
        def _(scalar, s=s, par=par):
            ve.wait(scalar, sel_ve[s])
            sc.inc(scalar.activation(out=g1tok[:, par * D:(par + 1) * D],
                                     in_=gtok[:, par * D:(par + 1) * D], func=AF.Silu))
            silu_sc[s] = sc.n

        import os as _os
        if upto < 3 or s >= int(_os.environ.get("TLIM", "16")):
            continue
        for c in range(DC):
            u = s * DC + c

            @block.tensor
            def _(tensor, s=s, c=c, u=u, par=par):
                sc.wait(tensor, silu_sc[s])
                if u >= 2:
                    g_trk, g_n = sel_ve[("tp", u - 2)]
                    sc.wait(tensor, g_n)
                mm = tensor.transpose(pst[:, (u % 2) * 128:(u % 2) * 128 + 128],
                                      g1tok[:, par * D + c * 128: par * D + (c + 1) * 128],
                                      ident[:, :])
                tn.inc(mm)
                sel_ve[("tptn", u)] = tn.n

            @block.scalar
            def _(scalar, s=s, c=c, u=u):
                tn.wait(scalar, sel_ve[("tptn", u)])
                sc.inc(scalar.copy(
                    out=g1T_sb[:, c * TOK + s * 128: c * TOK + (s + 1) * 128],
                    in_=pst[:, (u % 2) * 128:(u % 2) * 128 + 128]))
                sel_ve[("tp", u)] = ("sc", sc.n)

    if upto < 4:
        @block.gpsimd
        def _(gpsimd):
            ve.wait(gpsimd)
            sc.wait(gpsimd)
            for c in range(DC):
                st.inc(gpsimd.dma_start(out=t["yT"][c * 128:(c + 1) * 128, :],
                                        in_=g1T_sb[:, c * TOK:(c + 1) * TOK] if upto == 3 else s1_sb[:, c * TOK:(c + 1) * TOK]), 16)
            st.wait(gpsimd)
            T["gp"].wait(gpsimd)
            T["gp2"].wait(gpsimd)
        return

    with (
        nc.sbuf_tensor("x1stg", [128, 2 * TOK], F32) as x1stg,
        nc.sbuf_tensor("fstg", [128, 2 * 2 * TOK], F32) as fstg,
    ):
        ldt = [T["ld"], T["ld2"]]
        ld_after = {}

        @block.sync
        def _(sync):
            ldt[0].wait(sync)
            ldt[1].wait(sync)
            for c in range(2):
                ldt[c % 2].inc(sync.dma_start(out=x1stg[:, (c % 2) * TOK:((c % 2) + 1) * TOK],
                                              in_=t["x1T"][c * 128:(c + 1) * 128, :]), 16)
                ld_after[c] = ldt[c % 2].n

        fin_ve = {}
        for c in range(DC):
            par = c % 2
            sf = fstg[:, par * 2 * TOK: par * 2 * TOK + TOK]
            gf = fstg[:, par * 2 * TOK + TOK: par * 2 * TOK + 2 * TOK]

            @block.scalar
            def _(scalar, c=c, par=par, sf=sf, gf=gf):
                if c < 2:
                    ve.wait(scalar)
                    sc.wait(scalar)
                ldt[par].wait(scalar, ld_after[c])
                scalar.copy(out=sf, in_=s1_sb[:, c * TOK:(c + 1) * TOK])
                sc.inc(scalar.copy(out=gf, in_=g1T_sb[:, c * TOK:(c + 1) * TOK]))
                fin_ve[("sc", c)] = sc.n

            @block.vector
            def _(vector, c=c, par=par, sf=sf, gf=gf):
                sc.wait(vector, fin_ve[("sc", c)])
                ve.inc(vector.tensor_tensor(sf, sf, gf, ALU.mult))
                ve.wait(vector)
                ve.inc(vector.tensor_tensor(x1stg[:, par * TOK:(par + 1) * TOK],
                                            x1stg[:, par * TOK:(par + 1) * TOK], sf, ALU.add))
                fin_ve[c] = ve.n

            stt = [T["st"], T["st2"]]

            @block.gpsimd
            def _(gpsimd, c=c, par=par):
                ve.wait(gpsimd, fin_ve[c])
                stt[par].inc(gpsimd.dma_start(out=t["yT"][c * 128:(c + 1) * 128, :],
                                              in_=x1stg[:, par * TOK:(par + 1) * TOK]), 16)
                fin_ve[("st", c)] = stt[par].n

            nxt = c + 2
            if nxt < DC:
                @block.sync
                def _(sync, nxt=nxt, par=par, c=c):
                    stt[par].wait(sync, fin_ve[("st", c)])
                    ldt[par].inc(sync.dma_start(out=x1stg[:, par * TOK:(par + 1) * TOK],
                                                in_=t["x1T"][nxt * 128:(nxt + 1) * 128, :]), 16)
                    ld_after[nxt] = ldt[par].n

    @block.gpsimd
    def _(gpsimd):
        st.wait(gpsimd)
        T["st2"].wait(gpsimd)
        T["gp"].wait(gpsimd)
        T["gp2"].wait(gpsimd)


def build_stageE(upto=4):
    nc = bass.Bass("TRN2", target_bir_lowering=False, debug=False)
    t = {}
    t["x1T"] = nc.dram_tensor("x1T", [D, TOK], F32, kind="ExternalInput").ap()
    t["o1T"] = nc.dram_tensor("o1T", [D, TOK], BF16, kind="ExternalInput").ap()
    t["wproj"] = nc.dram_tensor("wproj", [DC, 128, DC, 128], F32, kind="ExternalInput").ap()
    t["moe_all"] = nc.dram_tensor("moe_all", [16 * 1280, D], BF16, kind="ExternalInput").ap()
    t["xn1tok"] = nc.dram_tensor("xn1tok", [TOK, D], BF16, kind="ExternalInput").ap()
    t["kept_my"] = nc.dram_tensor("kept_my", [128, 16, 16], F32, kind="ExternalInput").ap()
    t["slot_my"] = nc.dram_tensor("slot_my", [128, 16, 16], F32, kind="ExternalInput").ap()
    t["iota1280"] = nc.dram_tensor("iota1280", [128, 16], F32, kind="ExternalInput").ap()
    t["identbf"] = nc.dram_tensor("identbf", [128, 128], BF16, kind="ExternalInput").ap()
    t["consts"] = nc.dram_tensor("consts", [128, 2], F32, kind="ExternalInput").ap()
    t["yT"] = nc.dram_tensor("yT", [D, TOK], F32, kind="ExternalOutput").ap()
    with nc.Block() as block:
        T = make_tracks(nc)
        emit_stageE(nc, t, block, T, upto=upto)
    return nc






_PROGS = {}


def _progs():
    if not _PROGS:
        _PROGS["A"] = build_stageA()
        _PROGS["B"] = build_stageB()
        _PROGS["C"] = build_stageC()
        _PROGS["D1"] = build_stageD1()
        _PROGS["D2"] = build_stageD2()
        _PROGS["E"] = build_stageE()
    return _PROGS


def _launch(nc, in_maps):
    return run_bass_kernel_spmd(nc, in_maps, core_ids=list(range(8))).results


def kernel(**inputs):
    d = {k: np.asarray(v) for k, v in inputs.items()}
    P = _progs()
    NC = 8

    resA = _launch(P["A"], [host_inputs_stageA(d, c) for c in range(NC)])

    qT_full = np.stack([np.concatenate([resA[4 * b + i]["qT_o"] for i in range(4)], axis=1)
                        for b in range(2)])
    kT_full = np.stack([np.concatenate([resA[4 * b + i]["kT_o"] for i in range(4)], axis=1)
                        for b in range(2)])
    vv_full = np.stack([np.concatenate([resA[4 * b + i]["vv_o"] for i in range(4)], axis=0)
                        for b in range(2)])
    resB = _launch(P["B"], [host_inputs_stageB(c, qT_full, kT_full, vv_full)
                            for c in range(NC)])

    s0_full = np.empty((16384, 1024), np.float32)
    for b in range(2):
        sT = np.concatenate([resB[4 * b + i]["s0h"].reshape(256, 8192)
                             for i in range(4)], axis=0)
        s0_full[b * 8192:(b + 1) * 8192] = sT.astype(np.float32).T
    g0_full = np.empty((16384, 1024), np.float32)
    for c in range(NC):
        g0_full[c * TOK:(c + 1) * TOK] = resA[c]["g0T"].T
    x0_full = d["L0_attn_x"].reshape(16384, 1024)
    resC = _launch(P["C"], [host_inputs_stageC(d, c, s0_full, g0_full, x0_full)
                            for c in range(NC)])

    q1_full = np.stack([np.concatenate([resC[4 * b + i]["q1T"] for i in range(4)], axis=1)
                        for b in range(2)])
    k1_full = np.stack([np.concatenate([resC[4 * b + i]["k1T"] for i in range(4)], axis=1)
                        for b in range(2)])
    v1_full = np.stack([np.concatenate([resC[4 * b + i]["vv1"] for i in range(4)], axis=0)
                        for b in range(2)])
    resD1 = _launch(P["D1"], [host_inputs_stageD1(c, q1_full, k1_full, v1_full)
                              for c in range(NC)])

    p_vec = np.concatenate([resC[c]["pe_o"][:, 0:16].T.reshape(-1) for c in range(NC)])
    e_vec = np.concatenate([resC[c]["pe_o"][:, 16:32].T.reshape(-1) for c in range(NC)])
    xn1tok_full = np.concatenate([resC[c]["xn1tok"] for c in range(NC)], axis=0)
    resD2 = _launch(P["D2"], [host_inputs_stageD2(d, c, p_vec, e_vec, xn1tok_full)
                              for c in range(NC)])

    moe_all = np.concatenate([resD2[c]["moe_out"].reshape(2 * 1280, 1024)
                              for c in range(NC)], axis=0)
    kept_full = np.empty((16, 16384), np.float32)
    slot_full = np.empty((16, 16384), np.float32)
    for c in range(NC):
        for j in range(2):
            kept_full[2 * c + j] = resD2[c]["keptslot"][j, 0].reshape(-1)
            slot_full[2 * c + j] = resD2[c]["keptslot"][j, 1].reshape(-1)
    o1_fullT = [np.concatenate([resD1[4 * b + i]["s1h"].reshape(256, 8192)
                                for i in range(4)], axis=0) for b in range(2)]
    wproj_blk = blk_lhsT(d["L1_attn_w_proj"].T)
    iota1280 = np.broadcast_to(np.arange(16, dtype=np.float32) * 1280, (128, 16)).copy()
    identbf = np.eye(128, dtype=np.float32).astype(ml_dtypes.bfloat16)
    consts = np.stack([np.ones(128, np.float32), np.full(128, 1 / 1024, np.float32)], 1)
    in_mapsE = []
    for c in range(NC):
        b = c // 4
        sl = slice(c * TOK, (c + 1) * TOK)
        km = kept_full[:, sl].T.reshape(16, 128, 16).transpose(1, 0, 2)
        sm = slot_full[:, sl].T.reshape(16, 128, 16).transpose(1, 0, 2)
        in_mapsE.append({
            "x1T": resC[c]["x1T"],
            "o1T": np.ascontiguousarray(o1_fullT[b][:, (c % 4) * TOK:(c % 4 + 1) * TOK]),
            "wproj": wproj_blk,
            "moe_all": moe_all,
            "xn1tok": resC[c]["xn1tok"],
            "kept_my": np.ascontiguousarray(km),
            "slot_my": np.ascontiguousarray(sm),
            "iota1280": iota1280,
            "identbf": identbf,
            "consts": consts,
        })
    resE = _launch(P["E"], in_mapsE)

    y = np.empty((16384, 1024), np.float32)
    for c in range(NC):
        y[c * TOK:(c + 1) * TOK] = resE[c]["yT"].T
    return y.reshape(2, 8192, 1024)


# version 03 (final)
# speedup vs baseline: 1.5950x; 1.5950x over previous
import numpy as np
import ml_dtypes
import concourse.bass as bass
import concourse.mybir as mybir
from concourse.bass_utils import run_bass_kernel_spmd
from concourse.bass_core_rust import IndirectOffsetOnAxis



F32 = mybir.dt.float32
BF16 = mybir.dt.bfloat16
U32 = mybir.dt.uint32
U8 = mybir.dt.uint8
AF = mybir.ActivationFunctionType
ALU = mybir.AluOpType
AX = mybir.AxisListType

D = 1024
TOK = 2048
DC = 8
HID = 4096
HC = 32
QT = 4


class Track:
    def __init__(self, nc, name):
        self.sem = nc.alloc_semaphore(name)
        self.n = 0

    def inc(self, inst, k=1):
        inst.then_inc(self.sem, k)
        self.n += k
        return inst

    def wait(self, eng, n=None):
        eng.wait_ge(self.sem, self.n if n is None else n)


def make_tracks(nc):
    return {k: Track(nc, k) for k in ["ld", "ld2", "wl", "wl2", "tn", "sc", "ve", "st", "st2", "gp", "gp2"]}


def blk_lhsT(w_in_out):
    i, o = w_in_out.shape
    return np.ascontiguousarray(
        w_in_out.reshape(i // 128, 128, o // 128, 128).transpose(2, 1, 0, 3))


def blk_scales(s):
    return np.ascontiguousarray(s.reshape(-1, 128).T)



def emit_proj(nc, block, T, w_dram, n_oc, n_ic, rhs_tile, wbuf, psums, psum_free,
              evac_fn, evac_eng, evac_track, n_tt=QT, nbuf=2, extra_mm_wait=(),
              flush_fn=None, after_evac=None, evac_prewait=None):
    tn = T["tn"]
    wlt = [T["wl"], T["wl2"]]
    ev = evac_track
    blk_w = n_ic * 128
    wtile = lambda par, ic: wbuf[:, par * blk_w + ic * 128: par * blk_w + (ic + 1) * 128]
    tn_after = {}
    wl_after = {}
    ev_after = {}

    tn_at_entry = tn.n

    @block.gpsimd
    def _(gpsimd):
        if tn_at_entry:
            tn.wait(gpsimd, tn_at_entry)
        for oc in range(min(nbuf, n_oc)):
            wlt[oc % 2].inc(gpsimd.dma_start(
                out=wbuf[:, (oc % nbuf) * blk_w:((oc % nbuf) + 1) * blk_w],
                in_=w_dram[oc].rearrange("p ic o -> p (ic o)")), 16)
            wl_after[oc] = wlt[oc % 2].n

    for oc in range(n_oc):
        par = oc % nbuf

        @block.tensor
        def _(tensor, oc=oc, par=par):
            wlt[oc % 2].wait(tensor, wl_after[oc])
            if oc == 0:
                for t in extra_mm_wait:
                    t.wait(tensor)
            for tt in range(n_tt):
                slot = (oc * n_tt + tt) % len(psums)
                if slot in psum_free:
                    ftrk, fn = psum_free[slot]
                    ftrk.wait(tensor, fn)
                ps = psums[slot]
                for ic in range(n_ic):
                    mm = tensor.matmul(ps[:, :], wtile(par, ic),
                                       rhs_tile(ic)[:, tt * 512:(tt + 1) * 512],
                                       start=(ic == 0), stop=(ic == n_ic - 1))
                tn.inc(mm)
                tn_after[(oc, tt)] = tn.n

        eng_sel = getattr(block, evac_eng)

        @eng_sel
        def _(eng, oc=oc):
            for tt in range(n_tt):
                slot = (oc * n_tt + tt) % len(psums)
                tn.wait(eng, tn_after[(oc, tt)])
                if evac_prewait is not None:
                    evac_prewait(eng, oc, tt)
                last = evac_fn(eng, oc, tt, psums[slot])
                ev.inc(last)
                psum_free[slot] = (ev, ev.n)
                ev_after[(oc, tt)] = ev.n
                if after_evac is not None:
                    after_evac(oc, tt, ev.n)

        if flush_fn is not None:
            @block.gpsimd
            def _(gpsimd, oc=oc):
                ev.wait(gpsimd, ev_after[(oc, n_tt - 1)])
                flush_fn(gpsimd, oc)

        nxt = oc + nbuf
        if nxt < n_oc:
            @block.gpsimd
            def _(gpsimd, oc=oc, nxt=nxt):
                tn.wait(gpsimd, tn_after[(oc, n_tt - 1)])
                wlt[nxt % 2].inc(gpsimd.dma_start(
                    out=wbuf[:, (nxt % nbuf) * blk_w:((nxt % nbuf) + 1) * blk_w],
                    in_=w_dram[nxt].rearrange("p ic o -> p (ic o)")), 16)
            wl_after[nxt] = wlt[nxt % 2].n
    return tn_after, ev_after


def run_cores(nc, in_maps, **kw):
    return run_bass_kernel_spmd(nc, in_maps, core_ids=list(range(len(in_maps))), **kw)




import ml_dtypes

SC_S1, SC_S2, SC_H1, SC_H2, SC_S3 = 0, HC, 2 * HC, 3 * HC, 4 * HC


def emit_ln(nc, block, T, x_tile, xn_tile, misc, onesrow, rowbuf, sq_sb, bcast,
            psr, psB, pre_wait=(), ntok=TOK):
    tn, sc, ve = T["tn"], T["sc"], T["ve"]
    nq = ntok // 512
    psr_free = [0]
    psB_free = [0]
    for q in range(nq):
        qs = slice(q * 512, (q + 1) * 512)
        R = lambda i: rowbuf[0:1, i * 512:(i + 1) * 512]
        m_row, msq_row, var_row, sd_row, r_row, mr_row = (R(i) for i in range(6))

        @block.scalar
        def _(scalar, q=q, qs=qs):
            if q == 0:
                for t in pre_wait:
                    t.wait(scalar)
            else:
                tn.wait(scalar)
                ve.wait(scalar)
            for c in range(DC):
                inst = scalar.activation(out=sq_sb[:, c * 512:(c + 1) * 512],
                                         in_=x_tile(c)[:, qs], func=AF.Square)
            sc.inc(inst)

        @block.tensor
        def _(tensor, q=q, qs=qs):
            sc.wait(tensor)
            if psr_free[0]:
                ve.wait(tensor, psr_free[0])
            for c in range(DC):
                tensor.matmul(psr[0:1, 0:512], misc[:, 1:2], x_tile(c)[:, qs],
                              start=(c == 0), stop=(c == DC - 1))
            for c in range(DC):
                mm = tensor.matmul(psr[0:1, 512:1024], misc[:, 1:2],
                                   sq_sb[:, c * 512:(c + 1) * 512],
                                   start=(c == 0), stop=(c == DC - 1))
            tn.inc(mm)

        @block.vector
        def _(vector, q=q):
            tn.wait(vector)
            ve.inc(vector.tensor_copy(m_row, psr[0:1, 0:512]))
            ve.wait(vector)
            ve.inc(vector.tensor_tensor(msq_row, m_row, m_row, ALU.mult))
            ve.wait(vector)
            ve.inc(vector.tensor_tensor(var_row, psr[0:1, 512:1024], msq_row, ALU.subtract))
            ve.wait(vector)
            ve.inc(vector.tensor_scalar_add(var_row, var_row, 1e-5))
        psr_free[0] = ve.n

        @block.scalar
        def _(scalar, q=q):
            ve.wait(scalar)
            sc.inc(scalar.sqrt(sd_row, var_row))

        @block.vector
        def _(vector, q=q):
            sc.wait(vector)
            ve.inc(vector.reciprocal(r_row, sd_row))
            ve.wait(vector)
            ve.inc(vector.tensor_tensor(mr_row, m_row, r_row, ALU.mult))

        @block.tensor
        def _(tensor, q=q):
            ve.wait(tensor)
            if psB_free[0]:
                ve.wait(tensor, psB_free[0])
            tensor.matmul(psB[:, 0:512], onesrow[0:1, :], r_row, start=True, stop=True)
            tn.inc(tensor.matmul(psB[:, 512:1024], onesrow[0:1, :], mr_row,
                                 start=True, stop=True))

        @block.vector
        def _(vector, q=q):
            tn.wait(vector)
            vector.tensor_copy(bcast[:, 0:512], psB[:, 0:512])
            ve.inc(vector.tensor_copy(bcast[:, 512:1024], psB[:, 512:1024]))

        @block.vector
        def _(vector, q=q, qs=qs):
            for c in range(DC):
                vector.tensor_tensor(sq_sb[:, c * 512:(c + 1) * 512], x_tile(c)[:, qs],
                                     bcast[:, 0:512], ALU.mult)
                inst = vector.tensor_tensor(xn_tile(c)[:, qs],
                                            sq_sb[:, c * 512:(c + 1) * 512],
                                            bcast[:, 512:1024], ALU.subtract)
            ve.inc(inst)
        psB_free[0] = ve.n
        if q + 1 < nq:
            pass

    return


def emit_stageA(nc, t, block, T):
    ld, wl, tn, sc, ve, st, gp = (T[k] for k in ["ld", "wl", "tn", "sc", "ve", "st", "gp"])

    xn_sb = nc.alloc_sbuf_tensor("xn_sb", [128, DC * TOK], BF16)
    misc = nc.alloc_sbuf_tensor("misc", [128, 8], F32)
    scal_sb = nc.alloc_sbuf_tensor("scal_sb", [128, 4 * HC + DC], F32)
    onesrow = nc.alloc_sbuf_tensor("onesrow", [1, 128], F32)
    ident = nc.alloc_sbuf_tensor("ident", [128, 128], BF16)

    psA = nc.alloc_psum_tensor("psA", [128, 1024], F32)
    psB = nc.alloc_psum_tensor("psB", [128, 1024], F32)
    psC = nc.alloc_psum_tensor("psC", [128, 1024], F32)
    pst = nc.alloc_psum_tensor("pst", [128, 512], BF16)
    psr = psA[0:1, :]

    xn_t = lambda c: xn_sb[:, c * TOK:(c + 1) * TOK]
    psums = [psA[:, 0:512], psA[:, 512:1024], psC[:, 0:512], psC[:, 512:1024]]
    psum_free = {}

    @block.sync
    def _(sync):
        ld.inc(sync.dma_start(out=misc[:, 0:2], in_=t["consts"]), 16)
        ld.inc(sync.dma_start(out=scal_sb[:, :], in_=t["scls"]), 16)
        ld.inc(sync.dma_start(out=ident[:, :], in_=t["identbf"]), 16)

    @block.scalar
    def _(scalar):
        ld.wait(scalar)
        sc.inc(scalar.copy(out=onesrow[0:1, :], in_=misc[0:1, 0:1].broadcast_to([1, 128])))

    with (
        nc.sbuf_tensor("x_sb", [128, DC * TOK], F32) as x_sb,
        nc.sbuf_tensor("sq_sb", [128, DC * 512], F32) as sq_sb,
        nc.sbuf_tensor("rowbuf", [1, 6 * 512], F32) as rowbuf,
        nc.sbuf_tensor("bcast", [128, 1024], F32) as bcast,
    ):
        x_t = lambda c: x_sb[:, c * TOK:(c + 1) * TOK]

        @block.sync
        def _(sync):
            for c in range(DC):
                ld.inc(sync.dma_start(out=x_t(c), in_=t["xT"][c * 128:(c + 1) * 128, :]), 16)

        emit_ln(nc, block, T, x_t, xn_t, misc, onesrow, rowbuf, sq_sb, bcast,
                psr, psB, pre_wait=(ld, sc))

        @block.gpsimd
        def _(gpsimd):
            ve.wait(gpsimd)
            for c in range(DC):
                st.inc(gpsimd.dma_start(out=t["xnT_o"][c * 128:(c + 1) * 128, :], in_=xn_t(c)), 16)

    with (
        nc.sbuf_tensor("q_sb", [128, DC * TOK], BF16) as q_sb,
        nc.sbuf_tensor("k_sb", [128, DC * TOK], BF16) as k_sb,
        nc.sbuf_tensor("vT_sb", [128, DC * TOK], BF16) as vT_sb,
        nc.sbuf_tensor("vv_sb", [128, 16 * D], BF16) as vv_sb,
        nc.sbuf_tensor("wqkv", [128, 2 * DC * 128], BF16) as wqkv,
    ):
        for wname, dst in (("wq", q_sb), ("wk", k_sb), ("wv", vT_sb)):
            def evac(eng, oc, tt, ps, dst=dst):
                return eng.tensor_copy(
                    dst[:, oc * TOK + tt * 512: oc * TOK + (tt + 1) * 512], ps)

            emit_proj(nc, block, T, t[wname], DC, DC, xn_t, wqkv, psums, psum_free,
                      evac, "vector", ve, extra_mm_wait=(ve,) if wname == "wq" else ())

        @block.gpsimd
        def _(gpsimd):
            ve.wait(gpsimd)
            for c in range(DC):
                st.inc(gpsimd.dma_start(out=t["qT_o"][c * 128:(c + 1) * 128, :],
                                        in_=q_sb[:, c * TOK:(c + 1) * TOK]), 16)
                st.inc(gpsimd.dma_start(out=t["kT_o"][c * 128:(c + 1) * 128, :],
                                        in_=k_sb[:, c * TOK:(c + 1) * TOK]), 16)

        vv_t = lambda s: vv_sb[:, s * D:(s + 1) * D]
        tp_tn = {}
        tp_sc = {}
        for u in range(16 * DC):
            s_i, oc = divmod(u, DC)

            @block.tensor
            def _(tensor, u=u, s_i=s_i, oc=oc):
                if u == 0:
                    ve.wait(tensor)
                if u >= 2:
                    sc.wait(tensor, tp_sc[u - 2])
                mm = tensor.transpose(
                    pst[:, (u % 2) * 128:(u % 2) * 128 + 128],
                    vT_sb[:, oc * TOK + s_i * 128: oc * TOK + (s_i + 1) * 128],
                    ident[:, :])
                tn.inc(mm)
                tp_tn[u] = tn.n

            @block.scalar
            def _(scalar, u=u, s_i=s_i, oc=oc):
                tn.wait(scalar, tp_tn[u])
                sc.inc(scalar.copy(out=vv_t(s_i)[:, oc * 128:(oc + 1) * 128],
                                   in_=pst[:, (u % 2) * 128:(u % 2) * 128 + 128]))
                tp_sc[u] = sc.n

        psum_free[2] = (sc, sc.n)
        psum_free[3] = (sc, sc.n)

        @block.gpsimd
        def _(gpsimd):
            sc.wait(gpsimd)
            for s in range(16):
                st.inc(gpsimd.dma_start(out=t["vv_o"][s * 128:(s + 1) * 128, :], in_=vv_t(s)), 16)

    with (
        nc.sbuf_tensor("hbuf", [128, HC * TOK], BF16) as hbuf,
        nc.sbuf_tensor("wffn", [128, 2 * HC * 128], BF16) as wffn,
        nc.sbuf_tensor("stg", [128, 4 * 512], BF16) as stg,
        nc.sbuf_tensor("ostg", [128, 2 * TOK], BF16) as ostg,
        nc.sbuf_tensor("gstg", [128, 2 * TOK], F32) as gstg,
    ):
        h_t = lambda j: hbuf[:, j * TOK:(j + 1) * TOK]

        @block.gpsimd
        def _(gpsimd):
            st.wait(gpsimd)

        def evac_a(eng, oc, tt, ps):
            return eng.activation(out=h_t(oc)[:, tt * 512:(tt + 1) * 512], in_=ps,
                                  func=AF.Silu, scale=scal_sb[:, SC_S1 + oc:SC_S1 + oc + 1])

        emit_proj(nc, block, T, t["w1"], HC, DC, xn_t, wffn, psums, psum_free,
                  evac_a, "scalar", sc, extra_mm_wait=(st,))

        sc_after = {}

        def evac_b(eng, oc, tt, ps):
            u = oc * QT + tt
            if u >= 4:
                ve.wait(eng, sc_after.get(("vmul", u - 4), 0))
            return eng.activation(out=stg[:, tt * 512:(tt + 1) * 512], in_=ps,
                                  func=AF.Identity, scale=scal_sb[:, SC_S2 + oc:SC_S2 + oc + 1])

        def rec_b(oc, tt, n):
            sc_after[(oc, tt)] = n

        tnA, evA = emit_proj(nc, block, T, t["w2"], HC, DC, xn_t, wffn, psums, psum_free,
                             evac_b, "scalar", sc, after_evac=rec_b)

        for oc in range(HC):
            @block.vector
            def _(vector, oc=oc):
                for tt in range(QT):
                    sc.wait(vector, sc_after[(oc, tt)])
                    inst = vector.tensor_tensor(
                        h_t(oc)[:, tt * 512:(tt + 1) * 512],
                        h_t(oc)[:, tt * 512:(tt + 1) * 512],
                        stg[:, tt * 512:(tt + 1) * 512], ALU.mult)
                    ve.inc(inst)
                    sc_after[("vmul", oc * QT + tt)] = ve.n

        st_after = {}

        def evac_h2(eng, oc, tt, ps):
            u = oc % 2
            if oc >= 2:
                st.wait(eng, st_after.get(oc - 2, 0))
            return eng.activation(out=ostg[:, u * TOK + tt * 512:u * TOK + (tt + 1) * 512],
                                  in_=ps, func=AF.Gelu,
                                  scale=scal_sb[:, SC_H1 + oc:SC_H1 + oc + 1])

        def flush_h2(gpsimd, oc):
            u = oc % 2
            st.inc(gpsimd.dma_start(
                out=t["h2_dram"][oc * 128:(oc + 1) * 128, :],
                in_=ostg[:, u * TOK:(u + 1) * TOK]), 16)
            st_after[oc] = st.n

        emit_proj(nc, block, T, t["h1w"], HC, HC, h_t, wffn, psums, psum_free,
                  evac_h2, "scalar", sc, extra_mm_wait=(ve,), flush_fn=flush_h2)

        @block.sync
        def _(sync):
            tn.wait(sync)
            st.wait(sync)
            ld.wait(sync)
            for j in range(HC):
                ld.inc(sync.dma_start(out=h_t(j), in_=t["h2_dram"][j * 128:(j + 1) * 128, :]), 16)

        def evac_h3(eng, oc, tt, ps):
            u = oc % 2
            if oc >= 2:
                st.wait(eng, st_after.get(("h3", oc - 2), 0))
            return eng.activation(out=ostg[:, u * TOK + tt * 512:u * TOK + (tt + 1) * 512],
                                  in_=ps, func=AF.Gelu,
                                  scale=scal_sb[:, SC_H2 + oc:SC_H2 + oc + 1])

        def flush_h3(gpsimd, oc):
            u = oc % 2
            st.inc(gpsimd.dma_start(
                out=t["h3_dram"][oc * 128:(oc + 1) * 128, :],
                in_=ostg[:, u * TOK:(u + 1) * TOK]), 16)
            st_after[("h3", oc)] = st.n

        emit_proj(nc, block, T, t["h2w"], HC, HC, h_t, wffn, psums, psum_free,
                  evac_h3, "scalar", sc, extra_mm_wait=(ld,), flush_fn=flush_h3)

        @block.sync
        def _(sync):
            tn.wait(sync)
            st.wait(sync)
            ld.wait(sync)
            for j in range(HC):
                ld.inc(sync.dma_start(out=h_t(j), in_=t["h3_dram"][j * 128:(j + 1) * 128, :]), 16)

        def evac_g(eng, oc, tt, ps):
            u = oc % 2
            if oc >= 2:
                st.wait(eng, st_after.get(("g", oc - 2), 0))
            return eng.activation(out=gstg[:, u * TOK + tt * 512:u * TOK + (tt + 1) * 512],
                                  in_=ps, func=AF.Silu,
                                  scale=scal_sb[:, SC_S3 + oc:SC_S3 + oc + 1])

        def flush_g(gpsimd, oc):
            u = oc % 2
            st.inc(gpsimd.dma_start(
                out=t["g0T"][oc * 128:(oc + 1) * 128, :],
                in_=gstg[:, u * TOK:(u + 1) * TOK]), 16)
            st_after[("g", oc)] = st.n

        emit_proj(nc, block, T, t["w3"], DC, HC, h_t, wffn, psums, psum_free,
                  evac_g, "scalar", sc, extra_mm_wait=(ld,), flush_fn=flush_g)

        @block.gpsimd
        def _(gpsimd):
            st.wait(gpsimd)


def build_stageA():
    nc = bass.Bass("TRN2", target_bir_lowering=False, debug=False)
    t = {}
    t["xT"] = nc.dram_tensor("xT", [D, TOK], F32, kind="ExternalInput").ap()
    for n, shp in (("wq", [DC, 128, DC, 128]), ("wk", [DC, 128, DC, 128]),
                   ("wv", [DC, 128, DC, 128]), ("w1", [HC, 128, DC, 128]),
                   ("w2", [HC, 128, DC, 128]), ("h1w", [HC, 128, HC, 128]),
                   ("h2w", [HC, 128, HC, 128]), ("w3", [DC, 128, HC, 128])):
        t[n] = nc.dram_tensor(n, shp, F32, kind="ExternalInput").ap()
    t["scls"] = nc.dram_tensor("scls", [128, 4 * HC + DC], F32, kind="ExternalInput").ap()
    t["consts"] = nc.dram_tensor("consts", [128, 2], F32, kind="ExternalInput").ap()
    t["identbf"] = nc.dram_tensor("identbf", [128, 128], BF16, kind="ExternalInput").ap()
    t["g0T"] = nc.dram_tensor("g0T", [D, TOK], F32, kind="ExternalOutput").ap()
    t["qT_o"] = nc.dram_tensor("qT_o", [D, TOK], BF16, kind="ExternalOutput").ap()
    t["kT_o"] = nc.dram_tensor("kT_o", [D, TOK], BF16, kind="ExternalOutput").ap()
    t["vv_o"] = nc.dram_tensor("vv_o", [TOK, D], BF16, kind="ExternalOutput").ap()
    t["xnT_o"] = nc.dram_tensor("xnT_o", [D, TOK], BF16, kind="ExternalOutput").ap()
    t["h2_dram"] = nc.dram_tensor("h2_spill", [HC * 128, TOK], BF16).ap()
    t["h3_dram"] = nc.dram_tensor("h3_spill", [HC * 128, TOK], BF16).ap()

    with nc.Block() as block:
        T = make_tracks(nc)
        emit_stageA(nc, t, block, T)
    return nc


def host_inputs_stageA(d, core):
    x = np.asarray(d["L0_attn_x"]).reshape(16384, 1024)
    sl = slice(core * TOK, (core + 1) * TOK)
    im = {
        "xT": np.ascontiguousarray(x[sl].T),
        "wq": blk_lhsT(np.asarray(d["L0_attn_W_q"])),
        "wk": blk_lhsT(np.asarray(d["L0_attn_W_k"])),
        "wv": blk_lhsT(np.asarray(d["L0_attn_W_v"])),
        "w1": blk_lhsT(np.asarray(d["L0_ffn_w1_w"]).T),
        "w2": blk_lhsT(np.asarray(d["L0_ffn_w2_w"]).T),
        "h1w": blk_lhsT(np.asarray(d["L0_ffn_h1_w"]).T),
        "h2w": blk_lhsT(np.asarray(d["L0_ffn_h2_w"]).T),
        "w3": blk_lhsT(np.asarray(d["L0_ffn_w3_w"]).T),
        "scls": np.concatenate([blk_scales(np.asarray(d[k])) for k in
                                ("L0_ffn_w1_s", "L0_ffn_w2_s", "L0_ffn_h1_s",
                                 "L0_ffn_h2_s", "L0_ffn_w3_s")], axis=1),
        "consts": np.stack([np.ones(128, np.float32),
                            np.full(128, 1.0 / 1024, np.float32)], axis=1),
        "identbf": np.eye(128, dtype=np.float32).astype(ml_dtypes.bfloat16),
    }
    return im







NB = 2731
NPC = 22
NQT = 6
QW = [512, 512, 512, 512, 512, 171]
PADB = 2816
SCALE = 0.125


def emit_stageB(nc, t, block, T):
    ld, tn, sc, ve, st = (T[k] for k in ["ld", "tn", "sc", "ve", "st"])

    qh_sb = nc.alloc_sbuf_tensor("qh_sb", [128, 2 * 8448], BF16)
    kh_sb = nc.alloc_sbuf_tensor("kh_sb", [128, 2 * 8448], BF16)
    vaug_h = nc.alloc_sbuf_tensor("vaug", [128, NPC * 512], BF16)
    e_sb = nc.alloc_sbuf_tensor("e_sb", [128, 16 * 512], BF16)
    s0_sb = nc.alloc_sbuf_tensor("s0_sb", [128, 2 * 8192], BF16)
    bm_sb = nc.alloc_sbuf_tensor("bm_sb", [128, 6 * 512], BF16)
    tm_sb = nc.alloc_sbuf_tensor("tm_sb", [128, 1], F32)
    cvec_sb = nc.alloc_sbuf_tensor("cvec_sb", [1, PADB], F32)
    onesrow = nc.alloc_sbuf_tensor("onesrowB", [1, 64], F32)
    misc = nc.alloc_sbuf_tensor("miscB", [128, 2], F32)
    rows = nc.alloc_sbuf_tensor("rowsB", [1, 4 * 512], F32)
    rb_sb = nc.alloc_sbuf_tensor("rb_sb", [64, 2 * 512], F32)

    psS = nc.alloc_psum_tensor("psS", [128, 4 * 512], F32)
    psO = nc.alloc_psum_tensor("psO", [128, 1024], F32)
    psb2 = nc.alloc_psum_tensor("psb2", [64, 1024], F32)

    @block.sync
    def _(sync):
        for i in range(2):
            ld.inc(sync.dma_start(out=qh_sb[:, i * 8448:(i + 1) * 8448], in_=t["qh"][i]), 16)
            ld.inc(sync.dma_start(out=kh_sb[:, i * 8448:(i + 1) * 8448], in_=t["kh"][i]), 16)
        ld.inc(sync.dma_start(out=bm_sb[:, :], in_=t["bmasks"]), 16)
        ld.inc(sync.dma_start(out=tm_sb[:, :], in_=t["tmask"]), 16)
        ld.inc(sync.dma_start(out=cvec_sb[:, :], in_=t["cvec"]), 16)
        ld.inc(sync.dma_start(out=misc[:, :], in_=t["consts"]), 16)

    @block.scalar
    def _(scalar):
        ld.wait(scalar)
        sc.inc(scalar.copy(out=onesrow[0:1, :], in_=misc[0:1, 0:1].broadcast_to([1, 64])))

    def vaug_lhsT(pc, h):
        return vaug_h[:, pc * 512 + 128 * h: pc * 512 + 128 * h + 128]

    GRP = 8
    stn = {}
    mask_done = {}
    av_done = {}
    po_free = {}
    rb_free = {}
    psb_free = {}
    v_ld = [0]
    eplan = lambda g: g // 2 + 2

    uq = 0
    gbase = 0
    for s in range(3):
        @block.gpsimd
        def _(gpsimd, s=s):
            if s > 0:
                tn.wait(gpsimd)
            src = t["vh"].rearrange("(r three) d -> r three d", three=3)
            for pc in range(NPC):
                st.inc(gpsimd.dma_start(
                    out=vaug_h[:, pc * 512: (pc + 1) * 512],
                    in_=src[pc * 128:(pc + 1) * 128, s:s + 1, :].squeeze(1)), 16)
        v_ld[0] = st.n

        for h in range(4):
            hp, hr = divmod(h, 2)
            qoff = hp * 8448
            hrow = hr * 64
            kt = lambda a, n, hrow=hrow, qoff=qoff: kh_sb[hrow:hrow + 64, qoff + a: qoff + a + 3 * (n - 1) + 1: 3]
            qt_ = lambda a, n, hrow=hrow, qoff=qoff: qh_sb[hrow:hrow + 64, qoff + a: qoff + a + 3 * (n - 1) + 1: 3]

            for qt in range(NQT):
                W = QW[qt]
                oslot = uq % 2
                ngr = (NPC + GRP - 1) // GRP

                for j in range(ngr + 1):
                    s_list = list(range(j * GRP, min((j + 1) * GRP, NPC))) if j < ngr else []
                    a_list = list(range((j - 1) * GRP, min(j * GRP, NPC))) if j >= 1 else []

                    @block.tensor
                    def _(tensor, s=s, h=h, qt=qt, W=W, oslot=oslot, uq=uq,
                          kt=kt, qt_=qt_, gbase=gbase, s_list=s_list, a_list=a_list):
                        for i in range(max(len(s_list), len(a_list))):
                            if i < len(s_list):
                                pc = s_list[i]
                                g = gbase + pc
                                if uq == 0 and pc == 0:
                                    ld.wait(tensor)
                                if g >= 4:
                                    sc.wait(tensor, g // 2)
                                tn.inc(tensor.matmul(
                                    psS[:, (g % 4) * 512:(g % 4) * 512 + W],
                                    kt(s + 384 * pc, 128),
                                    qt_(s + 1536 * qt, W),
                                    start=True, stop=True))
                                stn[g] = tn.n
                            if i < len(a_list):
                                pa = a_list[i]
                                ga = gbase + pa
                                if ga in mask_done:
                                    ve.wait(tensor, mask_done[ga])
                                else:
                                    sc.wait(tensor, eplan(ga))
                                if pa == 0:
                                    st.wait(tensor, v_ld[0])
                                    if oslot in po_free:
                                        ve.wait(tensor, po_free[oslot])
                                tn.inc(tensor.matmul(
                                    psO[:, oslot * 512: oslot * 512 + W],
                                    vaug_lhsT(pa, h),
                                    e_sb[:, (ga % 16) * 512:(ga % 16) * 512 + W],
                                    start=(pa == 0), stop=(pa == NPC - 1)))
                                av_done[ga] = tn.n

                    if s_list:
                        @block.scalar
                        def _(scalar, s_list=s_list, gbase=gbase):
                            for k in range(0, len(s_list), 2):
                                g0 = gbase + s_list[k]
                                tn.wait(scalar, stn[g0 + 1])
                                sc.inc(scalar.activation(
                                    out=e_sb[:, (g0 % 16) * 512:(g0 % 16) * 512 + 1024],
                                    in_=psS[:, (g0 % 4) * 512:(g0 % 4) * 512 + 1024],
                                    func=AF.Exp, scale=SCALE))
                                assert sc.n == eplan(g0), (sc.n, g0)

                        mk = []
                        for pc in s_list:
                            jj = pc - 4 * qt + 1
                            need_band = 0 <= jj < 6
                            need_tail = pc == NPC - 1
                            if need_band or need_tail:
                                mk.append((pc, jj, need_band, need_tail))
                        if mk:
                            @block.vector
                            def _(vector, mk=mk, gbase=gbase, W=W):
                                for pc, jj, need_band, need_tail in mk:
                                    g = gbase + pc
                                    sc.wait(vector, eplan(g))
                                    esl = e_sb[:, (g % 16) * 512:(g % 16) * 512 + W]
                                    if need_band:
                                        inst = vector.tensor_tensor(
                                            esl, esl, bm_sb[:, jj * 512: jj * 512 + W], ALU.mult)
                                    if need_tail:
                                        if need_band:
                                            ve.inc(inst)
                                            ve.wait(vector)
                                        inst = vector.tensor_scalar_mul(esl, esl, tm_sb[:, 0:1])
                                    ve.inc(inst)
                                    mask_done[g] = ve.n

                Wr = W if not (s == 2 and qt == NQT - 1) else W - 1
                dr = rows[0:1, oslot * 1024: oslot * 1024 + 512]
                dr2 = rows[0:1, oslot * 1024 + 512: oslot * 1024 + 1024]
                rb = rb_sb[:, oslot * 512: oslot * 512 + 512]

                @block.vector
                def _(vector, qt=qt, W=W, oslot=oslot, gbase=gbase):
                    tn.wait(vector, av_done[gbase + NPC - 1])
                    if oslot in rb_free:
                        ve.wait(vector, rb_free[oslot])
                    ve.inc(vector.tensor_tensor(dr[0:1, 0:W], psO[64:65, oslot * 512: oslot * 512 + W],
                                         cvec_sb[0:1, qt * 512: qt * 512 + W], ALU.add))
                    ve.wait(vector)
                    ve.inc(vector.reciprocal(dr2[0:1, 0:W], dr[0:1, 0:W]))
                recip_done = ve.n

                @block.tensor
                def _(tensor, W=W, oslot=oslot):
                    ve.wait(tensor, recip_done)
                    if oslot in psb_free:
                        ve.wait(tensor, psb_free[oslot])
                    tn.inc(tensor.matmul(psb2[:, oslot * 512: oslot * 512 + W],
                                         onesrow[0:1, :], dr2[0:1, 0:W],
                                         start=True, stop=True))
                bcast_done = tn.n

                @block.vector
                def _(vector, s=s, qt=qt, W=W, Wr=Wr, oslot=oslot, hp=hp, hrow=hrow, rb=rb):
                    tn.wait(vector, bcast_done)
                    ve.inc(vector.tensor_copy(rb[:, 0:W],
                                              psb2[:, oslot * 512: oslot * 512 + W]))
                    psb_free[oslot] = ve.n
                    ve.wait(vector)
                    base = hp * 8192 + s + 1536 * qt
                    out_ap = s0_sb[hrow:hrow + 64, base: base + 3 * (Wr - 1) + 1: 3]
                    ve.inc(vector.tensor_tensor(
                        out_ap, psO[0:64, oslot * 512: oslot * 512 + Wr],
                        rb[:, 0:Wr], ALU.mult))
                    po_free[oslot] = ve.n
                    rb_free[oslot] = ve.n
                uq += 1
                gbase += NPC

    @block.gpsimd
    def _(gpsimd):
        ve.wait(gpsimd)
        for i in range(2):
            st.inc(gpsimd.dma_start(out=t["s0h"][i], in_=s0_sb[:, i * 8192:(i + 1) * 8192]), 16)
        st.wait(gpsimd)


def build_stageB():
    nc = bass.Bass("TRN2", target_bir_lowering=False, debug=False)
    t = {}
    t["qh"] = nc.dram_tensor("qh", [2, 128, 8448], BF16, kind="ExternalInput").ap()
    t["kh"] = nc.dram_tensor("kh", [2, 128, 8448], BF16, kind="ExternalInput").ap()
    t["vh"] = nc.dram_tensor("vh", [8448, 512], BF16, kind="ExternalInput").ap()
    t["cvec"] = nc.dram_tensor("cvec", [1, PADB], F32, kind="ExternalInput").ap()
    t["bmasks"] = nc.dram_tensor("bmasks", [128, 6 * 512], BF16, kind="ExternalInput").ap()
    t["tmask"] = nc.dram_tensor("tmask", [128, 1], F32, kind="ExternalInput").ap()
    t["consts"] = nc.dram_tensor("consts", [128, 2], F32, kind="ExternalInput").ap()
    t["s0h"] = nc.dram_tensor("s0h", [2, 128, 8192], BF16, kind="ExternalOutput").ap()
    with nc.Block() as block:
        T = make_tracks(nc)
        emit_stageB(nc, t, block, T)
    return nc


def host_consts_B():
    import ml_dtypes
    cvec = np.zeros((1, PADB), np.float32)
    cvec[0, 0] = 2.0
    cvec[0, 1] = 1.0
    cvec[0, NB - 2] += 1.0
    cvec[0, NB - 1] += 2.0
    bm = np.ones((128, 6 * 512), np.float32)
    for j in range(6):
        d0 = (j - 1) * 128
        pl = np.arange(128)[:, None]
        ql = np.arange(512)[None, :]
        bm[:, j * 512:(j + 1) * 512] += (np.abs(d0 + pl - ql) <= 2).astype(np.float32)
    tm = (np.arange(128) < (NB - NPC * 128 + 128)).astype(np.float32).reshape(128, 1)
    consts = np.stack([np.ones(128, np.float32), np.full(128, 1 / 1024, np.float32)], 1)
    return (cvec, bm.astype(ml_dtypes.bfloat16), tm, consts)


def host_inputs_stageB(core, qT_full, kT_full, vv_full):
    import ml_dtypes
    b = core // 4
    h0 = 4 * (core % 4)
    cvec, bm, tm, consts = host_consts_B()
    qh = np.zeros((2, 128, 8448), dtype=ml_dtypes.bfloat16)
    kh = np.zeros((2, 128, 8448), dtype=ml_dtypes.bfloat16)
    qh[:, :, 0:8192] = qT_full[b][h0 * 64:(h0 + 4) * 64].reshape(2, 128, 8192)
    kh[:, :, 0:8192] = kT_full[b][h0 * 64:(h0 + 4) * 64].reshape(2, 128, 8192)
    vh = np.zeros((8448, 512), dtype=ml_dtypes.bfloat16)
    for hh in range(4):
        vh[0:8192, 128 * hh:128 * hh + 64] = vv_full[b][:, (h0 + hh) * 64:(h0 + hh + 1) * 64]
        vh[:, 128 * hh + 64:128 * hh + 128] = 1.0
    return {"qh": qh, "kh": kh, "vh": vh, "cvec": cvec, "bmasks": bm,
            "tmask": tm, "consts": consts}








def emit_stageC(nc, t, block, T):
    ld, wl, tn, sc, ve, st = (T[k] for k in ["ld", "wl", "tn", "sc", "ve", "st"])

    x_sb = nc.alloc_sbuf_tensor("x_sb", [128, DC * TOK], F32)
    xn_sb = nc.alloc_sbuf_tensor("xn_sb", [128, DC * TOK], BF16)
    misc = nc.alloc_sbuf_tensor("misc", [128, 8], F32)
    onesrow = nc.alloc_sbuf_tensor("onesrow", [1, 128], F32)
    ident = nc.alloc_sbuf_tensor("ident_sb", [128, 128], BF16)
    iota16 = nc.alloc_sbuf_tensor("iota16_sb", [128, 16], F32)
    wg_sb = nc.alloc_sbuf_tensor("wg_sb", [128, DC * 16], BF16)
    pe_sb = nc.alloc_sbuf_tensor("pe_sb", [128, 32], F32)

    psA = nc.alloc_psum_tensor("psA", [128, 1024], F32)
    psB = nc.alloc_psum_tensor("psB", [128, 1024], F32)
    psC = nc.alloc_psum_tensor("psC", [128, 1024], F32)
    pst = nc.alloc_psum_tensor("pst", [128, 512], BF16)
    psr = psA[0:1, :]

    x_t = lambda c: x_sb[:, c * TOK:(c + 1) * TOK]
    xn_t = lambda c: xn_sb[:, c * TOK:(c + 1) * TOK]
    psums = [psA[:, 0:512], psA[:, 512:1024], psC[:, 0:512], psC[:, 512:1024]]
    psum_free = {}

    @block.sync
    def _(sync):
        ld.inc(sync.dma_start(out=misc[:, 0:2], in_=t["consts"]), 16)
        ld.inc(sync.dma_start(out=ident[:, :], in_=t["identbf"]), 16)
        ld.inc(sync.dma_start(out=iota16[:, :], in_=t["iota16"]), 16)
        for c in range(DC):
            ld.inc(sync.dma_start(out=x_t(c), in_=t["xT"][c * 128:(c + 1) * 128, :]), 16)

    @block.gpsimd
    def _(gpsimd):
        wl.inc(gpsimd.dma_start(out=wg_sb.rearrange("p (c n) -> p c n", n=16), in_=t["wg"].transpose([1, 0, 2])), 16)

    @block.scalar
    def _(scalar):
        ld.wait(scalar)
        sc.inc(scalar.copy(out=onesrow[0:1, :], in_=misc[0:1, 0:1].broadcast_to([1, 128])))

    with (
        nc.sbuf_tensor("cstg", [128, 2 * (TOK + 2 * TOK)], F32) as cstg,
        nc.sbuf_tensor("s0bf", [128, 2 * TOK], BF16) as s0bf,
    ):
        ld_after = {}

        ldt = [T["ld"], T["ld2"]]

        @block.sync
        def _(sync):
            ldt[0].wait(sync)
            ldt[1].wait(sync)
            for c in range(2):
                ldt[c % 2].inc(sync.dma_start(out=s0bf[:, (c % 2) * TOK:((c % 2) + 1) * TOK],
                                              in_=t["s0T"][c * 128:(c + 1) * 128, :]), 16)
                ldt[c % 2].inc(sync.dma_start(out=cstg[:, (c % 2) * 3 * TOK:(c % 2) * 3 * TOK + TOK],
                                              in_=t["g0T"][c * 128:(c + 1) * 128, :]), 16)
                ld_after[c] = ldt[c % 2].n

        ve_after = {}
        for c in range(DC):
            par = c % 2
            g0v = cstg[:, par * 3 * TOK: par * 3 * TOK + TOK]
            s0f = cstg[:, par * 3 * TOK + TOK: par * 3 * TOK + 2 * TOK]
            tmpv = cstg[:, par * 3 * TOK + 2 * TOK: par * 3 * TOK + 3 * TOK]

            @block.scalar
            def _(scalar, c=c, par=par, s0f=s0f):
                ldt[c % 2].wait(scalar, ld_after[c])
                sc.inc(scalar.copy(out=s0f, in_=s0bf[:, par * TOK:(par + 1) * TOK]))

            @block.vector
            def _(vector, c=c, g0v=g0v, s0f=s0f, tmpv=tmpv):
                sc.wait(vector)
                vector.tensor_tensor(tmpv, s0f, g0v, ALU.mult)
                ve.inc(vector.tensor_tensor(x_t(c), x_t(c), tmpv, ALU.add))
                ve_after[c] = ve.n

            nxt = c + 2
            if nxt < DC:
                @block.sync
                def _(sync, nxt=nxt, par=par, c=c):
                    ve.wait(sync, ve_after[c])
                    ldt[par].inc(sync.dma_start(out=s0bf[:, par * TOK:(par + 1) * TOK],
                                                in_=t["s0T"][nxt * 128:(nxt + 1) * 128, :]), 16)
                    ldt[par].inc(sync.dma_start(out=cstg[:, par * 3 * TOK: par * 3 * TOK + TOK],
                                                in_=t["g0T"][nxt * 128:(nxt + 1) * 128, :]), 16)
                    ld_after[nxt] = ldt[par].n

        @block.gpsimd
        def _(gpsimd):
            ve.wait(gpsimd)
            for c in range(DC):
                st.inc(gpsimd.dma_start(out=t["x1T"][c * 128:(c + 1) * 128, :], in_=x_t(c)), 16)

    with (
        nc.sbuf_tensor("sq_sb", [128, DC * 512], F32) as sq_sb,
        nc.sbuf_tensor("rowbuf", [1, 6 * 512], F32) as rowbuf,
        nc.sbuf_tensor("bcast", [128, 1024], F32) as bcast,
    ):
        emit_ln(nc, block, T, x_t, xn_t, misc, onesrow, rowbuf, sq_sb, bcast,
                psr, psB, pre_wait=(ve,))

    pslog = psB
    log_sb = nc.alloc_sbuf_tensor("log_sb", [128, 2 * 16], F32)
    rt_sb = nc.alloc_sbuf_tensor("rt_sb", [128, 8], F32)
    msk_sb = nc.alloc_sbuf_tensor("msk_sb", [128, 16], U8)
    sel_sb = nc.alloc_sbuf_tensor("sel_sb", [128, 16], F32)
    big_sb = nc.alloc_sbuf_tensor("big_sb", [128, 16], F32)

    @block.vector
    def _(vector):
        ve.inc(vector.memset(big_sb[:, :], 1e9))

    lg_tn = {}
    lg_done = {}
    se_prev = [0]
    for j in range(16):
        slot = j % 2

        @block.tensor
        def _(tensor, j=j, slot=slot):
            if j == 0:
                ve.wait(tensor)
                wl.wait(tensor)
            if j >= 2:
                ve.wait(tensor, lg_done[j - 2])
            for c in range(DC):
                mm = tensor.matmul(pslog[:, slot * 512: slot * 512 + 16],
                                   xn_t(c)[:, j * 128:(j + 1) * 128],
                                   wg_sb[:, c * 16:(c + 1) * 16],
                                   start=(c == 0), stop=(c == DC - 1))
            tn.inc(mm)
            lg_tn[j] = tn.n

        @block.vector
        def _(vector, j=j, slot=slot):
            tn.wait(vector, lg_tn[j])
            if j > 0:
                sc.wait(vector, se_prev[0])
            lg = log_sb[:, slot * 16:(slot + 1) * 16]
            ve.inc(vector.tensor_copy(lg, pslog[:, slot * 512: slot * 512 + 16]))
            ve.wait(vector)
            ve.inc(vector.reduce_max(rt_sb[:, 0:1], lg, axis=AX.X))
            ve.wait(vector)
            ve.inc(vector.tensor_scalar_mul(rt_sb[:, 1:2], rt_sb[:, 0:1], -1.0))
        vm = ve.n

        @block.scalar
        def _(scalar, j=j, slot=slot):
            ve.wait(scalar, vm)
            lg = log_sb[:, slot * 16:(slot + 1) * 16]
            sc.inc(scalar.activation(out=sel_sb[:, :], in_=lg,
                                     func=AF.Exp, bias=rt_sb[:, 1:2]))
        se = sc.n
        se_prev[0] = se

        @block.vector
        def _(vector, j=j, slot=slot):
            sc.wait(vector, se)
            lg = log_sb[:, slot * 16:(slot + 1) * 16]
            ve.inc(vector.reduce_sum(rt_sb[:, 2:3], sel_sb[:, :], axis=AX.X))
            ve.wait(vector)
            vector.reciprocal(pe_sb[:, j:j + 1], rt_sb[:, 2:3])
            ve.inc(vector.tensor_scalar(out=msk_sb[:, :], in0=lg, scalar1=rt_sb[:, 0:1],
                                        scalar2=None, op0=ALU.is_ge))
            ve.wait(vector)
            ve.inc(vector.select(sel_sb[:, :], msk_sb[:, :], iota16[:, :], big_sb[:, :]))
            ve.wait(vector)
            ve.inc(vector.tensor_reduce(pe_sb[:, 16 + j:17 + j], sel_sb[:, :],
                                        axis=AX.X, op=ALU.min))
        lg_done[j] = ve.n

    @block.gpsimd
    def _(gpsimd):
        ve.wait(gpsimd)
        st.inc(gpsimd.dma_start(out=t["pe_o"], in_=pe_sb[:, :]), 16)

    with (
        nc.sbuf_tensor("qkstg", [128, 4 * TOK], BF16) as qkstg,
        nc.sbuf_tensor("v1T_sb", [128, DC * TOK], BF16) as v1T_sb,
        nc.sbuf_tensor("wqkv_buf", [128, 2 * DC * 128], BF16) as wqkv_buf,
        nc.sbuf_tensor("tstg", [128, 2 * D], BF16) as tstg,
    ):
        st_after = {}

        def evac(eng, oc, tt, ps):
            if oc < 16:
                par = oc % 2
                if oc >= 2:
                    st.wait(eng, st_after.get(oc - 2, 0))
                return eng.tensor_copy(
                    qkstg[:, (par * QT + tt) * 512:(par * QT + tt) * 512 + 512], ps)
            return eng.tensor_copy(
                v1T_sb[:, (oc - 16) * TOK + tt * 512:(oc - 16) * TOK + (tt + 1) * 512], ps)

        def flush(gpsimd, oc):
            if oc >= 16:
                return
            par = oc % 2
            dst = t["q1T"] if oc < 8 else t["k1T"]
            row = (oc % 8) * 128
            st.inc(gpsimd.dma_start(out=dst[row:row + 128, :],
                                    in_=qkstg[:, par * QT * 512:(par * QT + QT) * 512]), 16)
            st_after[oc] = st.n

        emit_proj(nc, block, T, t["wqkv1"], 24, DC, xn_t, wqkv_buf, psums, psum_free,
                  evac, "vector", ve, flush_fn=flush, extra_mm_wait=(ve,))

        tp_tn = {}
        tp_sc = {}
        u = 0
        for which, src_t, outdram in (("v", lambda c: v1T_sb[:, c * TOK:(c + 1) * TOK], t["vv1"]),
                                      ("xn", xn_t, t["xn1tok"])):
            for s_i in range(16):
                for oc in range(DC):
                    @block.tensor
                    def _(tensor, u=u, s_i=s_i, oc=oc, src_t=src_t):
                        if u == 0:
                            ve.wait(tensor)
                        if u >= 2:
                            sc.wait(tensor, tp_sc[u - 2])
                        mm = tensor.transpose(
                            pst[:, (u % 2) * 128:(u % 2) * 128 + 128],
                            src_t(oc)[:, s_i * 128:(s_i + 1) * 128], ident[:, :])
                        tn.inc(mm)
                        tp_tn[u] = tn.n

                    @block.scalar
                    def _(scalar, u=u, s_i=s_i, oc=oc):
                        tn.wait(scalar, tp_tn[u])
                        par = (s_i % 2)
                        sc.inc(scalar.copy(
                            out=tstg[:, par * D + oc * 128: par * D + (oc + 1) * 128],
                            in_=pst[:, (u % 2) * 128:(u % 2) * 128 + 128]))
                        tp_sc[u] = sc.n
                    u += 1

                @block.gpsimd
                def _(gpsimd, s_i=s_i, outdram=outdram, u=u):
                    sc.wait(gpsimd, tp_sc[u - 1])
                    st.inc(gpsimd.dma_start(
                        out=outdram[s_i * 128:(s_i + 1) * 128, :],
                        in_=tstg[:, (s_i % 2) * D:((s_i % 2) + 1) * D]), 16)

                @block.scalar
                def _(scalar):
                    st.wait(scalar)
                    sc.inc(scalar.copy(out=misc[:, 4:5], in_=misc[:, 0:1]))
                tp_sc[u - 1] = sc.n

    @block.gpsimd
    def _(gpsimd):
        st.wait(gpsimd)


def build_stageC():
    nc = bass.Bass("TRN2", target_bir_lowering=False, debug=False)
    t = {}
    t["xT"] = nc.dram_tensor("xT", [D, TOK], F32, kind="ExternalInput").ap()
    t["s0T"] = nc.dram_tensor("s0T", [D, TOK], BF16, kind="ExternalInput").ap()
    t["g0T"] = nc.dram_tensor("g0T", [D, TOK], F32, kind="ExternalInput").ap()
    t["wqkv1"] = nc.dram_tensor("wqkv1", [24, 128, DC, 128], F32, kind="ExternalInput").ap()
    t["wg"] = nc.dram_tensor("wg", [DC, 128, 16], F32, kind="ExternalInput").ap()
    t["identbf"] = nc.dram_tensor("identbf", [128, 128], BF16, kind="ExternalInput").ap()
    t["iota16"] = nc.dram_tensor("iota16", [128, 16], F32, kind="ExternalInput").ap()
    t["consts"] = nc.dram_tensor("consts", [128, 2], F32, kind="ExternalInput").ap()
    t["x1T"] = nc.dram_tensor("x1T", [D, TOK], F32, kind="ExternalOutput").ap()
    t["q1T"] = nc.dram_tensor("q1T", [D, TOK], BF16, kind="ExternalOutput").ap()
    t["k1T"] = nc.dram_tensor("k1T", [D, TOK], BF16, kind="ExternalOutput").ap()
    t["vv1"] = nc.dram_tensor("vv1", [TOK, D], BF16, kind="ExternalOutput").ap()
    t["xn1tok"] = nc.dram_tensor("xn1tok", [TOK, D], BF16, kind="ExternalOutput").ap()
    t["pe_o"] = nc.dram_tensor("pe_o", [128, 32], F32, kind="ExternalOutput").ap()
    with nc.Block() as block:
        T = make_tracks(nc)
        emit_stageC(nc, t, block, T)
    return nc


def host_inputs_stageC(d, core, s0_full, g0_full, x0_full):
    import ml_dtypes
    sl = slice(core * TOK, (core + 1) * TOK)
    iota = np.broadcast_to(np.arange(16, dtype=np.float32), (128, 16)).copy()
    return {
        "xT": np.ascontiguousarray(x0_full[sl].T).astype(np.float32),
        "s0T": np.ascontiguousarray(s0_full[sl].T.astype(ml_dtypes.bfloat16)),
        "g0T": np.ascontiguousarray(g0_full[sl].T).astype(np.float32),
        "wqkv1": blk_lhsT(np.asarray(d["L1_attn_w_qkv"]).T),
        "wg": np.ascontiguousarray(np.asarray(d["L1_ffn_Wg"]).reshape(DC, 128, 16)),
        "identbf": np.eye(128, dtype=np.float32).astype(ml_dtypes.bfloat16),
        "iota16": iota,
        "consts": np.stack([np.ones(128, np.float32),
                            np.full(128, 1 / 1024, np.float32)], 1),
    }







D1_NQT = 16
D1_SCALE = 0.125
LSEQ = 8192


def emit_stageD1(nc, t, block, T):
    ld, tn, sc, ve, st = (T[k] for k in ["ld", "tn", "sc", "ve", "st"])

    qh_sb = nc.alloc_sbuf_tensor("qh1_sb", [128, 2 * LSEQ], BF16)
    kh_sb = nc.alloc_sbuf_tensor("kh1_sb", [128, 2 * LSEQ], BF16)
    vaug_h = nc.alloc_sbuf_tensor("vaug1", [128, 64 * 512], BF16)
    e_sb = nc.alloc_sbuf_tensor("e1_sb", [128, 16 * 512], BF16)
    s1_sb = nc.alloc_sbuf_tensor("s1_sb", [128, 2 * LSEQ], BF16)
    cm_sb = nc.alloc_sbuf_tensor("cm_sb", [128, 4 * 512], BF16)
    onesrow = nc.alloc_sbuf_tensor("onesrowD1", [1, 64], F32)
    misc = nc.alloc_sbuf_tensor("miscD1", [128, 2], F32)
    rows = nc.alloc_sbuf_tensor("rowsD1", [1, 4 * 512], F32)
    rb_sb = nc.alloc_sbuf_tensor("rb1_sb", [64, 2 * 512], F32)

    psS = nc.alloc_psum_tensor("psS1", [128, 4 * 512], F32)
    psO = nc.alloc_psum_tensor("psO1", [128, 1024], F32)
    psb2 = nc.alloc_psum_tensor("psb21", [64, 1024], F32)

    @block.sync
    def _(sync):
        for i in range(2):
            ld.inc(sync.dma_start(out=qh_sb[:, i * LSEQ:(i + 1) * LSEQ], in_=t["q1h"][i]), 16)
            ld.inc(sync.dma_start(out=kh_sb[:, i * LSEQ:(i + 1) * LSEQ], in_=t["k1h"][i]), 16)
        ld.inc(sync.dma_start(out=cm_sb[:, :], in_=t["cmasks"]), 16)
        ld.inc(sync.dma_start(out=misc[:, :], in_=t["consts"]), 16)
        for pc in range(64):
            ld.inc(sync.dma_start(out=vaug_h[:, pc * 512:(pc + 1) * 512],
                                  in_=t["vh1"][pc * 128:(pc + 1) * 128, :]), 16)

    @block.scalar
    def _(scalar):
        ld.wait(scalar)
        sc.inc(scalar.copy(out=onesrow[0:1, :], in_=misc[0:1, 0:1].broadcast_to([1, 64])))

    vaug_lhsT = lambda pc, h: vaug_h[:, pc * 512 + 128 * h: pc * 512 + 128 * h + 128]

    GRP = 8
    stn = {}
    mask_done = {}
    av_done = {}
    po_free = {}
    rb_free = {}
    psb_free = {}
    eplan = lambda g: g // 2 + 2

    uq = 0
    gbase = 0
    for h in range(4):
        hp, hr = divmod(h, 2)
        qoff = hp * LSEQ
        hrow = hr * 64
        kt = lambda pc, hrow=hrow, qoff=qoff: kh_sb[hrow:hrow + 64, qoff + pc * 128: qoff + (pc + 1) * 128]
        qt_ = lambda qt, hrow=hrow, qoff=qoff: qh_sb[hrow:hrow + 64, qoff + qt * 512: qoff + (qt + 1) * 512]

        for qt in range(D1_NQT):
            npc = 4 * qt + 4
            oslot = uq % 2
            ngr = (npc + GRP - 1) // GRP

            for j in range(ngr + 1):
                s_list = list(range(j * GRP, min((j + 1) * GRP, npc))) if j < ngr else []
                a_list = list(range((j - 1) * GRP, min(j * GRP, npc))) if j >= 1 else []

                @block.tensor
                def _(tensor, h=h, qt=qt, oslot=oslot, uq=uq, kt=kt, qt_=qt_,
                      gbase=gbase, npc=npc, s_list=s_list, a_list=a_list):
                    for i in range(max(len(s_list), len(a_list))):
                        if i < len(s_list):
                            pc = s_list[i]
                            g = gbase + pc
                            if uq == 0 and pc == 0:
                                ld.wait(tensor)
                                sc.wait(tensor)
                            if g >= 4:
                                sc.wait(tensor, g // 2)
                            tn.inc(tensor.matmul(
                                psS[:, (g % 4) * 512:(g % 4 + 1) * 512],
                                kt(pc), qt_(qt), start=True, stop=True))
                            stn[g] = tn.n
                        if i < len(a_list):
                            pa = a_list[i]
                            ga = gbase + pa
                            if ga in mask_done:
                                ve.wait(tensor, mask_done[ga])
                            else:
                                sc.wait(tensor, eplan(ga))
                            if pa == 0 and oslot in po_free:
                                ve.wait(tensor, po_free[oslot])
                            tn.inc(tensor.matmul(
                                psO[:, oslot * 512:(oslot + 1) * 512],
                                vaug_lhsT(pa, h),
                                e_sb[:, (ga % 16) * 512:(ga % 16 + 1) * 512],
                                start=(pa == 0), stop=(pa == npc - 1)))
                            av_done[ga] = tn.n

                if s_list:
                    @block.scalar
                    def _(scalar, s_list=s_list, gbase=gbase):
                        for k in range(0, len(s_list), 2):
                            g0 = gbase + s_list[k]
                            tn.wait(scalar, stn[g0 + 1])
                            sc.inc(scalar.activation(
                                out=e_sb[:, (g0 % 16) * 512:(g0 % 16) * 512 + 1024],
                                in_=psS[:, (g0 % 4) * 512:(g0 % 4) * 512 + 1024],
                                func=AF.Exp, scale=D1_SCALE))
                            assert sc.n == eplan(g0), (sc.n, g0)

                    masks_here = [pc for pc in s_list if 0 <= pc - 4 * qt < 4]
                    if masks_here:
                        @block.vector
                        def _(vector, masks_here=masks_here, gbase=gbase, qt=qt):
                            for pc in masks_here:
                                g = gbase + pc
                                jj = pc - 4 * qt
                                sc.wait(vector, eplan(g))
                                esl = e_sb[:, (g % 16) * 512:(g % 16 + 1) * 512]
                                ve.inc(vector.tensor_tensor(
                                    esl, esl, cm_sb[:, jj * 512:(jj + 1) * 512], ALU.mult))
                                mask_done[g] = ve.n

            dr2 = rows[0:1, oslot * 1024 + 512: oslot * 1024 + 1024]
            rb = rb_sb[:, oslot * 512:(oslot + 1) * 512]

            @block.vector
            def _(vector, qt=qt, oslot=oslot, npc=npc, gbase=gbase):
                tn.wait(vector, av_done[gbase + npc - 1])
                if oslot in rb_free:
                    ve.wait(vector, rb_free[oslot])
                ve.inc(vector.reciprocal(dr2, psO[64:65, oslot * 512:(oslot + 1) * 512]))
            recip_done = ve.n

            @block.tensor
            def _(tensor, oslot=oslot):
                ve.wait(tensor, recip_done)
                if oslot in psb_free:
                    ve.wait(tensor, psb_free[oslot])
                tn.inc(tensor.matmul(psb2[:, oslot * 512:(oslot + 1) * 512],
                                     onesrow[0:1, :], dr2, start=True, stop=True))
            bcast_done = tn.n

            @block.vector
            def _(vector, qt=qt, oslot=oslot, hp=hp, hrow=hrow, rb=rb):
                tn.wait(vector, bcast_done)
                ve.inc(vector.tensor_copy(rb, psb2[:, oslot * 512:(oslot + 1) * 512]))
                psb_free[oslot] = ve.n
                ve.wait(vector)
                out_ap = s1_sb[hrow:hrow + 64, hp * LSEQ + qt * 512: hp * LSEQ + (qt + 1) * 512]
                ve.inc(vector.tensor_tensor(
                    out_ap, psO[0:64, oslot * 512:(oslot + 1) * 512], rb, ALU.mult))
                po_free[oslot] = ve.n
                rb_free[oslot] = ve.n
            uq += 1
            gbase += npc

    @block.gpsimd
    def _(gpsimd):
        ve.wait(gpsimd)
        for i in range(2):
            st.inc(gpsimd.dma_start(out=t["s1h"][i], in_=s1_sb[:, i * LSEQ:(i + 1) * LSEQ]), 16)
        st.wait(gpsimd)


def build_stageD1():
    nc = bass.Bass("TRN2", target_bir_lowering=False, debug=False)
    t = {}
    t["q1h"] = nc.dram_tensor("q1h", [2, 128, LSEQ], BF16, kind="ExternalInput").ap()
    t["k1h"] = nc.dram_tensor("k1h", [2, 128, LSEQ], BF16, kind="ExternalInput").ap()
    t["vh1"] = nc.dram_tensor("vh1", [LSEQ, 512], BF16, kind="ExternalInput").ap()
    t["cmasks"] = nc.dram_tensor("cmasks", [128, 4 * 512], BF16, kind="ExternalInput").ap()
    t["consts"] = nc.dram_tensor("consts", [128, 2], F32, kind="ExternalInput").ap()
    t["s1h"] = nc.dram_tensor("s1h", [2, 128, LSEQ], BF16, kind="ExternalOutput").ap()
    with nc.Block() as block:
        T = make_tracks(nc)
        emit_stageD1(nc, t, block, T)
    return nc


def host_consts_D1():
    import ml_dtypes
    cm = np.zeros((128, 4 * 512), np.float32)
    pl = np.arange(128)[:, None]
    ql = np.arange(512)[None, :]
    for j in range(4):
        cm[:, j * 512:(j + 1) * 512] = (128 * j + pl <= ql).astype(np.float32)
    return cm.astype(ml_dtypes.bfloat16)


def host_inputs_stageD1(core, qT_full, kT_full, vv_full):
    import ml_dtypes
    b = core // 4
    h0 = 4 * (core % 4)
    qh = qT_full[b][h0 * 64:(h0 + 4) * 64].reshape(2, 128, LSEQ).astype(ml_dtypes.bfloat16)
    kh = kT_full[b][h0 * 64:(h0 + 4) * 64].reshape(2, 128, LSEQ).astype(ml_dtypes.bfloat16)
    vh = np.ones((LSEQ, 512), dtype=ml_dtypes.bfloat16)
    for hh in range(4):
        vh[:, 128 * hh:128 * hh + 64] = vv_full[b][:, (h0 + hh) * 64:(h0 + hh + 1) * 64]
    return {"q1h": qh, "k1h": kh, "vh1": vh, "cmasks": host_consts_D1(),
            "consts": np.stack([np.ones(128, np.float32),
                                np.full(128, 1 / 1024, np.float32)], 1)}







CAP = 1280
NEXP = 2


def emit_stageD2(nc, t, block, T, upto=3):
    ld, wl, tn, sc, ve, st = (T[k] for k in ["ld", "wl", "tn", "sc", "ve", "st"])

    pe_sb = nc.alloc_sbuf_tensor("pe_sb", [128, 256], F32)
    expid = nc.alloc_sbuf_tensor("expid_sb", [128, 2], F32)
    iotat = nc.alloc_sbuf_tensor("iotat", [128, 128], F32)
    iotaf = nc.alloc_sbuf_tensor("iotaf_sb", [128, 128], F32)
    iotau = nc.alloc_sbuf_tensor("iotau", [128, 128 * 8], U32)
    sl_sb = nc.alloc_sbuf_tensor("sl_sb", [128, 128], F32)
    misc = nc.alloc_sbuf_tensor("miscD", [128, 4], F32)
    onesrow = nc.alloc_sbuf_tensor("onesrowD", [1, 128], F32)
    ident = nc.alloc_sbuf_tensor("identD", [128, 128], BF16)
    ones128 = nc.alloc_sbuf_tensor("ones128", [128, 128], F32)

    ind = nc.alloc_sbuf_tensor("ind", [128, 128], F32)
    pm = nc.alloc_sbuf_tensor("pm", [128, 128], F32)
    ge = nc.alloc_sbuf_tensor("ge", [128, 128], F32)
    sc128 = nc.alloc_sbuf_tensor("sc128", [128, 16], F32)
    scu8 = nc.alloc_sbuf_tensor("scu8", [128, 1], U8)
    tie = nc.alloc_sbuf_tensor("tie", [128, 128], F32)
    kept = nc.alloc_sbuf_tensor("kept", [128, 128], F32)
    slot = nc.alloc_sbuf_tensor("slot", [128, 128], F32)
    scan = nc.alloc_sbuf_tensor("scan", [128, 128], F32)
    zero128 = nc.alloc_sbuf_tensor("zero128", [128, 128], F32)
    offu = nc.alloc_sbuf_tensor("offu", [128, 128], U32)
    gidx_sb = nc.alloc_sbuf_tensor("gidx_sb", [128, 80], U32)
    initu = nc.alloc_sbuf_tensor("initu", [128, 80], U32)

    psq = nc.alloc_psum_tensor("psq", [128, 1024], F32)
    psw = nc.alloc_psum_tensor("psw", [128, 1024], F32)
    pst = nc.alloc_psum_tensor("pstD", [128, 512], BF16)

    gidx_dram = nc.dram_tensor("gidx_dram", [CAP + 128, 8], U32).ap()

    LO, HI, MID, CNT, SEL, MM, OFFS, T1 = range(8)
    col = lambda i: sc128[:, i:i + 1]

    @block.sync
    def _(sync):
        ld.inc(sync.dma_start(out=pe_sb[:, :], in_=t["pe_all"]), 16)
        ld.inc(sync.dma_start(out=expid[:, :], in_=t["expid"]), 16)
        ld.inc(sync.dma_start(out=iotat[:, :], in_=t["iotatok"]), 16)
        ld.inc(sync.dma_start(out=iotaf[:, :], in_=t["iotaf"]), 16)
        ld.inc(sync.dma_start(out=sl_sb[:, :], in_=t["slmat"]), 16)
        ld.inc(sync.dma_start(out=misc[:, 0:2], in_=t["consts"]), 16)
        ld.inc(sync.dma_start(out=ident[:, :], in_=t["identbf"]), 16)

    @block.scalar
    def _(scalar):
        ld.wait(scalar)
        sc.inc(scalar.copy(out=onesrow[0:1, :], in_=misc[0:1, 0:1].broadcast_to([1, 128])))

    @block.vector
    def _(vector):
        ld.wait(vector)
        vector.memset(zero128[:, :], 0.0)
        vector.memset(initu[:, :], 16383)
        vector.memset(ones128[:, :], 1.0)
        ve.inc(vector.tensor_copy(
            iotau[:, :].rearrange("p (f k) -> p f k", k=8),
            iotat[:, :, None].broadcast_to([128, 128, 8])))

    def vsync(vector, inst):
        ve.inc(inst)
        ve.wait(vector)

    p_all = pe_sb[:, 0:128]
    e_all = pe_sb[:, 128:256]

    for e_i in range(NEXP):
        @block.vector
        def _(vector, e_i=e_i):
            if e_i == 0:
                sc.wait(vector)
            else:
                st.wait(vector)
            vsync(vector, vector.tensor_scalar(out=ind[:, :], in0=e_all,
                                               scalar1=expid[:, e_i:e_i + 1],
                                               scalar2=None, op0=ALU.is_equal))
            vsync(vector, vector.tensor_tensor(pm[:, :], p_all, ind[:, :], ALU.mult))
            vsync(vector, vector.memset(col(LO), 0.0))
            vsync(vector, vector.memset(col(HI), 1.0))

        for it in range(40):
            @block.vector
            def _(vector, it=it):
                vsync(vector, vector.tensor_tensor(col(MID), col(LO), col(HI), ALU.add))
                vsync(vector, vector.tensor_scalar_mul(col(MID), col(MID), 0.5))
                vsync(vector, vector.tensor_scalar(out=ge[:, :], in0=pm[:, :],
                                                   scalar1=col(MID), scalar2=None,
                                                   op0=ALU.is_gt))
                vsync(vector, vector.reduce_sum(scan[:, 0:1], ge[:, :], axis=AX.X))
            rdy = ve.n

            @block.tensor
            def _(tensor, it=it):
                ve.wait(tensor, rdy)
                tn.inc(tensor.matmul(psq[:, 512:513], ones128[:, :], scan[:, 0:1],
                                     start=True, stop=True))
            bc = tn.n

            @block.vector
            def _(vector, it=it):
                tn.wait(vector, bc)
                vsync(vector, vector.tensor_copy(col(CNT), psq[:, 512:513]))
                vsync(vector, vector.tensor_scalar(out=col(SEL), in0=col(CNT),
                                                   scalar1=float(CAP), scalar2=None,
                                                   op0=ALU.is_ge))
                vsync(vector, vector.tensor_tensor(col(T1), col(MID), col(LO), ALU.subtract))
                vsync(vector, vector.tensor_tensor(col(T1), col(T1), col(SEL), ALU.mult))
                vsync(vector, vector.tensor_tensor(col(LO), col(LO), col(T1), ALU.add))
                vsync(vector, vector.tensor_tensor(col(T1), col(HI), col(MID), ALU.subtract))
                vsync(vector, vector.tensor_tensor(col(T1), col(T1), col(SEL), ALU.mult))
                vsync(vector, vector.tensor_tensor(col(HI), col(MID), col(T1), ALU.add))

        @block.vector
        def _(vector, e_i=e_i):
            vsync(vector, vector.tensor_scalar(out=kept[:, :], in0=pm[:, :],
                                               scalar1=col(HI), scalar2=None,
                                               op0=ALU.is_gt))
            vsync(vector, vector.tensor_scalar(out=tie[:, :], in0=pm[:, :],
                                               scalar1=col(LO), scalar2=None,
                                               op0=ALU.is_gt))
            vsync(vector, vector.tensor_tensor(tie[:, :], tie[:, :], kept[:, :],
                                               ALU.subtract))
            vsync(vector, vector.reduce_sum(scan[:, 0:1], kept[:, :], axis=AX.X))
        rdy2 = ve.n

        @block.tensor
        def _(tensor, e_i=e_i):
            ve.wait(tensor, rdy2)
            tn.inc(tensor.matmul(psq[:, 512:513], ones128[:, :], scan[:, 0:1],
                                 start=True, stop=True))
        bcm = tn.n

        @block.vector
        def _(vector, e_i=e_i):
            tn.wait(vector, bcm)
            vsync(vector, vector.tensor_copy(col(MM), psq[:, 512:513]))
            vsync(vector, vector.tensor_tensor_scan(scan[:, :], tie[:, :],
                                                    zero128[:, :], 0.0,
                                                    ALU.add, ALU.add))
            vsync(vector, vector.reduce_sum(sc128[:, T1:T1 + 1], tie[:, :], axis=AX.X))
        rdy3 = ve.n

        @block.tensor
        def _(tensor, e_i=e_i):
            ve.wait(tensor, rdy3)
            tn.inc(tensor.matmul(psq[:, 512:513], sl_sb[:, :], sc128[:, T1:T1 + 1],
                                 start=True, stop=True))
        bco = tn.n

        @block.vector
        def _(vector, e_i=e_i):
            tn.wait(vector, bco)
            vsync(vector, vector.tensor_copy(col(OFFS), psq[:, 512:513]))
            vsync(vector, vector.tensor_scalar(out=scan[:, :], in0=scan[:, :],
                                               scalar1=col(OFFS), scalar2=None,
                                               op0=ALU.add))
            vsync(vector, vector.tensor_tensor(scan[:, :], scan[:, :], tie[:, :],
                                               ALU.subtract))
            vsync(vector, vector.tensor_scalar(out=scan[:, :], in0=scan[:, :],
                                               scalar1=col(MM), scalar2=None,
                                               op0=ALU.add))
            vsync(vector, vector.tensor_scalar(out=scan[:, :], in0=scan[:, :],
                                               scalar1=float(CAP), scalar2=None,
                                               op0=ALU.is_lt))
            vsync(vector, vector.tensor_tensor(scan[:, :], scan[:, :], tie[:, :],
                                               ALU.mult))
            vsync(vector, vector.tensor_tensor(kept[:, :], kept[:, :], scan[:, :],
                                               ALU.add))
            vsync(vector, vector.tensor_tensor_scan(slot[:, :], kept[:, :],
                                                    zero128[:, :], 0.0,
                                                    ALU.add, ALU.add))
            vsync(vector, vector.reduce_sum(sc128[:, T1:T1 + 1], kept[:, :], axis=AX.X))
        rdy4 = ve.n

        @block.tensor
        def _(tensor, e_i=e_i):
            ve.wait(tensor, rdy4)
            tn.inc(tensor.matmul(psq[:, 512:513], sl_sb[:, :], sc128[:, T1:T1 + 1],
                                 start=True, stop=True))
        bco2 = tn.n

        @block.vector
        def _(vector, e_i=e_i):
            tn.wait(vector, bco2)
            vsync(vector, vector.tensor_copy(col(OFFS), psq[:, 512:513]))
            vsync(vector, vector.tensor_scalar(out=slot[:, :], in0=slot[:, :],
                                               scalar1=col(OFFS), scalar2=None,
                                               op0=ALU.add))
            vsync(vector, vector.tensor_tensor(slot[:, :], slot[:, :], kept[:, :],
                                               ALU.subtract))
            vsync(vector, vector.tensor_scalar(out=scan[:, :], in0=iotaf[:, :],
                                               scalar1=float(CAP), scalar2=None,
                                               op0=ALU.add))
            vsync(vector, vector.tensor_scalar_mul(ge[:, :], kept[:, :], -1.0))
            vsync(vector, vector.tensor_scalar_add(ge[:, :], ge[:, :], 1.0))
            vsync(vector, vector.tensor_tensor(scan[:, :], scan[:, :], ge[:, :],
                                               ALU.mult))
            vsync(vector, vector.tensor_tensor(ge[:, :], slot[:, :], kept[:, :],
                                               ALU.mult))
            vsync(vector, vector.tensor_tensor(scan[:, :], scan[:, :], ge[:, :],
                                               ALU.add))
            vsync(vector, vector.tensor_copy(offu[:, :], scan[:, :]))

        @block.gpsimd
        def _(gpsimd, e_i=e_i):
            ve.wait(gpsimd)
            st.inc(gpsimd.dma_start(out=t["keptslot"][e_i, 0], in_=kept[:, :]), 16)
            st.inc(gpsimd.dma_start(out=t["keptslot"][e_i, 1], in_=slot[:, :]), 16)
            st.inc(gpsimd.dma_start(
                out=gidx_dram[0:CAP, :].rearrange("(p g) k -> p (g k)", g=10),
                in_=initu[:, :]), 16)

        if upto < 2:
            continue
        @block.gpsimd
        def _(gpsimd, e_i=e_i):
            st.wait(gpsimd)
            for f in range(128):
                inst = gpsimd.indirect_dma_start(
                    out=gidx_dram, out_offset=IndirectOffsetOnAxis(ap=offu[:, f:f + 1], axis=0),
                    in_=iotau[:, f * 8:(f + 1) * 8], in_offset=None)
                st.inc(inst, 16)

        with (
            nc.sbuf_tensor("xe_sb%d" % e_i, [128, 10 * D], BF16) as xe_sb,
            nc.sbuf_tensor("xeT_sb%d" % e_i, [128, 8 * CAP], BF16) as xeT_sb,
            nc.sbuf_tensor("hT_sb%d" % e_i, [128, 16 * CAP], BF16) as hT_sb,
            nc.sbuf_tensor("w1_sb%d" % e_i, [128, 2 * 8 * 128], BF16) as w1_sb,
            nc.sbuf_tensor("w2_sb%d" % e_i, [128, 16 * 1024], BF16) as w2_sb,
            nc.sbuf_tensor("b1_sb%d" % e_i, [128, 16], F32) as b1_sb,
            nc.sbuf_tensor("b2b_sb%d" % e_i, [128, 1024], F32) as b2b_sb,
            nc.sbuf_tensor("b2r_sb%d" % e_i, [1, 1024], F32) as b2r_sb,
            nc.sbuf_tensor("mo_sb%d" % e_i, [128, 10 * D], BF16) as mo_sb,
        ):
            @block.sync
            def _(sync, e_i=e_i):
                st.wait(sync)
                ld.wait(sync)
                ld.inc(sync.dma_start(out=gidx_sb[:, :],
                                      in_=gidx_dram[0:CAP, :].rearrange("(p g) k -> p (g k)", g=10)), 16)
                ld.inc(sync.dma_start(out=b1_sb[:, :], in_=t["b1s"][e_i]), 16)
                ld.inc(sync.dma_start(out=b2r_sb[:, :], in_=t["b2r"][e_i]), 16)

            @block.gpsimd
            def _(gpsimd, e_i=e_i):
                ld.wait(gpsimd)
                for g in range(10):
                    inst = gpsimd.indirect_dma_start(
                        out=xe_sb[:, g * D:(g + 1) * D], out_offset=None,
                        in_=t["xn1tok"],
                        in_offset=IndirectOffsetOnAxis(ap=gidx_sb[:, g * 8:g * 8 + 1], axis=0))
                    st.inc(inst, 16)
                T["gp"].inc(gpsimd.dma_start(
                    out=w2_sb.rearrange("p (c n) -> p c n", n=1024),
                    in_=t["w2blk"][e_i].transpose([1, 0, 2])), 16)

            if upto < 3:
                @block.gpsimd
                def _(gpsimd, e_i=e_i, xe_sb=xe_sb):
                    st.wait(gpsimd)
                    mo3 = t["moe_out"][e_i].rearrange("(p g) d -> p g d", g=10)
                    for g in range(10):
                        st.inc(gpsimd.dma_start(
                            out=mo3[:, g:g + 1, :].squeeze(1),
                            in_=xe_sb[:, g * D:(g + 1) * D]), 16)
                continue
            tp_tn = {}
            tp_sc = {}
            for u in range(80):
                g, c = divmod(u, 8)

                @block.tensor
                def _(tensor, u=u, g=g, c=c, xe_sb=xe_sb):
                    if u == 0:
                        st.wait(tensor)
                    if u >= 2:
                        sc.wait(tensor, tp_sc[u - 2])
                    mm = tensor.transpose(pst[:, (u % 2) * 128:(u % 2) * 128 + 128],
                                          xe_sb[:, g * D + c * 128: g * D + (c + 1) * 128],
                                          ident[:, :])
                    tn.inc(mm)
                    tp_tn[u] = tn.n

                @block.scalar
                def _(scalar, u=u, g=g, c=c, xeT_sb=xeT_sb):
                    tn.wait(scalar, tp_tn[u])
                    sc.inc(scalar.copy(
                        out=xeT_sb[:, c * CAP + g * 128: c * CAP + (g + 1) * 128],
                        in_=pst[:, (u % 2) * 128:(u % 2) * 128 + 128]))
                    tp_sc[u] = sc.n

            if upto == 25:
                @block.gpsimd
                def _(gpsimd, e_i=e_i, xeT_sb=xeT_sb):
                    sc.wait(gpsimd)
                    mo3 = t["moe_out"][e_i].rearrange("(p g) d -> p g d", g=10)
                    for g in range(10):
                        st.inc(gpsimd.dma_start(
                            out=mo3[:, g:g + 1, :].squeeze(1),
                            in_=xe_sb[:, g * D:(g + 1) * D]), 16)
                continue
            xeT_t = lambda c: xeT_sb[:, c * CAP:(c + 1) * CAP]
            hT_t = lambda j: hT_sb[:, j * CAP:(j + 1) * CAP]
            NT3 = [(0, 512), (512, 512), (1024, 256)]
            wl_after = {}

            wlt = [T["wl"], T["wl2"]]

            @block.gpsimd
            def _(gpsimd, e_i=e_i):
                sc.wait(gpsimd)
                for oc in range(2):
                    wlt[oc % 2].inc(gpsimd.dma_start(
                        out=w1_sb[:, (oc % 2) * 1024:((oc % 2) + 1) * 1024],
                        in_=t["w1blk"][e_i, oc].rearrange("p ic o -> p (ic o)")), 16)
                    wl_after[oc] = wlt[oc % 2].n

            h_tn = {}
            h_sc = {}
            for oc in range(16):
                for ti, (t0, tw) in enumerate(NT3):
                    u = oc * 3 + ti

                    @block.tensor
                    def _(tensor, oc=oc, ti=ti, t0=t0, tw=tw, u=u, xeT_t=xeT_t):
                        if ti == 0:
                            wlt[oc % 2].wait(tensor, wl_after[oc])
                        slot_ps = u % 2
                        if u >= 2:
                            sc.wait(tensor, h_sc[u - 2])
                        for ic in range(8):
                            mm = tensor.matmul(
                                psw[:, slot_ps * 512: slot_ps * 512 + tw],
                                w1_sb[:, (oc % 2) * 1024 + ic * 128:(oc % 2) * 1024 + (ic + 1) * 128],
                                xeT_t(ic)[:, t0:t0 + tw],
                                start=(ic == 0), stop=(ic == 7))
                        tn.inc(mm)
                        h_tn[u] = tn.n

                    @block.scalar
                    def _(scalar, oc=oc, ti=ti, t0=t0, tw=tw, u=u, hT_t=hT_t):
                        if u == 0:
                            ld.wait(scalar)
                        slot_ps = u % 2
                        tn.wait(scalar, h_tn[u])
                        sc.inc(scalar.activation(
                            out=hT_t(oc)[:, t0:t0 + tw],
                            in_=psw[:, slot_ps * 512: slot_ps * 512 + tw],
                            func=AF.Gelu_apprx_tanh, bias=b1_sb[:, oc:oc + 1]))
                        h_sc[u] = sc.n

                nxt = oc + 2
                if nxt < 16:
                    @block.gpsimd
                    def _(gpsimd, oc=oc, nxt=nxt, e_i=e_i):
                        tn.wait(gpsimd, h_tn[oc * 3 + 2])
                        wlt[nxt % 2].inc(gpsimd.dma_start(
                            out=w1_sb[:, (nxt % 2) * 1024:((nxt % 2) + 1) * 1024],
                            in_=t["w1blk"][e_i, nxt].rearrange("p ic o -> p (ic o)")), 16)
                    wl_after[nxt] = wlt[nxt % 2].n

            @block.tensor
            def _(tensor):
                ld.wait(tensor)
                ve.wait(tensor)
                tensor.matmul(psq[:, 0:512], onesrow[0:1, :], b2r_sb[0:1, 0:512],
                              start=True, stop=True)
                tn.inc(tensor.matmul(psq[:, 512:1024], onesrow[0:1, :], b2r_sb[0:1, 512:1024],
                                     start=True, stop=True))
            b2bc = tn.n

            @block.vector
            def _(vector):
                tn.wait(vector, b2bc)
                ve.inc(vector.tensor_copy(b2b_sb[:, :], psq[:, 0:1024]))
            b2done = ve.n

            o_tn = {}
            o_ve = {}
            for mt in range(10):
                for nn in range(2):
                    u = mt * 2 + nn

                    @block.tensor
                    def _(tensor, mt=mt, nn=nn, u=u, hT_t=hT_t):
                        if u == 0:
                            T["gp"].wait(tensor)
                            sc.wait(tensor)
                        slot_ps = u % 2
                        if u >= 2:
                            ve.wait(tensor, o_ve[u - 2])
                        for hc in range(16):
                            mm = tensor.matmul(
                                psw[:, slot_ps * 512: slot_ps * 512 + 512],
                                hT_t(hc)[:, mt * 128:(mt + 1) * 128],
                                w2_sb[:, hc * 1024 + nn * 512: hc * 1024 + (nn + 1) * 512],
                                start=(hc == 0), stop=(hc == 15))
                        tn.inc(mm)
                        o_tn[u] = tn.n

                    @block.vector
                    def _(vector, mt=mt, nn=nn, u=u):
                        if u == 0:
                            ve.wait(vector, b2done)
                        slot_ps = u % 2
                        tn.wait(vector, o_tn[u])
                        ve.inc(vector.tensor_tensor(
                            mo_sb[:, mt * D + nn * 512: mt * D + (nn + 1) * 512],
                            psw[:, slot_ps * 512: slot_ps * 512 + 512],
                            b2b_sb[:, nn * 512:(nn + 1) * 512], ALU.add))
                        o_ve[u] = ve.n

            @block.gpsimd
            def _(gpsimd, e_i=e_i):
                ve.wait(gpsimd)
                mo3 = t["moe_out"][e_i].rearrange("(p g) d -> p g d", g=10)
                for g in range(10):
                    st.inc(gpsimd.dma_start(
                        out=mo3[:, g:g + 1, :].squeeze(1),
                        in_=mo_sb[:, g * D:(g + 1) * D]), 16)

    @block.gpsimd
    def _(gpsimd):
        st.wait(gpsimd)


def build_stageD2(upto=3):
    nc = bass.Bass("TRN2", target_bir_lowering=False, debug=False)
    t = {}
    t["pe_all"] = nc.dram_tensor("pe_all", [128, 256], F32, kind="ExternalInput").ap()
    t["xn1tok"] = nc.dram_tensor("xn1tok", [16384, D], BF16, kind="ExternalInput").ap()
    t["expid"] = nc.dram_tensor("expid", [128, 2], F32, kind="ExternalInput").ap()
    t["w1blk"] = nc.dram_tensor("w1blk", [2, 16, 128, 8, 128], F32, kind="ExternalInput").ap()
    t["w2blk"] = nc.dram_tensor("w2blk", [2, 16, 128, 1024], F32, kind="ExternalInput").ap()
    t["b1s"] = nc.dram_tensor("b1s", [2, 128, 16], F32, kind="ExternalInput").ap()
    t["b2r"] = nc.dram_tensor("b2r", [2, 1, 1024], F32, kind="ExternalInput").ap()
    t["iotatok"] = nc.dram_tensor("iotatok", [128, 128], F32, kind="ExternalInput").ap()
    t["slmat"] = nc.dram_tensor("slmat", [128, 128], F32, kind="ExternalInput").ap()
    t["iotaf"] = nc.dram_tensor("iotaf", [128, 128], F32, kind="ExternalInput").ap()
    t["consts"] = nc.dram_tensor("consts", [128, 2], F32, kind="ExternalInput").ap()
    t["identbf"] = nc.dram_tensor("identbf", [128, 128], BF16, kind="ExternalInput").ap()
    t["moe_out"] = nc.dram_tensor("moe_out", [2, CAP, D], BF16, kind="ExternalOutput").ap()
    t["keptslot"] = nc.dram_tensor("keptslot", [2, 2, 128, 128], F32, kind="ExternalOutput").ap()
    with nc.Block() as block:
        T = make_tracks(nc)
        emit_stageD2(nc, t, block, T, upto=upto)
    return nc


def host_inputs_stageD2(d, core, p_vec, e_vec, xn1tok_full):
    import ml_dtypes
    n0 = 2 * core
    W1 = np.asarray(d["L1_ffn_W1"])
    W2 = np.asarray(d["L1_ffn_W2"])
    b1 = np.asarray(d["L1_ffn_b1"])
    b2 = np.asarray(d["L1_ffn_b2"])
    pe = np.concatenate([p_vec.reshape(128, 128), e_vec.reshape(128, 128).astype(np.float32)], 1)
    return {
        "pe_all": pe.astype(np.float32),
        "xn1tok": xn1tok_full.astype(ml_dtypes.bfloat16),
        "expid": np.stack([np.full(128, n0, np.float32), np.full(128, n0 + 1, np.float32)], 1),
        "w1blk": np.stack([blk_lhsT(W1[n0]), blk_lhsT(W1[n0 + 1])]),
        "w2blk": np.stack([W2[n0].reshape(16, 128, 1024), W2[n0 + 1].reshape(16, 128, 1024)]),
        "b1s": np.stack([blk_scales(b1[n0]), blk_scales(b1[n0 + 1])]),
        "b2r": np.stack([b2[n0].reshape(1, 1024), b2[n0 + 1].reshape(1, 1024)]),
        "iotatok": (np.arange(128)[:, None] * 128 + np.arange(128)[None, :]).astype(np.float32),
        "slmat": (np.arange(128)[:, None] < np.arange(128)[None, :]).astype(np.float32),
        "iotaf": np.broadcast_to(np.arange(128, dtype=np.float32), (128, 128)).copy(),
        "consts": np.stack([np.ones(128, np.float32), np.full(128, 1 / 1024, np.float32)], 1),
        "identbf": np.eye(128, dtype=np.float32).astype(ml_dtypes.bfloat16),
    }








def emit_stageE(nc, t, block, T, upto=4):
    ld, tn, sc, ve, st = (T[k] for k in ["ld", "tn", "sc", "ve", "st"])

    o1_sb = nc.alloc_sbuf_tensor("o1_sb", [128, DC * TOK], BF16)
    s1_sb = nc.alloc_sbuf_tensor("s1p_sb", [128, DC * TOK], BF16)
    g1T_sb = nc.alloc_sbuf_tensor("g1T_sb", [128, DC * TOK], BF16)
    xn_tok = nc.alloc_sbuf_tensor("xntok_sb", [128, 16 * D], BF16)
    ks_sb = nc.alloc_sbuf_tensor("ks_sb", [128, 2 * 16 * 16], F32)
    io_sb = nc.alloc_sbuf_tensor("io1280_sb", [128, 16], F32)
    ident = nc.alloc_sbuf_tensor("identE", [128, 128], BF16)
    misc = nc.alloc_sbuf_tensor("miscE", [128, 8], F32)
    offu = nc.alloc_sbuf_tensor("offuE", [128, 16], U32)
    gtok = nc.alloc_sbuf_tensor("gtok", [128, 2 * D], F32)
    gbf = nc.alloc_sbuf_tensor("gbf", [128, 2 * D], BF16)
    g1tok = nc.alloc_sbuf_tensor("g1tok", [128, 2 * D], BF16)
    wbufE = nc.alloc_sbuf_tensor("wbufE", [128, 2 * DC * 128], BF16)
    rt = nc.alloc_sbuf_tensor("rtE", [128, 8], F32)

    psA = nc.alloc_psum_tensor("psAE", [128, 1024], F32)
    psC = nc.alloc_psum_tensor("psCE", [128, 1024], F32)
    pst = nc.alloc_psum_tensor("pstE", [128, 512], BF16)

    psums = [psA[:, 0:512], psA[:, 512:1024], psC[:, 0:512], psC[:, 512:1024]]
    psum_free = {}

    o1_t = lambda c: o1_sb[:, c * TOK:(c + 1) * TOK]
    kept_my = ks_sb[:, 0:256]
    slot_my = ks_sb[:, 256:512]

    @block.sync
    def _(sync):
        for c in range(DC):
            ld.inc(sync.dma_start(out=o1_t(c), in_=t["o1T"][c * 128:(c + 1) * 128, :]), 16)
        ld.inc(sync.dma_start(out=ks_sb[:, 0:256],
                              in_=t["kept_my"].rearrange("p a e -> p (a e)")), 16)
        ld.inc(sync.dma_start(out=ks_sb[:, 256:512],
                              in_=t["slot_my"].rearrange("p a e -> p (a e)")), 16)
        ld.inc(sync.dma_start(out=io_sb[:, :], in_=t["iota1280"]), 16)
        ld.inc(sync.dma_start(out=ident[:, :], in_=t["identbf"]), 16)
        ld.inc(sync.dma_start(out=misc[:, 0:2], in_=t["consts"]), 16)
        for s in range(16):
            ld.inc(sync.dma_start(out=xn_tok[:, s * D:(s + 1) * D],
                                  in_=t["xn1tok"][s * 128:(s + 1) * 128, :]), 16)

    def evac(eng, oc, tt, ps):
        return eng.tensor_copy(s1_sb[:, oc * TOK + tt * 512: oc * TOK + (tt + 1) * 512], ps)

    emit_proj(nc, block, T, t["wproj"], DC, DC, o1_t, wbufE, psums, psum_free,
              evac, "vector", ve, extra_mm_wait=(ld,))

    if upto < 2:
        @block.gpsimd
        def _(gpsimd):
            ve.wait(gpsimd)
            for c in range(DC):
                st.inc(gpsimd.dma_start(out=t["yT"][c * 128:(c + 1) * 128, :],
                                        in_=s1_sb[:, c * TOK:(c + 1) * TOK]), 16)
            st.wait(gpsimd)
        return

    gath_st = {}
    sel_ve = {}
    silu_sc = {}
    for s in range(16):
        par = s % 2

        @block.vector
        def _(vector, s=s, par=par):
            if s == 0:
                ld.wait(vector)
            if s >= 2:
                sc.wait(vector, silu_sc[s - 2])
            km = kept_my[:, s * 16:(s + 1) * 16]
            sm = slot_my[:, s * 16:(s + 1) * 16]
            ve.inc(vector.tensor_tensor(gtok[:, par * D: par * D + 16], sm, io_sb[:, :], ALU.add))
            ve.wait(vector)
            ve.inc(vector.tensor_tensor(gtok[:, par * D: par * D + 16],
                                        gtok[:, par * D: par * D + 16], km, ALU.mult))
            ve.wait(vector)
            ve.inc(vector.reduce_sum(rt[:, 0:1], gtok[:, par * D: par * D + 16], axis=AX.X))
            ve.wait(vector)
            ve.inc(vector.reduce_sum(rt[:, 1:2], km, axis=AX.X))
            ve.wait(vector)
            ve.inc(vector.tensor_copy(offu[:, s:s + 1], rt[:, 0:1]))
            sel_ve[("off", s)] = ve.n

        gpt = [T["gp"], T["gp2"]]

        @block.gpsimd
        def _(gpsimd, s=s, par=par):
            ve.wait(gpsimd, sel_ve[("off", s)])
            gpt[par].inc(gpsimd.indirect_dma_start(
                out=gbf[:, par * D:(par + 1) * D], out_offset=None,
                in_=t["moe_all"],
                in_offset=IndirectOffsetOnAxis(ap=offu[:, s:s + 1], axis=0)), 16)
            gath_st[s] = gpt[par].n

        @block.vector
        def _(vector, s=s, par=par):
            gpt[par].wait(vector, gath_st[s])
            ve.inc(vector.tensor_tensor(gtok[:, par * D:(par + 1) * D],
                                        gbf[:, par * D:(par + 1) * D],
                                        xn_tok[:, s * D:(s + 1) * D], ALU.subtract))
            ve.wait(vector)
            ve.inc(vector.tensor_scalar(out=gtok[:, par * D:(par + 1) * D],
                                        in0=gtok[:, par * D:(par + 1) * D],
                                        scalar1=rt[:, 1:2], scalar2=None, op0=ALU.mult))
            ve.wait(vector)
            ve.inc(vector.tensor_tensor(gtok[:, par * D:(par + 1) * D],
                                        gtok[:, par * D:(par + 1) * D],
                                        xn_tok[:, s * D:(s + 1) * D], ALU.add))
            sel_ve[s] = ve.n

        @block.scalar
        def _(scalar, s=s, par=par):
            ve.wait(scalar, sel_ve[s])
            sc.inc(scalar.activation(out=g1tok[:, par * D:(par + 1) * D],
                                     in_=gtok[:, par * D:(par + 1) * D], func=AF.Silu))
            silu_sc[s] = sc.n

        import os as _os
        if upto < 3 or s >= int(_os.environ.get("TLIM", "16")):
            continue
        for c in range(DC):
            u = s * DC + c

            @block.tensor
            def _(tensor, s=s, c=c, u=u, par=par):
                sc.wait(tensor, silu_sc[s])
                if u >= 2:
                    g_trk, g_n = sel_ve[("tp", u - 2)]
                    sc.wait(tensor, g_n)
                mm = tensor.transpose(pst[:, (u % 2) * 128:(u % 2) * 128 + 128],
                                      g1tok[:, par * D + c * 128: par * D + (c + 1) * 128],
                                      ident[:, :])
                tn.inc(mm)
                sel_ve[("tptn", u)] = tn.n

            @block.scalar
            def _(scalar, s=s, c=c, u=u):
                tn.wait(scalar, sel_ve[("tptn", u)])
                sc.inc(scalar.copy(
                    out=g1T_sb[:, c * TOK + s * 128: c * TOK + (s + 1) * 128],
                    in_=pst[:, (u % 2) * 128:(u % 2) * 128 + 128]))
                sel_ve[("tp", u)] = ("sc", sc.n)

    if upto < 4:
        @block.gpsimd
        def _(gpsimd):
            ve.wait(gpsimd)
            sc.wait(gpsimd)
            for c in range(DC):
                st.inc(gpsimd.dma_start(out=t["yT"][c * 128:(c + 1) * 128, :],
                                        in_=g1T_sb[:, c * TOK:(c + 1) * TOK] if upto == 3 else s1_sb[:, c * TOK:(c + 1) * TOK]), 16)
            st.wait(gpsimd)
            T["gp"].wait(gpsimd)
            T["gp2"].wait(gpsimd)
        return

    with (
        nc.sbuf_tensor("x1stg", [128, 2 * TOK], F32) as x1stg,
        nc.sbuf_tensor("fstg", [128, 2 * 2 * TOK], F32) as fstg,
    ):
        ldt = [T["ld"], T["ld2"]]
        ld_after = {}

        @block.sync
        def _(sync):
            ldt[0].wait(sync)
            ldt[1].wait(sync)
            for c in range(2):
                ldt[c % 2].inc(sync.dma_start(out=x1stg[:, (c % 2) * TOK:((c % 2) + 1) * TOK],
                                              in_=t["x1T"][c * 128:(c + 1) * 128, :]), 16)
                ld_after[c] = ldt[c % 2].n

        fin_ve = {}
        for c in range(DC):
            par = c % 2
            sf = fstg[:, par * 2 * TOK: par * 2 * TOK + TOK]
            gf = fstg[:, par * 2 * TOK + TOK: par * 2 * TOK + 2 * TOK]

            @block.scalar
            def _(scalar, c=c, par=par, sf=sf, gf=gf):
                if c < 2:
                    ve.wait(scalar)
                    sc.wait(scalar)
                ldt[par].wait(scalar, ld_after[c])
                scalar.copy(out=sf, in_=s1_sb[:, c * TOK:(c + 1) * TOK])
                sc.inc(scalar.copy(out=gf, in_=g1T_sb[:, c * TOK:(c + 1) * TOK]))
                fin_ve[("sc", c)] = sc.n

            @block.vector
            def _(vector, c=c, par=par, sf=sf, gf=gf):
                sc.wait(vector, fin_ve[("sc", c)])
                ve.inc(vector.tensor_tensor(sf, sf, gf, ALU.mult))
                ve.wait(vector)
                ve.inc(vector.tensor_tensor(x1stg[:, par * TOK:(par + 1) * TOK],
                                            x1stg[:, par * TOK:(par + 1) * TOK], sf, ALU.add))
                fin_ve[c] = ve.n

            stt = [T["st"], T["st2"]]

            @block.gpsimd
            def _(gpsimd, c=c, par=par):
                ve.wait(gpsimd, fin_ve[c])
                stt[par].inc(gpsimd.dma_start(out=t["yT"][c * 128:(c + 1) * 128, :],
                                              in_=x1stg[:, par * TOK:(par + 1) * TOK]), 16)
                fin_ve[("st", c)] = stt[par].n

            nxt = c + 2
            if nxt < DC:
                @block.sync
                def _(sync, nxt=nxt, par=par, c=c):
                    stt[par].wait(sync, fin_ve[("st", c)])
                    ldt[par].inc(sync.dma_start(out=x1stg[:, par * TOK:(par + 1) * TOK],
                                                in_=t["x1T"][nxt * 128:(nxt + 1) * 128, :]), 16)
                    ld_after[nxt] = ldt[par].n

    @block.gpsimd
    def _(gpsimd):
        st.wait(gpsimd)
        T["st2"].wait(gpsimd)
        T["gp"].wait(gpsimd)
        T["gp2"].wait(gpsimd)


def build_stageE(upto=4):
    nc = bass.Bass("TRN2", target_bir_lowering=False, debug=False)
    t = {}
    t["x1T"] = nc.dram_tensor("x1T", [D, TOK], F32, kind="ExternalInput").ap()
    t["o1T"] = nc.dram_tensor("o1T", [D, TOK], BF16, kind="ExternalInput").ap()
    t["wproj"] = nc.dram_tensor("wproj", [DC, 128, DC, 128], F32, kind="ExternalInput").ap()
    t["moe_all"] = nc.dram_tensor("moe_all", [16 * 1280, D], BF16, kind="ExternalInput").ap()
    t["xn1tok"] = nc.dram_tensor("xn1tok", [TOK, D], BF16, kind="ExternalInput").ap()
    t["kept_my"] = nc.dram_tensor("kept_my", [128, 16, 16], F32, kind="ExternalInput").ap()
    t["slot_my"] = nc.dram_tensor("slot_my", [128, 16, 16], F32, kind="ExternalInput").ap()
    t["iota1280"] = nc.dram_tensor("iota1280", [128, 16], F32, kind="ExternalInput").ap()
    t["identbf"] = nc.dram_tensor("identbf", [128, 128], BF16, kind="ExternalInput").ap()
    t["consts"] = nc.dram_tensor("consts", [128, 2], F32, kind="ExternalInput").ap()
    t["yT"] = nc.dram_tensor("yT", [D, TOK], F32, kind="ExternalOutput").ap()
    with nc.Block() as block:
        T = make_tracks(nc)
        emit_stageE(nc, t, block, T, upto=upto)
    return nc






_PROGS = {}


def _progs():
    if not _PROGS:
        _PROGS["A"] = build_stageA()
        _PROGS["B"] = build_stageB()
        _PROGS["C"] = build_stageC()
        _PROGS["D1"] = build_stageD1()
        _PROGS["D2"] = build_stageD2()
        _PROGS["E"] = build_stageE()
    return _PROGS


def _launch(nc, in_maps):
    return run_bass_kernel_spmd(nc, in_maps, core_ids=list(range(8))).results


def kernel(**inputs):
    d = {k: np.asarray(v) for k, v in inputs.items()}
    P = _progs()
    NC = 8

    resA = _launch(P["A"], [host_inputs_stageA(d, c) for c in range(NC)])

    qT_full = np.stack([np.concatenate([resA[4 * b + i]["qT_o"] for i in range(4)], axis=1)
                        for b in range(2)])
    kT_full = np.stack([np.concatenate([resA[4 * b + i]["kT_o"] for i in range(4)], axis=1)
                        for b in range(2)])
    vv_full = np.stack([np.concatenate([resA[4 * b + i]["vv_o"] for i in range(4)], axis=0)
                        for b in range(2)])
    resB = _launch(P["B"], [host_inputs_stageB(c, qT_full, kT_full, vv_full)
                            for c in range(NC)])

    s0_full = np.empty((16384, 1024), np.float32)
    for b in range(2):
        sT = np.concatenate([resB[4 * b + i]["s0h"].reshape(256, 8192)
                             for i in range(4)], axis=0)
        s0_full[b * 8192:(b + 1) * 8192] = sT.astype(np.float32).T
    g0_full = np.empty((16384, 1024), np.float32)
    for c in range(NC):
        g0_full[c * TOK:(c + 1) * TOK] = resA[c]["g0T"].T
    x0_full = d["L0_attn_x"].reshape(16384, 1024)
    resC = _launch(P["C"], [host_inputs_stageC(d, c, s0_full, g0_full, x0_full)
                            for c in range(NC)])

    q1_full = np.stack([np.concatenate([resC[4 * b + i]["q1T"] for i in range(4)], axis=1)
                        for b in range(2)])
    k1_full = np.stack([np.concatenate([resC[4 * b + i]["k1T"] for i in range(4)], axis=1)
                        for b in range(2)])
    v1_full = np.stack([np.concatenate([resC[4 * b + i]["vv1"] for i in range(4)], axis=0)
                        for b in range(2)])
    resD1 = _launch(P["D1"], [host_inputs_stageD1(c, q1_full, k1_full, v1_full)
                              for c in range(NC)])

    p_vec = np.concatenate([resC[c]["pe_o"][:, 0:16].T.reshape(-1) for c in range(NC)])
    e_vec = np.concatenate([resC[c]["pe_o"][:, 16:32].T.reshape(-1) for c in range(NC)])
    xn1tok_full = np.concatenate([resC[c]["xn1tok"] for c in range(NC)], axis=0)
    resD2 = _launch(P["D2"], [host_inputs_stageD2(d, c, p_vec, e_vec, xn1tok_full)
                              for c in range(NC)])

    moe_all = np.concatenate([resD2[c]["moe_out"].reshape(2 * 1280, 1024)
                              for c in range(NC)], axis=0)
    kept_full = np.empty((16, 16384), np.float32)
    slot_full = np.empty((16, 16384), np.float32)
    for c in range(NC):
        for j in range(2):
            kept_full[2 * c + j] = resD2[c]["keptslot"][j, 0].reshape(-1)
            slot_full[2 * c + j] = resD2[c]["keptslot"][j, 1].reshape(-1)
    o1_fullT = [np.concatenate([resD1[4 * b + i]["s1h"].reshape(256, 8192)
                                for i in range(4)], axis=0) for b in range(2)]
    wproj_blk = blk_lhsT(d["L1_attn_w_proj"].T)
    iota1280 = np.broadcast_to(np.arange(16, dtype=np.float32) * 1280, (128, 16)).copy()
    identbf = np.eye(128, dtype=np.float32).astype(ml_dtypes.bfloat16)
    consts = np.stack([np.ones(128, np.float32), np.full(128, 1 / 1024, np.float32)], 1)
    in_mapsE = []
    for c in range(NC):
        b = c // 4
        sl = slice(c * TOK, (c + 1) * TOK)
        km = kept_full[:, sl].T.reshape(16, 128, 16).transpose(1, 0, 2)
        sm = slot_full[:, sl].T.reshape(16, 128, 16).transpose(1, 0, 2)
        in_mapsE.append({
            "x1T": resC[c]["x1T"],
            "o1T": np.ascontiguousarray(o1_fullT[b][:, (c % 4) * TOK:(c % 4 + 1) * TOK]),
            "wproj": wproj_blk,
            "moe_all": moe_all,
            "xn1tok": resC[c]["xn1tok"],
            "kept_my": np.ascontiguousarray(km),
            "slot_my": np.ascontiguousarray(sm),
            "iota1280": iota1280,
            "identbf": identbf,
            "consts": consts,
        })
    resE = _launch(P["E"], in_mapsE)

    y = np.empty((16384, 1024), np.float32)
    for c in range(NC):
        y[c * TOK:(c + 1) * TOK] = resE[c]["yT"].T
    return y.reshape(2, 8192, 1024)


# version 04
# speedup vs baseline: 1.6043x; 1.0058x over previous
import numpy as np
import ml_dtypes
import concourse.bass as bass
import concourse.mybir as mybir
from concourse.bass_utils import run_bass_kernel_spmd
from concourse.bass_core_rust import IndirectOffsetOnAxis



F32 = mybir.dt.float32
BF16 = mybir.dt.bfloat16
U32 = mybir.dt.uint32
U8 = mybir.dt.uint8
AF = mybir.ActivationFunctionType
ALU = mybir.AluOpType
AX = mybir.AxisListType

D = 1024
TOK = 2048
DC = 8
HID = 4096
HC = 32
QT = 4


class Track:
    def __init__(self, nc, name):
        self.sem = nc.alloc_semaphore(name)
        self.n = 0

    def inc(self, inst, k=1):
        inst.then_inc(self.sem, k)
        self.n += k
        return inst

    def wait(self, eng, n=None):
        eng.wait_ge(self.sem, self.n if n is None else n)


def make_tracks(nc):
    return {k: Track(nc, k) for k in ["ld", "ld2", "wl", "wl2", "tn", "sc", "ve", "st", "st2", "gp", "gp2"]}


def blk_lhsT(w_in_out):
    i, o = w_in_out.shape
    return np.ascontiguousarray(
        w_in_out.reshape(i // 128, 128, o // 128, 128).transpose(2, 1, 0, 3))


def blk_scales(s):
    return np.ascontiguousarray(s.reshape(-1, 128).T)



def emit_proj(nc, block, T, w_dram, n_oc, n_ic, rhs_tile, wbuf, psums, psum_free,
              evac_fn, evac_eng, evac_track, n_tt=QT, nbuf=2, extra_mm_wait=(),
              flush_fn=None, after_evac=None, evac_prewait=None):
    tn = T["tn"]
    wlt = [T["wl"], T["wl2"]]
    ev = evac_track
    blk_w = n_ic * 128
    wtile = lambda par, ic: wbuf[:, par * blk_w + ic * 128: par * blk_w + (ic + 1) * 128]
    tn_after = {}
    wl_after = {}
    ev_after = {}

    tn_at_entry = tn.n

    @block.gpsimd
    def _(gpsimd):
        if tn_at_entry:
            tn.wait(gpsimd, tn_at_entry)
        for oc in range(min(nbuf, n_oc)):
            wlt[oc % 2].inc(gpsimd.dma_start(
                out=wbuf[:, (oc % nbuf) * blk_w:((oc % nbuf) + 1) * blk_w],
                in_=w_dram[oc].rearrange("p ic o -> p (ic o)")), 16)
            wl_after[oc] = wlt[oc % 2].n

    for oc in range(n_oc):
        par = oc % nbuf

        @block.tensor
        def _(tensor, oc=oc, par=par):
            wlt[oc % 2].wait(tensor, wl_after[oc])
            if oc == 0:
                for t in extra_mm_wait:
                    t.wait(tensor)
            for tt in range(n_tt):
                slot = (oc * n_tt + tt) % len(psums)
                if slot in psum_free:
                    ftrk, fn = psum_free[slot]
                    ftrk.wait(tensor, fn)
                ps = psums[slot]
                for ic in range(n_ic):
                    mm = tensor.matmul(ps[:, :], wtile(par, ic),
                                       rhs_tile(ic)[:, tt * 512:(tt + 1) * 512],
                                       start=(ic == 0), stop=(ic == n_ic - 1))
                tn.inc(mm)
                tn_after[(oc, tt)] = tn.n

        eng_sel = getattr(block, evac_eng)

        @eng_sel
        def _(eng, oc=oc):
            for tt in range(n_tt):
                slot = (oc * n_tt + tt) % len(psums)
                tn.wait(eng, tn_after[(oc, tt)])
                if evac_prewait is not None:
                    evac_prewait(eng, oc, tt)
                last = evac_fn(eng, oc, tt, psums[slot])
                ev.inc(last)
                psum_free[slot] = (ev, ev.n)
                ev_after[(oc, tt)] = ev.n
                if after_evac is not None:
                    after_evac(oc, tt, ev.n)

        if flush_fn is not None:
            @block.gpsimd
            def _(gpsimd, oc=oc):
                ev.wait(gpsimd, ev_after[(oc, n_tt - 1)])
                flush_fn(gpsimd, oc)

        nxt = oc + nbuf
        if nxt < n_oc:
            @block.gpsimd
            def _(gpsimd, oc=oc, nxt=nxt):
                tn.wait(gpsimd, tn_after[(oc, n_tt - 1)])
                wlt[nxt % 2].inc(gpsimd.dma_start(
                    out=wbuf[:, (nxt % nbuf) * blk_w:((nxt % nbuf) + 1) * blk_w],
                    in_=w_dram[nxt].rearrange("p ic o -> p (ic o)")), 16)
            wl_after[nxt] = wlt[nxt % 2].n
    return tn_after, ev_after


def run_cores(nc, in_maps, **kw):
    return run_bass_kernel_spmd(nc, in_maps, core_ids=list(range(len(in_maps))), **kw)




import ml_dtypes

SC_S1, SC_S2, SC_H1, SC_H2, SC_S3 = 0, HC, 2 * HC, 3 * HC, 4 * HC


def emit_ln(nc, block, T, x_tile, xn_tile, misc, onesrow, rowbuf, sq_sb, bcast,
            psr, psB, pre_wait=(), ntok=TOK):
    tn, sc, ve = T["tn"], T["sc"], T["ve"]
    nq = ntok // 512
    psr_free = [0]
    psB_free = [0]
    for q in range(nq):
        qs = slice(q * 512, (q + 1) * 512)
        R = lambda i: rowbuf[0:1, i * 512:(i + 1) * 512]
        m_row, msq_row, var_row, sd_row, r_row, mr_row = (R(i) for i in range(6))

        @block.scalar
        def _(scalar, q=q, qs=qs):
            if q == 0:
                for t in pre_wait:
                    t.wait(scalar)
            else:
                tn.wait(scalar)
                ve.wait(scalar)
            for c in range(DC):
                inst = scalar.activation(out=sq_sb[:, c * 512:(c + 1) * 512],
                                         in_=x_tile(c)[:, qs], func=AF.Square)
            sc.inc(inst)

        @block.tensor
        def _(tensor, q=q, qs=qs):
            sc.wait(tensor)
            if psr_free[0]:
                ve.wait(tensor, psr_free[0])
            for c in range(DC):
                tensor.matmul(psr[0:1, 0:512], misc[:, 1:2], x_tile(c)[:, qs],
                              start=(c == 0), stop=(c == DC - 1))
            for c in range(DC):
                mm = tensor.matmul(psr[0:1, 512:1024], misc[:, 1:2],
                                   sq_sb[:, c * 512:(c + 1) * 512],
                                   start=(c == 0), stop=(c == DC - 1))
            tn.inc(mm)

        @block.vector
        def _(vector, q=q):
            tn.wait(vector)
            ve.inc(vector.tensor_copy(m_row, psr[0:1, 0:512]))
            ve.wait(vector)
            ve.inc(vector.tensor_tensor(msq_row, m_row, m_row, ALU.mult))
            ve.wait(vector)
            ve.inc(vector.tensor_tensor(var_row, psr[0:1, 512:1024], msq_row, ALU.subtract))
            ve.wait(vector)
            ve.inc(vector.tensor_scalar_add(var_row, var_row, 1e-5))
        psr_free[0] = ve.n

        @block.scalar
        def _(scalar, q=q):
            ve.wait(scalar)
            sc.inc(scalar.sqrt(sd_row, var_row))

        @block.vector
        def _(vector, q=q):
            sc.wait(vector)
            ve.inc(vector.reciprocal(r_row, sd_row))
            ve.wait(vector)
            ve.inc(vector.tensor_tensor(mr_row, m_row, r_row, ALU.mult))

        @block.tensor
        def _(tensor, q=q):
            ve.wait(tensor)
            if psB_free[0]:
                ve.wait(tensor, psB_free[0])
            tensor.matmul(psB[:, 0:512], onesrow[0:1, :], r_row, start=True, stop=True)
            tn.inc(tensor.matmul(psB[:, 512:1024], onesrow[0:1, :], mr_row,
                                 start=True, stop=True))

        @block.vector
        def _(vector, q=q):
            tn.wait(vector)
            vector.tensor_copy(bcast[:, 0:512], psB[:, 0:512])
            ve.inc(vector.tensor_copy(bcast[:, 512:1024], psB[:, 512:1024]))

        @block.vector
        def _(vector, q=q, qs=qs):
            for c in range(DC):
                vector.tensor_tensor(sq_sb[:, c * 512:(c + 1) * 512], x_tile(c)[:, qs],
                                     bcast[:, 0:512], ALU.mult)
                inst = vector.tensor_tensor(xn_tile(c)[:, qs],
                                            sq_sb[:, c * 512:(c + 1) * 512],
                                            bcast[:, 512:1024], ALU.subtract)
            ve.inc(inst)
        psB_free[0] = ve.n
        if q + 1 < nq:
            pass

    return


def emit_stageA(nc, t, block, T):
    ld, wl, tn, sc, ve, st, gp = (T[k] for k in ["ld", "wl", "tn", "sc", "ve", "st", "gp"])

    xn_sb = nc.alloc_sbuf_tensor("xn_sb", [128, DC * TOK], BF16)
    misc = nc.alloc_sbuf_tensor("misc", [128, 8], F32)
    scal_sb = nc.alloc_sbuf_tensor("scal_sb", [128, 4 * HC + DC], F32)
    onesrow = nc.alloc_sbuf_tensor("onesrow", [1, 128], F32)
    ident = nc.alloc_sbuf_tensor("ident", [128, 128], BF16)

    psA = nc.alloc_psum_tensor("psA", [128, 1024], F32)
    psB = nc.alloc_psum_tensor("psB", [128, 1024], F32)
    psC = nc.alloc_psum_tensor("psC", [128, 1024], F32)
    pst = nc.alloc_psum_tensor("pst", [128, 512], BF16)
    psr = psA[0:1, :]

    xn_t = lambda c: xn_sb[:, c * TOK:(c + 1) * TOK]
    psums = [psA[:, 0:512], psA[:, 512:1024], psC[:, 0:512], psC[:, 512:1024]]
    psum_free = {}

    @block.sync
    def _(sync):
        ld.inc(sync.dma_start(out=misc[:, 0:2], in_=t["consts"]), 16)
        ld.inc(sync.dma_start(out=scal_sb[:, :], in_=t["scls"]), 16)
        ld.inc(sync.dma_start(out=ident[:, :], in_=t["identbf"]), 16)

    @block.scalar
    def _(scalar):
        ld.wait(scalar)
        sc.inc(scalar.copy(out=onesrow[0:1, :], in_=misc[0:1, 0:1].broadcast_to([1, 128])))

    with (
        nc.sbuf_tensor("x_sb", [128, DC * TOK], F32) as x_sb,
        nc.sbuf_tensor("sq_sb", [128, DC * 512], F32) as sq_sb,
        nc.sbuf_tensor("rowbuf", [1, 6 * 512], F32) as rowbuf,
        nc.sbuf_tensor("bcast", [128, 1024], F32) as bcast,
    ):
        x_t = lambda c: x_sb[:, c * TOK:(c + 1) * TOK]

        @block.sync
        def _(sync):
            for c in range(DC):
                ld.inc(sync.dma_start(out=x_t(c), in_=t["xT"][c * 128:(c + 1) * 128, :]), 16)

        emit_ln(nc, block, T, x_t, xn_t, misc, onesrow, rowbuf, sq_sb, bcast,
                psr, psB, pre_wait=(ld, sc))

        @block.gpsimd
        def _(gpsimd):
            ve.wait(gpsimd)
            for c in range(DC):
                st.inc(gpsimd.dma_start(out=t["xnT_o"][c * 128:(c + 1) * 128, :], in_=xn_t(c)), 16)

    with (
        nc.sbuf_tensor("q_sb", [128, DC * TOK], BF16) as q_sb,
        nc.sbuf_tensor("k_sb", [128, DC * TOK], BF16) as k_sb,
        nc.sbuf_tensor("vT_sb", [128, DC * TOK], BF16) as vT_sb,
        nc.sbuf_tensor("vv_sb", [128, 16 * D], BF16) as vv_sb,
        nc.sbuf_tensor("wqkv", [128, 2 * DC * 128], BF16) as wqkv,
    ):
        for wname, dst in (("wq", q_sb), ("wk", k_sb), ("wv", vT_sb)):
            def evac(eng, oc, tt, ps, dst=dst):
                return eng.tensor_copy(
                    dst[:, oc * TOK + tt * 512: oc * TOK + (tt + 1) * 512], ps)

            emit_proj(nc, block, T, t[wname], DC, DC, xn_t, wqkv, psums, psum_free,
                      evac, "vector", ve, extra_mm_wait=(ve,) if wname == "wq" else ())

        @block.gpsimd
        def _(gpsimd):
            ve.wait(gpsimd)
            for c in range(DC):
                st.inc(gpsimd.dma_start(out=t["qT_o"][c * 128:(c + 1) * 128, :],
                                        in_=q_sb[:, c * TOK:(c + 1) * TOK]), 16)
                st.inc(gpsimd.dma_start(out=t["kT_o"][c * 128:(c + 1) * 128, :],
                                        in_=k_sb[:, c * TOK:(c + 1) * TOK]), 16)

        vv_t = lambda s: vv_sb[:, s * D:(s + 1) * D]
        tp_tn = {}
        tp_sc = {}
        for u in range(16 * DC):
            s_i, oc = divmod(u, DC)

            @block.tensor
            def _(tensor, u=u, s_i=s_i, oc=oc):
                if u == 0:
                    ve.wait(tensor)
                if u >= 2:
                    sc.wait(tensor, tp_sc[u - 2])
                mm = tensor.transpose(
                    pst[:, (u % 2) * 128:(u % 2) * 128 + 128],
                    vT_sb[:, oc * TOK + s_i * 128: oc * TOK + (s_i + 1) * 128],
                    ident[:, :])
                tn.inc(mm)
                tp_tn[u] = tn.n

            @block.scalar
            def _(scalar, u=u, s_i=s_i, oc=oc):
                tn.wait(scalar, tp_tn[u])
                sc.inc(scalar.copy(out=vv_t(s_i)[:, oc * 128:(oc + 1) * 128],
                                   in_=pst[:, (u % 2) * 128:(u % 2) * 128 + 128]))
                tp_sc[u] = sc.n

        psum_free[2] = (sc, sc.n)
        psum_free[3] = (sc, sc.n)

        @block.gpsimd
        def _(gpsimd):
            sc.wait(gpsimd)
            for s in range(16):
                st.inc(gpsimd.dma_start(out=t["vv_o"][s * 128:(s + 1) * 128, :], in_=vv_t(s)), 16)

    with (
        nc.sbuf_tensor("hbuf", [128, HC * TOK], BF16) as hbuf,
        nc.sbuf_tensor("wffn", [128, 2 * HC * 128], BF16) as wffn,
        nc.sbuf_tensor("stg", [128, 4 * 512], BF16) as stg,
        nc.sbuf_tensor("ostg", [128, 2 * TOK], BF16) as ostg,
        nc.sbuf_tensor("gstg", [128, 2 * TOK], F32) as gstg,
    ):
        h_t = lambda j: hbuf[:, j * TOK:(j + 1) * TOK]

        @block.gpsimd
        def _(gpsimd):
            st.wait(gpsimd)

        def evac_a(eng, oc, tt, ps):
            return eng.activation(out=h_t(oc)[:, tt * 512:(tt + 1) * 512], in_=ps,
                                  func=AF.Silu, scale=scal_sb[:, SC_S1 + oc:SC_S1 + oc + 1])

        emit_proj(nc, block, T, t["w1"], HC, DC, xn_t, wffn, psums, psum_free,
                  evac_a, "scalar", sc, extra_mm_wait=(st,))

        sc_after = {}

        def evac_b(eng, oc, tt, ps):
            u = oc * QT + tt
            if u >= 4:
                ve.wait(eng, sc_after.get(("vmul", u - 4), 0))
            return eng.activation(out=stg[:, tt * 512:(tt + 1) * 512], in_=ps,
                                  func=AF.Identity, scale=scal_sb[:, SC_S2 + oc:SC_S2 + oc + 1])

        def rec_b(oc, tt, n):
            sc_after[(oc, tt)] = n

        tnA, evA = emit_proj(nc, block, T, t["w2"], HC, DC, xn_t, wffn, psums, psum_free,
                             evac_b, "scalar", sc, after_evac=rec_b)

        for oc in range(HC):
            @block.vector
            def _(vector, oc=oc):
                for tt in range(QT):
                    sc.wait(vector, sc_after[(oc, tt)])
                    inst = vector.tensor_tensor(
                        h_t(oc)[:, tt * 512:(tt + 1) * 512],
                        h_t(oc)[:, tt * 512:(tt + 1) * 512],
                        stg[:, tt * 512:(tt + 1) * 512], ALU.mult)
                    ve.inc(inst)
                    sc_after[("vmul", oc * QT + tt)] = ve.n

        st_after = {}

        def evac_h2(eng, oc, tt, ps):
            u = oc % 2
            if oc >= 2:
                st.wait(eng, st_after.get(oc - 2, 0))
            return eng.activation(out=ostg[:, u * TOK + tt * 512:u * TOK + (tt + 1) * 512],
                                  in_=ps, func=AF.Gelu,
                                  scale=scal_sb[:, SC_H1 + oc:SC_H1 + oc + 1])

        def flush_h2(gpsimd, oc):
            u = oc % 2
            st.inc(gpsimd.dma_start(
                out=t["h2_dram"][oc * 128:(oc + 1) * 128, :],
                in_=ostg[:, u * TOK:(u + 1) * TOK]), 16)
            st_after[oc] = st.n

        emit_proj(nc, block, T, t["h1w"], HC, HC, h_t, wffn, psums, psum_free,
                  evac_h2, "scalar", sc, extra_mm_wait=(ve,), flush_fn=flush_h2)

        @block.sync
        def _(sync):
            tn.wait(sync)
            st.wait(sync)
            ld.wait(sync)
            for j in range(HC):
                ld.inc(sync.dma_start(out=h_t(j), in_=t["h2_dram"][j * 128:(j + 1) * 128, :]), 16)

        def evac_h3(eng, oc, tt, ps):
            u = oc % 2
            if oc >= 2:
                st.wait(eng, st_after.get(("h3", oc - 2), 0))
            return eng.activation(out=ostg[:, u * TOK + tt * 512:u * TOK + (tt + 1) * 512],
                                  in_=ps, func=AF.Gelu,
                                  scale=scal_sb[:, SC_H2 + oc:SC_H2 + oc + 1])

        def flush_h3(gpsimd, oc):
            u = oc % 2
            st.inc(gpsimd.dma_start(
                out=t["h3_dram"][oc * 128:(oc + 1) * 128, :],
                in_=ostg[:, u * TOK:(u + 1) * TOK]), 16)
            st_after[("h3", oc)] = st.n

        emit_proj(nc, block, T, t["h2w"], HC, HC, h_t, wffn, psums, psum_free,
                  evac_h3, "scalar", sc, extra_mm_wait=(ld,), flush_fn=flush_h3)

        @block.sync
        def _(sync):
            tn.wait(sync)
            st.wait(sync)
            ld.wait(sync)
            for j in range(HC):
                ld.inc(sync.dma_start(out=h_t(j), in_=t["h3_dram"][j * 128:(j + 1) * 128, :]), 16)

        def evac_g(eng, oc, tt, ps):
            u = oc % 2
            if oc >= 2:
                st.wait(eng, st_after.get(("g", oc - 2), 0))
            return eng.activation(out=gstg[:, u * TOK + tt * 512:u * TOK + (tt + 1) * 512],
                                  in_=ps, func=AF.Silu,
                                  scale=scal_sb[:, SC_S3 + oc:SC_S3 + oc + 1])

        def flush_g(gpsimd, oc):
            u = oc % 2
            st.inc(gpsimd.dma_start(
                out=t["g0T"][oc * 128:(oc + 1) * 128, :],
                in_=gstg[:, u * TOK:(u + 1) * TOK]), 16)
            st_after[("g", oc)] = st.n

        emit_proj(nc, block, T, t["w3"], DC, HC, h_t, wffn, psums, psum_free,
                  evac_g, "scalar", sc, extra_mm_wait=(ld,), flush_fn=flush_g)

        @block.gpsimd
        def _(gpsimd):
            st.wait(gpsimd)


def build_stageA():
    nc = bass.Bass("TRN2", target_bir_lowering=False, debug=False)
    t = {}
    t["xT"] = nc.dram_tensor("xT", [D, TOK], F32, kind="ExternalInput").ap()
    for n, shp in (("wq", [DC, 128, DC, 128]), ("wk", [DC, 128, DC, 128]),
                   ("wv", [DC, 128, DC, 128]), ("w1", [HC, 128, DC, 128]),
                   ("w2", [HC, 128, DC, 128]), ("h1w", [HC, 128, HC, 128]),
                   ("h2w", [HC, 128, HC, 128]), ("w3", [DC, 128, HC, 128])):
        t[n] = nc.dram_tensor(n, shp, F32, kind="ExternalInput").ap()
    t["scls"] = nc.dram_tensor("scls", [128, 4 * HC + DC], F32, kind="ExternalInput").ap()
    t["consts"] = nc.dram_tensor("consts", [128, 2], F32, kind="ExternalInput").ap()
    t["identbf"] = nc.dram_tensor("identbf", [128, 128], BF16, kind="ExternalInput").ap()
    t["g0T"] = nc.dram_tensor("g0T", [D, TOK], F32, kind="ExternalOutput").ap()
    t["qT_o"] = nc.dram_tensor("qT_o", [D, TOK], BF16, kind="ExternalOutput").ap()
    t["kT_o"] = nc.dram_tensor("kT_o", [D, TOK], BF16, kind="ExternalOutput").ap()
    t["vv_o"] = nc.dram_tensor("vv_o", [TOK, D], BF16, kind="ExternalOutput").ap()
    t["xnT_o"] = nc.dram_tensor("xnT_o", [D, TOK], BF16, kind="ExternalOutput").ap()
    t["h2_dram"] = nc.dram_tensor("h2_spill", [HC * 128, TOK], BF16).ap()
    t["h3_dram"] = nc.dram_tensor("h3_spill", [HC * 128, TOK], BF16).ap()

    with nc.Block() as block:
        T = make_tracks(nc)
        emit_stageA(nc, t, block, T)
    return nc


def host_inputs_stageA(d, core):
    x = np.asarray(d["L0_attn_x"]).reshape(16384, 1024)
    sl = slice(core * TOK, (core + 1) * TOK)
    im = {
        "xT": np.ascontiguousarray(x[sl].T),
        "wq": blk_lhsT(np.asarray(d["L0_attn_W_q"])),
        "wk": blk_lhsT(np.asarray(d["L0_attn_W_k"])),
        "wv": blk_lhsT(np.asarray(d["L0_attn_W_v"])),
        "w1": blk_lhsT(np.asarray(d["L0_ffn_w1_w"]).T),
        "w2": blk_lhsT(np.asarray(d["L0_ffn_w2_w"]).T),
        "h1w": blk_lhsT(np.asarray(d["L0_ffn_h1_w"]).T),
        "h2w": blk_lhsT(np.asarray(d["L0_ffn_h2_w"]).T),
        "w3": blk_lhsT(np.asarray(d["L0_ffn_w3_w"]).T),
        "scls": np.concatenate([blk_scales(np.asarray(d[k])) for k in
                                ("L0_ffn_w1_s", "L0_ffn_w2_s", "L0_ffn_h1_s",
                                 "L0_ffn_h2_s", "L0_ffn_w3_s")], axis=1),
        "consts": np.stack([np.ones(128, np.float32),
                            np.full(128, 1.0 / 1024, np.float32)], axis=1),
        "identbf": np.eye(128, dtype=np.float32).astype(ml_dtypes.bfloat16),
    }
    return im







NB = 2731
NPC = 22
NQT = 6
QW = [512, 512, 512, 512, 512, 171]
PADB = 2816
SCALE = 0.125


def emit_stageB(nc, t, block, T):
    ld, tn, sc, ve, st = (T[k] for k in ["ld", "tn", "sc", "ve", "st"])

    qh_sb = nc.alloc_sbuf_tensor("qh_sb", [128, 2 * 8448], BF16)
    kh_sb = nc.alloc_sbuf_tensor("kh_sb", [128, 2 * 8448], BF16)
    vaug_h = nc.alloc_sbuf_tensor("vaug", [128, NPC * 512], BF16)
    e_sb = nc.alloc_sbuf_tensor("e_sb", [128, 16 * 512], BF16)
    s0_sb = nc.alloc_sbuf_tensor("s0_sb", [128, 2 * 8192], BF16)
    bm_sb = nc.alloc_sbuf_tensor("bm_sb", [128, 6 * 512], BF16)
    tm_sb = nc.alloc_sbuf_tensor("tm_sb", [128, 1], F32)
    cvec_sb = nc.alloc_sbuf_tensor("cvec_sb", [1, PADB], F32)
    onesrow = nc.alloc_sbuf_tensor("onesrowB", [1, 64], F32)
    misc = nc.alloc_sbuf_tensor("miscB", [128, 2], F32)
    rows = nc.alloc_sbuf_tensor("rowsB", [1, 4 * 512], F32)
    rb_sb = nc.alloc_sbuf_tensor("rb_sb", [64, 2 * 512], F32)

    psS = nc.alloc_psum_tensor("psS", [128, 4 * 512], F32)
    psO = nc.alloc_psum_tensor("psO", [128, 1024], F32)
    psb2 = nc.alloc_psum_tensor("psb2", [64, 1024], F32)

    @block.sync
    def _(sync):
        for i in range(2):
            ld.inc(sync.dma_start(out=qh_sb[:, i * 8448:(i + 1) * 8448], in_=t["qh"][i]), 16)
            ld.inc(sync.dma_start(out=kh_sb[:, i * 8448:(i + 1) * 8448], in_=t["kh"][i]), 16)
        ld.inc(sync.dma_start(out=bm_sb[:, :], in_=t["bmasks"]), 16)
        ld.inc(sync.dma_start(out=tm_sb[:, :], in_=t["tmask"]), 16)
        ld.inc(sync.dma_start(out=cvec_sb[:, :], in_=t["cvec"]), 16)
        ld.inc(sync.dma_start(out=misc[:, :], in_=t["consts"]), 16)

    @block.scalar
    def _(scalar):
        ld.wait(scalar)
        sc.inc(scalar.copy(out=onesrow[0:1, :], in_=misc[0:1, 0:1].broadcast_to([1, 64])))

    def vaug_lhsT(pc, h):
        return vaug_h[:, pc * 512 + 128 * h: pc * 512 + 128 * h + 128]

    GRP = 8
    stn = {}
    mask_done = {}
    av_done = {}
    po_free = {}
    rb_free = {}
    psb_free = {}
    v_ld = [0]
    eplan = lambda g: g // 2 + 2

    uq = 0
    gbase = 0
    for s in range(3):
        @block.gpsimd
        def _(gpsimd, s=s):
            if s > 0:
                tn.wait(gpsimd)
            src = t["vh"].rearrange("(r three) d -> r three d", three=3)
            for pc in range(NPC):
                st.inc(gpsimd.dma_start(
                    out=vaug_h[:, pc * 512: (pc + 1) * 512],
                    in_=src[pc * 128:(pc + 1) * 128, s:s + 1, :].squeeze(1)), 16)
        v_ld[0] = st.n

        for h in range(4):
            hp, hr = divmod(h, 2)
            qoff = hp * 8448
            hrow = hr * 64
            kt = lambda a, n, hrow=hrow, qoff=qoff: kh_sb[hrow:hrow + 64, qoff + a: qoff + a + 3 * (n - 1) + 1: 3]
            qt_ = lambda a, n, hrow=hrow, qoff=qoff: qh_sb[hrow:hrow + 64, qoff + a: qoff + a + 3 * (n - 1) + 1: 3]

            for qt in range(NQT):
                W = QW[qt]
                oslot = uq % 2
                ngr = (NPC + GRP - 1) // GRP

                for j in range(ngr + 1):
                    s_list = list(range(j * GRP, min((j + 1) * GRP, NPC))) if j < ngr else []
                    a_list = list(range((j - 1) * GRP, min(j * GRP, NPC))) if j >= 1 else []

                    @block.tensor
                    def _(tensor, s=s, h=h, qt=qt, W=W, oslot=oslot, uq=uq,
                          kt=kt, qt_=qt_, gbase=gbase, s_list=s_list, a_list=a_list):
                        for i in range(max(len(s_list), len(a_list))):
                            if i < len(s_list):
                                pc = s_list[i]
                                g = gbase + pc
                                if uq == 0 and pc == 0:
                                    ld.wait(tensor)
                                if g >= 4:
                                    sc.wait(tensor, g // 2)
                                tn.inc(tensor.matmul(
                                    psS[:, (g % 4) * 512:(g % 4) * 512 + W],
                                    kt(s + 384 * pc, 128),
                                    qt_(s + 1536 * qt, W),
                                    start=True, stop=True))
                                stn[g] = tn.n
                            if i < len(a_list):
                                pa = a_list[i]
                                ga = gbase + pa
                                if ga in mask_done:
                                    ve.wait(tensor, mask_done[ga])
                                else:
                                    sc.wait(tensor, eplan(ga))
                                if pa == 0:
                                    st.wait(tensor, v_ld[0])
                                    if oslot in po_free:
                                        ve.wait(tensor, po_free[oslot])
                                tn.inc(tensor.matmul(
                                    psO[:, oslot * 512: oslot * 512 + W],
                                    vaug_lhsT(pa, h),
                                    e_sb[:, (ga % 16) * 512:(ga % 16) * 512 + W],
                                    start=(pa == 0), stop=(pa == NPC - 1)))
                                av_done[ga] = tn.n

                    if s_list:
                        @block.scalar
                        def _(scalar, s_list=s_list, gbase=gbase):
                            for k in range(0, len(s_list), 2):
                                g0 = gbase + s_list[k]
                                tn.wait(scalar, stn[g0 + 1])
                                sc.inc(scalar.activation(
                                    out=e_sb[:, (g0 % 16) * 512:(g0 % 16) * 512 + 1024],
                                    in_=psS[:, (g0 % 4) * 512:(g0 % 4) * 512 + 1024],
                                    func=AF.Exp, scale=SCALE))
                                assert sc.n == eplan(g0), (sc.n, g0)

                        mk = []
                        for pc in s_list:
                            jj = pc - 4 * qt + 1
                            need_band = 0 <= jj < 6
                            need_tail = pc == NPC - 1
                            if need_band or need_tail:
                                mk.append((pc, jj, need_band, need_tail))
                        if mk:
                            @block.vector
                            def _(vector, mk=mk, gbase=gbase, W=W):
                                for pc, jj, need_band, need_tail in mk:
                                    g = gbase + pc
                                    sc.wait(vector, eplan(g))
                                    esl = e_sb[:, (g % 16) * 512:(g % 16) * 512 + W]
                                    if need_band:
                                        inst = vector.tensor_tensor(
                                            esl, esl, bm_sb[:, jj * 512: jj * 512 + W], ALU.mult)
                                    if need_tail:
                                        if need_band:
                                            ve.inc(inst)
                                            ve.wait(vector)
                                        inst = vector.tensor_scalar_mul(esl, esl, tm_sb[:, 0:1])
                                    ve.inc(inst)
                                    mask_done[g] = ve.n

                Wr = W if not (s == 2 and qt == NQT - 1) else W - 1
                dr = rows[0:1, oslot * 1024: oslot * 1024 + 512]
                dr2 = rows[0:1, oslot * 1024 + 512: oslot * 1024 + 1024]
                rb = rb_sb[:, oslot * 512: oslot * 512 + 512]

                @block.vector
                def _(vector, qt=qt, W=W, oslot=oslot, gbase=gbase):
                    tn.wait(vector, av_done[gbase + NPC - 1])
                    if oslot in rb_free:
                        ve.wait(vector, rb_free[oslot])
                    ve.inc(vector.tensor_tensor(dr[0:1, 0:W], psO[64:65, oslot * 512: oslot * 512 + W],
                                         cvec_sb[0:1, qt * 512: qt * 512 + W], ALU.add))
                    ve.wait(vector)
                    ve.inc(vector.reciprocal(dr2[0:1, 0:W], dr[0:1, 0:W]))
                recip_done = ve.n

                @block.tensor
                def _(tensor, W=W, oslot=oslot):
                    ve.wait(tensor, recip_done)
                    if oslot in psb_free:
                        ve.wait(tensor, psb_free[oslot])
                    tn.inc(tensor.matmul(psb2[:, oslot * 512: oslot * 512 + W],
                                         onesrow[0:1, :], dr2[0:1, 0:W],
                                         start=True, stop=True))
                bcast_done = tn.n

                @block.vector
                def _(vector, s=s, qt=qt, W=W, Wr=Wr, oslot=oslot, hp=hp, hrow=hrow, rb=rb):
                    tn.wait(vector, bcast_done)
                    ve.inc(vector.tensor_copy(rb[:, 0:W],
                                              psb2[:, oslot * 512: oslot * 512 + W]))
                    psb_free[oslot] = ve.n
                    ve.wait(vector)
                    base = hp * 8192 + s + 1536 * qt
                    out_ap = s0_sb[hrow:hrow + 64, base: base + 3 * (Wr - 1) + 1: 3]
                    ve.inc(vector.tensor_tensor(
                        out_ap, psO[0:64, oslot * 512: oslot * 512 + Wr],
                        rb[:, 0:Wr], ALU.mult))
                    po_free[oslot] = ve.n
                    rb_free[oslot] = ve.n
                uq += 1
                gbase += NPC

    @block.gpsimd
    def _(gpsimd):
        ve.wait(gpsimd)
        for i in range(2):
            st.inc(gpsimd.dma_start(out=t["s0h"][i], in_=s0_sb[:, i * 8192:(i + 1) * 8192]), 16)
        st.wait(gpsimd)


def build_stageB():
    nc = bass.Bass("TRN2", target_bir_lowering=False, debug=False)
    t = {}
    t["qh"] = nc.dram_tensor("qh", [2, 128, 8448], BF16, kind="ExternalInput").ap()
    t["kh"] = nc.dram_tensor("kh", [2, 128, 8448], BF16, kind="ExternalInput").ap()
    t["vh"] = nc.dram_tensor("vh", [8448, 512], BF16, kind="ExternalInput").ap()
    t["cvec"] = nc.dram_tensor("cvec", [1, PADB], F32, kind="ExternalInput").ap()
    t["bmasks"] = nc.dram_tensor("bmasks", [128, 6 * 512], BF16, kind="ExternalInput").ap()
    t["tmask"] = nc.dram_tensor("tmask", [128, 1], F32, kind="ExternalInput").ap()
    t["consts"] = nc.dram_tensor("consts", [128, 2], F32, kind="ExternalInput").ap()
    t["s0h"] = nc.dram_tensor("s0h", [2, 128, 8192], BF16, kind="ExternalOutput").ap()
    with nc.Block() as block:
        T = make_tracks(nc)
        emit_stageB(nc, t, block, T)
    return nc


def host_consts_B():
    import ml_dtypes
    cvec = np.zeros((1, PADB), np.float32)
    cvec[0, 0] = 2.0
    cvec[0, 1] = 1.0
    cvec[0, NB - 2] += 1.0
    cvec[0, NB - 1] += 2.0
    bm = np.ones((128, 6 * 512), np.float32)
    for j in range(6):
        d0 = (j - 1) * 128
        pl = np.arange(128)[:, None]
        ql = np.arange(512)[None, :]
        bm[:, j * 512:(j + 1) * 512] += (np.abs(d0 + pl - ql) <= 2).astype(np.float32)
    tm = (np.arange(128) < (NB - NPC * 128 + 128)).astype(np.float32).reshape(128, 1)
    consts = np.stack([np.ones(128, np.float32), np.full(128, 1 / 1024, np.float32)], 1)
    return (cvec, bm.astype(ml_dtypes.bfloat16), tm, consts)


def host_inputs_stageB(core, qT_full, kT_full, vv_full):
    import ml_dtypes
    b = core // 4
    h0 = 4 * (core % 4)
    cvec, bm, tm, consts = host_consts_B()
    qh = np.zeros((2, 128, 8448), dtype=ml_dtypes.bfloat16)
    kh = np.zeros((2, 128, 8448), dtype=ml_dtypes.bfloat16)
    qh[:, :, 0:8192] = qT_full[b][h0 * 64:(h0 + 4) * 64].reshape(2, 128, 8192)
    kh[:, :, 0:8192] = kT_full[b][h0 * 64:(h0 + 4) * 64].reshape(2, 128, 8192)
    vh = np.zeros((8448, 512), dtype=ml_dtypes.bfloat16)
    for hh in range(4):
        vh[0:8192, 128 * hh:128 * hh + 64] = vv_full[b][:, (h0 + hh) * 64:(h0 + hh + 1) * 64]
        vh[:, 128 * hh + 64:128 * hh + 128] = 1.0
    return {"qh": qh, "kh": kh, "vh": vh, "cvec": cvec, "bmasks": bm,
            "tmask": tm, "consts": consts}








def emit_stageC(nc, t, block, T):
    ld, wl, tn, sc, ve, st = (T[k] for k in ["ld", "wl", "tn", "sc", "ve", "st"])

    x_sb = nc.alloc_sbuf_tensor("x_sb", [128, DC * TOK], F32)
    xn_sb = nc.alloc_sbuf_tensor("xn_sb", [128, DC * TOK], BF16)
    misc = nc.alloc_sbuf_tensor("misc", [128, 8], F32)
    onesrow = nc.alloc_sbuf_tensor("onesrow", [1, 128], F32)
    ident = nc.alloc_sbuf_tensor("ident_sb", [128, 128], BF16)
    iota16 = nc.alloc_sbuf_tensor("iota16_sb", [128, 16], F32)
    wg_sb = nc.alloc_sbuf_tensor("wg_sb", [128, DC * 16], BF16)
    pe_sb = nc.alloc_sbuf_tensor("pe_sb", [128, 32], F32)

    psA = nc.alloc_psum_tensor("psA", [128, 1024], F32)
    psB = nc.alloc_psum_tensor("psB", [128, 1024], F32)
    psC = nc.alloc_psum_tensor("psC", [128, 1024], F32)
    pst = nc.alloc_psum_tensor("pst", [128, 512], BF16)
    psr = psA[0:1, :]

    x_t = lambda c: x_sb[:, c * TOK:(c + 1) * TOK]
    xn_t = lambda c: xn_sb[:, c * TOK:(c + 1) * TOK]
    psums = [psA[:, 0:512], psA[:, 512:1024], psC[:, 0:512], psC[:, 512:1024]]
    psum_free = {}

    @block.sync
    def _(sync):
        ld.inc(sync.dma_start(out=misc[:, 0:2], in_=t["consts"]), 16)
        ld.inc(sync.dma_start(out=ident[:, :], in_=t["identbf"]), 16)
        ld.inc(sync.dma_start(out=iota16[:, :], in_=t["iota16"]), 16)
        for c in range(DC):
            ld.inc(sync.dma_start(out=x_t(c), in_=t["xT"][c * 128:(c + 1) * 128, :]), 16)

    @block.gpsimd
    def _(gpsimd):
        wl.inc(gpsimd.dma_start(out=wg_sb.rearrange("p (c n) -> p c n", n=16), in_=t["wg"].transpose([1, 0, 2])), 16)

    @block.scalar
    def _(scalar):
        ld.wait(scalar)
        sc.inc(scalar.copy(out=onesrow[0:1, :], in_=misc[0:1, 0:1].broadcast_to([1, 128])))

    with (
        nc.sbuf_tensor("cstg", [128, 2 * (TOK + 2 * TOK)], F32) as cstg,
        nc.sbuf_tensor("s0bf", [128, 2 * TOK], BF16) as s0bf,
    ):
        ld_after = {}

        ldt = [T["ld"], T["ld2"]]

        @block.sync
        def _(sync):
            ldt[0].wait(sync)
            ldt[1].wait(sync)
            for c in range(2):
                ldt[c % 2].inc(sync.dma_start(out=s0bf[:, (c % 2) * TOK:((c % 2) + 1) * TOK],
                                              in_=t["s0T"][c * 128:(c + 1) * 128, :]), 16)
                ldt[c % 2].inc(sync.dma_start(out=cstg[:, (c % 2) * 3 * TOK:(c % 2) * 3 * TOK + TOK],
                                              in_=t["g0T"][c * 128:(c + 1) * 128, :]), 16)
                ld_after[c] = ldt[c % 2].n

        ve_after = {}
        for c in range(DC):
            par = c % 2
            g0v = cstg[:, par * 3 * TOK: par * 3 * TOK + TOK]
            s0f = cstg[:, par * 3 * TOK + TOK: par * 3 * TOK + 2 * TOK]
            tmpv = cstg[:, par * 3 * TOK + 2 * TOK: par * 3 * TOK + 3 * TOK]

            @block.scalar
            def _(scalar, c=c, par=par, s0f=s0f):
                ldt[c % 2].wait(scalar, ld_after[c])
                sc.inc(scalar.copy(out=s0f, in_=s0bf[:, par * TOK:(par + 1) * TOK]))

            @block.vector
            def _(vector, c=c, g0v=g0v, s0f=s0f, tmpv=tmpv):
                sc.wait(vector)
                vector.tensor_tensor(tmpv, s0f, g0v, ALU.mult)
                ve.inc(vector.tensor_tensor(x_t(c), x_t(c), tmpv, ALU.add))
                ve_after[c] = ve.n

            nxt = c + 2
            if nxt < DC:
                @block.sync
                def _(sync, nxt=nxt, par=par, c=c):
                    ve.wait(sync, ve_after[c])
                    ldt[par].inc(sync.dma_start(out=s0bf[:, par * TOK:(par + 1) * TOK],
                                                in_=t["s0T"][nxt * 128:(nxt + 1) * 128, :]), 16)
                    ldt[par].inc(sync.dma_start(out=cstg[:, par * 3 * TOK: par * 3 * TOK + TOK],
                                                in_=t["g0T"][nxt * 128:(nxt + 1) * 128, :]), 16)
                    ld_after[nxt] = ldt[par].n

        @block.gpsimd
        def _(gpsimd):
            ve.wait(gpsimd)
            for c in range(DC):
                st.inc(gpsimd.dma_start(out=t["x1T"][c * 128:(c + 1) * 128, :], in_=x_t(c)), 16)

    with (
        nc.sbuf_tensor("sq_sb", [128, DC * 512], F32) as sq_sb,
        nc.sbuf_tensor("rowbuf", [1, 6 * 512], F32) as rowbuf,
        nc.sbuf_tensor("bcast", [128, 1024], F32) as bcast,
    ):
        emit_ln(nc, block, T, x_t, xn_t, misc, onesrow, rowbuf, sq_sb, bcast,
                psr, psB, pre_wait=(ve,))

    pslog = psB
    log_sb = nc.alloc_sbuf_tensor("log_sb", [128, 2 * 16], F32)
    rt_sb = nc.alloc_sbuf_tensor("rt_sb", [128, 8], F32)
    msk_sb = nc.alloc_sbuf_tensor("msk_sb", [128, 16], U8)
    sel_sb = nc.alloc_sbuf_tensor("sel_sb", [128, 16], F32)
    big_sb = nc.alloc_sbuf_tensor("big_sb", [128, 16], F32)

    @block.vector
    def _(vector):
        ve.inc(vector.memset(big_sb[:, :], 1e9))

    lg_tn = {}
    lg_done = {}
    se_prev = [0]
    for j in range(16):
        slot = j % 2

        @block.tensor
        def _(tensor, j=j, slot=slot):
            if j == 0:
                ve.wait(tensor)
                wl.wait(tensor)
            if j >= 2:
                ve.wait(tensor, lg_done[j - 2])
            for c in range(DC):
                mm = tensor.matmul(pslog[:, slot * 512: slot * 512 + 16],
                                   xn_t(c)[:, j * 128:(j + 1) * 128],
                                   wg_sb[:, c * 16:(c + 1) * 16],
                                   start=(c == 0), stop=(c == DC - 1))
            tn.inc(mm)
            lg_tn[j] = tn.n

        @block.vector
        def _(vector, j=j, slot=slot):
            tn.wait(vector, lg_tn[j])
            if j > 0:
                sc.wait(vector, se_prev[0])
            lg = log_sb[:, slot * 16:(slot + 1) * 16]
            ve.inc(vector.tensor_copy(lg, pslog[:, slot * 512: slot * 512 + 16]))
            ve.wait(vector)
            ve.inc(vector.reduce_max(rt_sb[:, 0:1], lg, axis=AX.X))
            ve.wait(vector)
            ve.inc(vector.tensor_scalar_mul(rt_sb[:, 1:2], rt_sb[:, 0:1], -1.0))
        vm = ve.n

        @block.scalar
        def _(scalar, j=j, slot=slot):
            ve.wait(scalar, vm)
            lg = log_sb[:, slot * 16:(slot + 1) * 16]
            sc.inc(scalar.activation(out=sel_sb[:, :], in_=lg,
                                     func=AF.Exp, bias=rt_sb[:, 1:2]))
        se = sc.n
        se_prev[0] = se

        @block.vector
        def _(vector, j=j, slot=slot):
            sc.wait(vector, se)
            lg = log_sb[:, slot * 16:(slot + 1) * 16]
            ve.inc(vector.reduce_sum(rt_sb[:, 2:3], sel_sb[:, :], axis=AX.X))
            ve.wait(vector)
            vector.reciprocal(pe_sb[:, j:j + 1], rt_sb[:, 2:3])
            ve.inc(vector.tensor_scalar(out=msk_sb[:, :], in0=lg, scalar1=rt_sb[:, 0:1],
                                        scalar2=None, op0=ALU.is_ge))
            ve.wait(vector)
            ve.inc(vector.select(sel_sb[:, :], msk_sb[:, :], iota16[:, :], big_sb[:, :]))
            ve.wait(vector)
            ve.inc(vector.tensor_reduce(pe_sb[:, 16 + j:17 + j], sel_sb[:, :],
                                        axis=AX.X, op=ALU.min))
        lg_done[j] = ve.n

    @block.gpsimd
    def _(gpsimd):
        ve.wait(gpsimd)
        st.inc(gpsimd.dma_start(out=t["pe_o"], in_=pe_sb[:, :]), 16)

    with (
        nc.sbuf_tensor("qkstg", [128, 4 * TOK], BF16) as qkstg,
        nc.sbuf_tensor("v1T_sb", [128, DC * TOK], BF16) as v1T_sb,
        nc.sbuf_tensor("wqkv_buf", [128, 2 * DC * 128], BF16) as wqkv_buf,
        nc.sbuf_tensor("tstg", [128, 2 * D], BF16) as tstg,
    ):
        st_after = {}

        def evac(eng, oc, tt, ps):
            if oc < 16:
                par = oc % 2
                if oc >= 2:
                    st.wait(eng, st_after.get(oc - 2, 0))
                return eng.tensor_copy(
                    qkstg[:, (par * QT + tt) * 512:(par * QT + tt) * 512 + 512], ps)
            return eng.tensor_copy(
                v1T_sb[:, (oc - 16) * TOK + tt * 512:(oc - 16) * TOK + (tt + 1) * 512], ps)

        def flush(gpsimd, oc):
            if oc >= 16:
                return
            par = oc % 2
            dst = t["q1T"] if oc < 8 else t["k1T"]
            row = (oc % 8) * 128
            st.inc(gpsimd.dma_start(out=dst[row:row + 128, :],
                                    in_=qkstg[:, par * QT * 512:(par * QT + QT) * 512]), 16)
            st_after[oc] = st.n

        emit_proj(nc, block, T, t["wqkv1"], 24, DC, xn_t, wqkv_buf, psums, psum_free,
                  evac, "vector", ve, flush_fn=flush, extra_mm_wait=(ve,))

        tp_tn = {}
        tp_sc = {}
        u = 0
        for which, src_t, outdram in (("v", lambda c: v1T_sb[:, c * TOK:(c + 1) * TOK], t["vv1"]),
                                      ("xn", xn_t, t["xn1tok"])):
            for s_i in range(16):
                for oc in range(DC):
                    @block.tensor
                    def _(tensor, u=u, s_i=s_i, oc=oc, src_t=src_t):
                        if u == 0:
                            ve.wait(tensor)
                        if u >= 2:
                            sc.wait(tensor, tp_sc[u - 2])
                        mm = tensor.transpose(
                            pst[:, (u % 2) * 128:(u % 2) * 128 + 128],
                            src_t(oc)[:, s_i * 128:(s_i + 1) * 128], ident[:, :])
                        tn.inc(mm)
                        tp_tn[u] = tn.n

                    @block.scalar
                    def _(scalar, u=u, s_i=s_i, oc=oc):
                        tn.wait(scalar, tp_tn[u])
                        par = (s_i % 2)
                        sc.inc(scalar.copy(
                            out=tstg[:, par * D + oc * 128: par * D + (oc + 1) * 128],
                            in_=pst[:, (u % 2) * 128:(u % 2) * 128 + 128]))
                        tp_sc[u] = sc.n
                    u += 1

                @block.gpsimd
                def _(gpsimd, s_i=s_i, outdram=outdram, u=u):
                    sc.wait(gpsimd, tp_sc[u - 1])
                    st.inc(gpsimd.dma_start(
                        out=outdram[s_i * 128:(s_i + 1) * 128, :],
                        in_=tstg[:, (s_i % 2) * D:((s_i % 2) + 1) * D]), 16)

                @block.scalar
                def _(scalar):
                    st.wait(scalar)
                    sc.inc(scalar.copy(out=misc[:, 4:5], in_=misc[:, 0:1]))
                tp_sc[u - 1] = sc.n

    @block.gpsimd
    def _(gpsimd):
        st.wait(gpsimd)


def build_stageC():
    nc = bass.Bass("TRN2", target_bir_lowering=False, debug=False)
    t = {}
    t["xT"] = nc.dram_tensor("xT", [D, TOK], F32, kind="ExternalInput").ap()
    t["s0T"] = nc.dram_tensor("s0T", [D, TOK], BF16, kind="ExternalInput").ap()
    t["g0T"] = nc.dram_tensor("g0T", [D, TOK], F32, kind="ExternalInput").ap()
    t["wqkv1"] = nc.dram_tensor("wqkv1", [24, 128, DC, 128], F32, kind="ExternalInput").ap()
    t["wg"] = nc.dram_tensor("wg", [DC, 128, 16], F32, kind="ExternalInput").ap()
    t["identbf"] = nc.dram_tensor("identbf", [128, 128], BF16, kind="ExternalInput").ap()
    t["iota16"] = nc.dram_tensor("iota16", [128, 16], F32, kind="ExternalInput").ap()
    t["consts"] = nc.dram_tensor("consts", [128, 2], F32, kind="ExternalInput").ap()
    t["x1T"] = nc.dram_tensor("x1T", [D, TOK], F32, kind="ExternalOutput").ap()
    t["q1T"] = nc.dram_tensor("q1T", [D, TOK], BF16, kind="ExternalOutput").ap()
    t["k1T"] = nc.dram_tensor("k1T", [D, TOK], BF16, kind="ExternalOutput").ap()
    t["vv1"] = nc.dram_tensor("vv1", [TOK, D], BF16, kind="ExternalOutput").ap()
    t["xn1tok"] = nc.dram_tensor("xn1tok", [TOK, D], BF16, kind="ExternalOutput").ap()
    t["pe_o"] = nc.dram_tensor("pe_o", [128, 32], F32, kind="ExternalOutput").ap()
    with nc.Block() as block:
        T = make_tracks(nc)
        emit_stageC(nc, t, block, T)
    return nc


def host_inputs_stageC(d, core, s0_full, g0_full, x0_full):
    import ml_dtypes
    sl = slice(core * TOK, (core + 1) * TOK)
    iota = np.broadcast_to(np.arange(16, dtype=np.float32), (128, 16)).copy()
    return {
        "xT": np.ascontiguousarray(x0_full[sl].T).astype(np.float32),
        "s0T": np.ascontiguousarray(s0_full[sl].T.astype(ml_dtypes.bfloat16)),
        "g0T": np.ascontiguousarray(g0_full[sl].T).astype(np.float32),
        "wqkv1": blk_lhsT(np.asarray(d["L1_attn_w_qkv"]).T),
        "wg": np.ascontiguousarray(np.asarray(d["L1_ffn_Wg"]).reshape(DC, 128, 16)),
        "identbf": np.eye(128, dtype=np.float32).astype(ml_dtypes.bfloat16),
        "iota16": iota,
        "consts": np.stack([np.ones(128, np.float32),
                            np.full(128, 1 / 1024, np.float32)], 1),
    }







D1_NQT = 16
D1_SCALE = 0.125
LSEQ = 8192


def emit_stageD1(nc, t, block, T):
    ld, tn, sc, ve, st = (T[k] for k in ["ld", "tn", "sc", "ve", "st"])

    qh_sb = nc.alloc_sbuf_tensor("qh1_sb", [128, 2 * LSEQ], BF16)
    kh_sb = nc.alloc_sbuf_tensor("kh1_sb", [128, 2 * LSEQ], BF16)
    vaug_h = nc.alloc_sbuf_tensor("vaug1", [128, 64 * 512], BF16)
    e_sb = nc.alloc_sbuf_tensor("e1_sb", [128, 16 * 512], BF16)
    s1_sb = nc.alloc_sbuf_tensor("s1_sb", [128, 2 * LSEQ], BF16)
    cm_sb = nc.alloc_sbuf_tensor("cm_sb", [128, 4 * 512], BF16)
    onesrow = nc.alloc_sbuf_tensor("onesrowD1", [1, 64], F32)
    misc = nc.alloc_sbuf_tensor("miscD1", [128, 2], F32)
    rows = nc.alloc_sbuf_tensor("rowsD1", [1, 4 * 512], F32)
    rb_sb = nc.alloc_sbuf_tensor("rb1_sb", [64, 2 * 512], F32)

    psS = nc.alloc_psum_tensor("psS1", [128, 4 * 512], F32)
    psO = nc.alloc_psum_tensor("psO1", [128, 1024], F32)
    psb2 = nc.alloc_psum_tensor("psb21", [64, 1024], F32)

    @block.sync
    def _(sync):
        for i in range(2):
            ld.inc(sync.dma_start(out=qh_sb[:, i * LSEQ:(i + 1) * LSEQ], in_=t["q1h"][i]), 16)
            ld.inc(sync.dma_start(out=kh_sb[:, i * LSEQ:(i + 1) * LSEQ], in_=t["k1h"][i]), 16)
        ld.inc(sync.dma_start(out=cm_sb[:, :], in_=t["cmasks"]), 16)
        ld.inc(sync.dma_start(out=misc[:, :], in_=t["consts"]), 16)
        for pc in range(64):
            ld.inc(sync.dma_start(out=vaug_h[:, pc * 512:(pc + 1) * 512],
                                  in_=t["vh1"][pc * 128:(pc + 1) * 128, :]), 16)

    @block.scalar
    def _(scalar):
        ld.wait(scalar, 96)
        sc.inc(scalar.copy(out=onesrow[0:1, :], in_=misc[0:1, 0:1].broadcast_to([1, 64])))

    vaug_lhsT = lambda pc, h: vaug_h[:, pc * 512 + 128 * h: pc * 512 + 128 * h + 128]

    GRP = 8
    stn = {}
    mask_done = {}
    av_done = {}
    po_free = {}
    rb_free = {}
    psb_free = {}
    eplan = lambda g: g // 2 + 2

    uq = 0
    gbase = 0
    for h in range(4):
        hp, hr = divmod(h, 2)
        qoff = hp * LSEQ
        hrow = hr * 64
        kt = lambda pc, hrow=hrow, qoff=qoff: kh_sb[hrow:hrow + 64, qoff + pc * 128: qoff + (pc + 1) * 128]
        qt_ = lambda qt, hrow=hrow, qoff=qoff: qh_sb[hrow:hrow + 64, qoff + qt * 512: qoff + (qt + 1) * 512]

        for qt in range(D1_NQT):
            npc = 4 * qt + 4
            oslot = uq % 2
            ngr = (npc + GRP - 1) // GRP

            for j in range(ngr + 1):
                s_list = list(range(j * GRP, min((j + 1) * GRP, npc))) if j < ngr else []
                a_list = list(range((j - 1) * GRP, min(j * GRP, npc))) if j >= 1 else []

                @block.tensor
                def _(tensor, h=h, qt=qt, oslot=oslot, uq=uq, kt=kt, qt_=qt_,
                      gbase=gbase, npc=npc, s_list=s_list, a_list=a_list):
                    for i in range(max(len(s_list), len(a_list))):
                        if i < len(s_list):
                            pc = s_list[i]
                            g = gbase + pc
                            if uq == 0 and pc == 0:
                                ld.wait(tensor, 32)
                            if uq == 32 and pc == 0:
                                ld.wait(tensor, 64)
                            if g >= 4:
                                sc.wait(tensor, g // 2)
                            tn.inc(tensor.matmul(
                                psS[:, (g % 4) * 512:(g % 4 + 1) * 512],
                                kt(pc), qt_(qt), start=True, stop=True))
                            stn[g] = tn.n
                        if i < len(a_list):
                            pa = a_list[i]
                            ga = gbase + pa
                            if ga in mask_done:
                                ve.wait(tensor, mask_done[ga])
                            else:
                                sc.wait(tensor, eplan(ga))
                            if pa == 0:
                                if uq < 16:
                                    ld.wait(tensor, 16 * (6 + npc))
                                elif uq == 16:
                                    ld.wait(tensor, 16 * 70)
                                if oslot in po_free:
                                    ve.wait(tensor, po_free[oslot])
                            tn.inc(tensor.matmul(
                                psO[:, oslot * 512:(oslot + 1) * 512],
                                vaug_lhsT(pa, h),
                                e_sb[:, (ga % 16) * 512:(ga % 16 + 1) * 512],
                                start=(pa == 0), stop=(pa == npc - 1)))
                            av_done[ga] = tn.n

                if s_list:
                    @block.scalar
                    def _(scalar, s_list=s_list, gbase=gbase):
                        for k in range(0, len(s_list), 2):
                            g0 = gbase + s_list[k]
                            tn.wait(scalar, stn[g0 + 1])
                            sc.inc(scalar.activation(
                                out=e_sb[:, (g0 % 16) * 512:(g0 % 16) * 512 + 1024],
                                in_=psS[:, (g0 % 4) * 512:(g0 % 4) * 512 + 1024],
                                func=AF.Exp, scale=D1_SCALE))
                            assert sc.n == eplan(g0), (sc.n, g0)

                    masks_here = [pc for pc in s_list if 0 <= pc - 4 * qt < 4]
                    if masks_here:
                        @block.vector
                        def _(vector, masks_here=masks_here, gbase=gbase, qt=qt):
                            for pc in masks_here:
                                g = gbase + pc
                                jj = pc - 4 * qt
                                sc.wait(vector, eplan(g))
                                esl = e_sb[:, (g % 16) * 512:(g % 16 + 1) * 512]
                                ve.inc(vector.tensor_tensor(
                                    esl, esl, cm_sb[:, jj * 512:(jj + 1) * 512], ALU.mult))
                                mask_done[g] = ve.n

            dr2 = rows[0:1, oslot * 1024 + 512: oslot * 1024 + 1024]
            rb = rb_sb[:, oslot * 512:(oslot + 1) * 512]

            @block.vector
            def _(vector, qt=qt, oslot=oslot, npc=npc, gbase=gbase):
                tn.wait(vector, av_done[gbase + npc - 1])
                if oslot in rb_free:
                    ve.wait(vector, rb_free[oslot])
                ve.inc(vector.reciprocal(dr2, psO[64:65, oslot * 512:(oslot + 1) * 512]))
            recip_done = ve.n

            @block.tensor
            def _(tensor, oslot=oslot):
                ve.wait(tensor, recip_done)
                if oslot in psb_free:
                    ve.wait(tensor, psb_free[oslot])
                tn.inc(tensor.matmul(psb2[:, oslot * 512:(oslot + 1) * 512],
                                     onesrow[0:1, :], dr2, start=True, stop=True))
            bcast_done = tn.n

            @block.vector
            def _(vector, qt=qt, oslot=oslot, hp=hp, hrow=hrow, rb=rb):
                tn.wait(vector, bcast_done)
                ve.inc(vector.tensor_copy(rb, psb2[:, oslot * 512:(oslot + 1) * 512]))
                psb_free[oslot] = ve.n
                ve.wait(vector)
                out_ap = s1_sb[hrow:hrow + 64, hp * LSEQ + qt * 512: hp * LSEQ + (qt + 1) * 512]
                ve.inc(vector.tensor_tensor(
                    out_ap, psO[0:64, oslot * 512:(oslot + 1) * 512], rb, ALU.mult))
                po_free[oslot] = ve.n
                rb_free[oslot] = ve.n
            uq += 1
            gbase += npc

    @block.gpsimd
    def _(gpsimd):
        ve.wait(gpsimd)
        for i in range(2):
            st.inc(gpsimd.dma_start(out=t["s1h"][i], in_=s1_sb[:, i * LSEQ:(i + 1) * LSEQ]), 16)
        st.wait(gpsimd)


def build_stageD1():
    nc = bass.Bass("TRN2", target_bir_lowering=False, debug=False)
    t = {}
    t["q1h"] = nc.dram_tensor("q1h", [2, 128, LSEQ], BF16, kind="ExternalInput").ap()
    t["k1h"] = nc.dram_tensor("k1h", [2, 128, LSEQ], BF16, kind="ExternalInput").ap()
    t["vh1"] = nc.dram_tensor("vh1", [LSEQ, 512], BF16, kind="ExternalInput").ap()
    t["cmasks"] = nc.dram_tensor("cmasks", [128, 4 * 512], BF16, kind="ExternalInput").ap()
    t["consts"] = nc.dram_tensor("consts", [128, 2], F32, kind="ExternalInput").ap()
    t["s1h"] = nc.dram_tensor("s1h", [2, 128, LSEQ], BF16, kind="ExternalOutput").ap()
    with nc.Block() as block:
        T = make_tracks(nc)
        emit_stageD1(nc, t, block, T)
    return nc


def host_consts_D1():
    import ml_dtypes
    cm = np.zeros((128, 4 * 512), np.float32)
    pl = np.arange(128)[:, None]
    ql = np.arange(512)[None, :]
    for j in range(4):
        cm[:, j * 512:(j + 1) * 512] = (128 * j + pl <= ql).astype(np.float32)
    return cm.astype(ml_dtypes.bfloat16)


def host_inputs_stageD1(core, qT_full, kT_full, vv_full):
    import ml_dtypes
    b = core // 4
    h0 = 4 * (core % 4)
    qh = qT_full[b][h0 * 64:(h0 + 4) * 64].reshape(2, 128, LSEQ).astype(ml_dtypes.bfloat16)
    kh = kT_full[b][h0 * 64:(h0 + 4) * 64].reshape(2, 128, LSEQ).astype(ml_dtypes.bfloat16)
    vh = np.ones((LSEQ, 512), dtype=ml_dtypes.bfloat16)
    for hh in range(4):
        vh[:, 128 * hh:128 * hh + 64] = vv_full[b][:, (h0 + hh) * 64:(h0 + hh + 1) * 64]
    return {"q1h": qh, "k1h": kh, "vh1": vh, "cmasks": host_consts_D1(),
            "consts": np.stack([np.ones(128, np.float32),
                                np.full(128, 1 / 1024, np.float32)], 1)}







CAP = 1280
NEXP = 2


def emit_stageD2(nc, t, block, T, upto=3):
    ld, wl, tn, sc, ve, st = (T[k] for k in ["ld", "wl", "tn", "sc", "ve", "st"])

    pe_sb = nc.alloc_sbuf_tensor("pe_sb", [128, 256], F32)
    expid = nc.alloc_sbuf_tensor("expid_sb", [128, 2], F32)
    iotat = nc.alloc_sbuf_tensor("iotat", [128, 128], F32)
    iotaf = nc.alloc_sbuf_tensor("iotaf_sb", [128, 128], F32)
    iotau = nc.alloc_sbuf_tensor("iotau", [128, 128 * 8], U32)
    sl_sb = nc.alloc_sbuf_tensor("sl_sb", [128, 128], F32)
    misc = nc.alloc_sbuf_tensor("miscD", [128, 4], F32)
    onesrow = nc.alloc_sbuf_tensor("onesrowD", [1, 128], F32)
    ident = nc.alloc_sbuf_tensor("identD", [128, 128], BF16)
    ones128 = nc.alloc_sbuf_tensor("ones128", [128, 128], F32)

    ind = nc.alloc_sbuf_tensor("ind", [128, 128], F32)
    pm = nc.alloc_sbuf_tensor("pm", [128, 128], F32)
    ge = nc.alloc_sbuf_tensor("ge", [128, 128], F32)
    sc128 = nc.alloc_sbuf_tensor("sc128", [128, 16], F32)
    scu8 = nc.alloc_sbuf_tensor("scu8", [128, 1], U8)
    tie = nc.alloc_sbuf_tensor("tie", [128, 128], F32)
    kept = nc.alloc_sbuf_tensor("kept", [128, 128], F32)
    slot = nc.alloc_sbuf_tensor("slot", [128, 128], F32)
    scan = nc.alloc_sbuf_tensor("scan", [128, 128], F32)
    zero128 = nc.alloc_sbuf_tensor("zero128", [128, 128], F32)
    offu = nc.alloc_sbuf_tensor("offu", [128, 128], U32)
    gidx_sb = nc.alloc_sbuf_tensor("gidx_sb", [128, 80], U32)
    initu = nc.alloc_sbuf_tensor("initu", [128, 80], U32)

    psq = nc.alloc_psum_tensor("psq", [128, 1024], F32)
    psw = nc.alloc_psum_tensor("psw", [128, 1024], F32)
    pst = nc.alloc_psum_tensor("pstD", [128, 512], BF16)

    gidx_dram = nc.dram_tensor("gidx_dram", [CAP + 128, 8], U32).ap()

    LO, HI, MID, CNT, SEL, MM, OFFS, T1 = range(8)
    col = lambda i: sc128[:, i:i + 1]

    @block.sync
    def _(sync):
        ld.inc(sync.dma_start(out=pe_sb[:, :], in_=t["pe_all"]), 16)
        ld.inc(sync.dma_start(out=expid[:, :], in_=t["expid"]), 16)
        ld.inc(sync.dma_start(out=iotat[:, :], in_=t["iotatok"]), 16)
        ld.inc(sync.dma_start(out=iotaf[:, :], in_=t["iotaf"]), 16)
        ld.inc(sync.dma_start(out=sl_sb[:, :], in_=t["slmat"]), 16)
        ld.inc(sync.dma_start(out=misc[:, 0:2], in_=t["consts"]), 16)
        ld.inc(sync.dma_start(out=ident[:, :], in_=t["identbf"]), 16)

    @block.scalar
    def _(scalar):
        ld.wait(scalar)
        sc.inc(scalar.copy(out=onesrow[0:1, :], in_=misc[0:1, 0:1].broadcast_to([1, 128])))

    @block.vector
    def _(vector):
        ld.wait(vector)
        vector.memset(zero128[:, :], 0.0)
        vector.memset(initu[:, :], 16383)
        vector.memset(ones128[:, :], 1.0)
        ve.inc(vector.tensor_copy(
            iotau[:, :].rearrange("p (f k) -> p f k", k=8),
            iotat[:, :, None].broadcast_to([128, 128, 8])))

    def vsync(vector, inst):
        ve.inc(inst)
        ve.wait(vector)

    p_all = pe_sb[:, 0:128]
    e_all = pe_sb[:, 128:256]

    for e_i in range(NEXP):
        @block.vector
        def _(vector, e_i=e_i):
            if e_i == 0:
                sc.wait(vector)
            else:
                st.wait(vector)
            vsync(vector, vector.tensor_scalar(out=ind[:, :], in0=e_all,
                                               scalar1=expid[:, e_i:e_i + 1],
                                               scalar2=None, op0=ALU.is_equal))
            vsync(vector, vector.tensor_tensor(pm[:, :], p_all, ind[:, :], ALU.mult))
            vsync(vector, vector.memset(col(LO), 0.0))
            vsync(vector, vector.memset(col(HI), 1.0))

        for it in range(40):
            @block.vector
            def _(vector, it=it):
                vsync(vector, vector.tensor_tensor(col(MID), col(LO), col(HI), ALU.add))
                vsync(vector, vector.tensor_scalar_mul(col(MID), col(MID), 0.5))
                vsync(vector, vector.tensor_scalar(out=ge[:, :], in0=pm[:, :],
                                                   scalar1=col(MID), scalar2=None,
                                                   op0=ALU.is_gt))
                vsync(vector, vector.reduce_sum(scan[:, 0:1], ge[:, :], axis=AX.X))
            rdy = ve.n

            @block.tensor
            def _(tensor, it=it):
                ve.wait(tensor, rdy)
                tn.inc(tensor.matmul(psq[:, 512:513], ones128[:, :], scan[:, 0:1],
                                     start=True, stop=True))
            bc = tn.n

            @block.vector
            def _(vector, it=it):
                tn.wait(vector, bc)
                vsync(vector, vector.tensor_copy(col(CNT), psq[:, 512:513]))
                vsync(vector, vector.tensor_scalar(out=col(SEL), in0=col(CNT),
                                                   scalar1=float(CAP), scalar2=None,
                                                   op0=ALU.is_ge))
                vsync(vector, vector.tensor_tensor(col(T1), col(MID), col(LO), ALU.subtract))
                vsync(vector, vector.tensor_tensor(col(T1), col(T1), col(SEL), ALU.mult))
                vsync(vector, vector.tensor_tensor(col(LO), col(LO), col(T1), ALU.add))
                vsync(vector, vector.tensor_tensor(col(T1), col(HI), col(MID), ALU.subtract))
                vsync(vector, vector.tensor_tensor(col(T1), col(T1), col(SEL), ALU.mult))
                vsync(vector, vector.tensor_tensor(col(HI), col(MID), col(T1), ALU.add))

        @block.vector
        def _(vector, e_i=e_i):
            vsync(vector, vector.tensor_scalar(out=kept[:, :], in0=pm[:, :],
                                               scalar1=col(HI), scalar2=None,
                                               op0=ALU.is_gt))
            vsync(vector, vector.tensor_scalar(out=tie[:, :], in0=pm[:, :],
                                               scalar1=col(LO), scalar2=None,
                                               op0=ALU.is_gt))
            vsync(vector, vector.tensor_tensor(tie[:, :], tie[:, :], kept[:, :],
                                               ALU.subtract))
            vsync(vector, vector.reduce_sum(scan[:, 0:1], kept[:, :], axis=AX.X))
        rdy2 = ve.n

        @block.tensor
        def _(tensor, e_i=e_i):
            ve.wait(tensor, rdy2)
            tn.inc(tensor.matmul(psq[:, 512:513], ones128[:, :], scan[:, 0:1],
                                 start=True, stop=True))
        bcm = tn.n

        @block.vector
        def _(vector, e_i=e_i):
            tn.wait(vector, bcm)
            vsync(vector, vector.tensor_copy(col(MM), psq[:, 512:513]))
            vsync(vector, vector.tensor_tensor_scan(scan[:, :], tie[:, :],
                                                    zero128[:, :], 0.0,
                                                    ALU.add, ALU.add))
            vsync(vector, vector.reduce_sum(sc128[:, T1:T1 + 1], tie[:, :], axis=AX.X))
        rdy3 = ve.n

        @block.tensor
        def _(tensor, e_i=e_i):
            ve.wait(tensor, rdy3)
            tn.inc(tensor.matmul(psq[:, 512:513], sl_sb[:, :], sc128[:, T1:T1 + 1],
                                 start=True, stop=True))
        bco = tn.n

        @block.vector
        def _(vector, e_i=e_i):
            tn.wait(vector, bco)
            vsync(vector, vector.tensor_copy(col(OFFS), psq[:, 512:513]))
            vsync(vector, vector.tensor_scalar(out=scan[:, :], in0=scan[:, :],
                                               scalar1=col(OFFS), scalar2=None,
                                               op0=ALU.add))
            vsync(vector, vector.tensor_tensor(scan[:, :], scan[:, :], tie[:, :],
                                               ALU.subtract))
            vsync(vector, vector.tensor_scalar(out=scan[:, :], in0=scan[:, :],
                                               scalar1=col(MM), scalar2=None,
                                               op0=ALU.add))
            vsync(vector, vector.tensor_scalar(out=scan[:, :], in0=scan[:, :],
                                               scalar1=float(CAP), scalar2=None,
                                               op0=ALU.is_lt))
            vsync(vector, vector.tensor_tensor(scan[:, :], scan[:, :], tie[:, :],
                                               ALU.mult))
            vsync(vector, vector.tensor_tensor(kept[:, :], kept[:, :], scan[:, :],
                                               ALU.add))
            vsync(vector, vector.tensor_tensor_scan(slot[:, :], kept[:, :],
                                                    zero128[:, :], 0.0,
                                                    ALU.add, ALU.add))
            vsync(vector, vector.reduce_sum(sc128[:, T1:T1 + 1], kept[:, :], axis=AX.X))
        rdy4 = ve.n

        @block.tensor
        def _(tensor, e_i=e_i):
            ve.wait(tensor, rdy4)
            tn.inc(tensor.matmul(psq[:, 512:513], sl_sb[:, :], sc128[:, T1:T1 + 1],
                                 start=True, stop=True))
        bco2 = tn.n

        @block.vector
        def _(vector, e_i=e_i):
            tn.wait(vector, bco2)
            vsync(vector, vector.tensor_copy(col(OFFS), psq[:, 512:513]))
            vsync(vector, vector.tensor_scalar(out=slot[:, :], in0=slot[:, :],
                                               scalar1=col(OFFS), scalar2=None,
                                               op0=ALU.add))
            vsync(vector, vector.tensor_tensor(slot[:, :], slot[:, :], kept[:, :],
                                               ALU.subtract))
            vsync(vector, vector.tensor_scalar(out=scan[:, :], in0=iotaf[:, :],
                                               scalar1=float(CAP), scalar2=None,
                                               op0=ALU.add))
            vsync(vector, vector.tensor_scalar_mul(ge[:, :], kept[:, :], -1.0))
            vsync(vector, vector.tensor_scalar_add(ge[:, :], ge[:, :], 1.0))
            vsync(vector, vector.tensor_tensor(scan[:, :], scan[:, :], ge[:, :],
                                               ALU.mult))
            vsync(vector, vector.tensor_tensor(ge[:, :], slot[:, :], kept[:, :],
                                               ALU.mult))
            vsync(vector, vector.tensor_tensor(scan[:, :], scan[:, :], ge[:, :],
                                               ALU.add))
            vsync(vector, vector.tensor_copy(offu[:, :], scan[:, :]))

        @block.gpsimd
        def _(gpsimd, e_i=e_i):
            ve.wait(gpsimd)
            st.inc(gpsimd.dma_start(out=t["keptslot"][e_i, 0], in_=kept[:, :]), 16)
            st.inc(gpsimd.dma_start(out=t["keptslot"][e_i, 1], in_=slot[:, :]), 16)
            st.inc(gpsimd.dma_start(
                out=gidx_dram[0:CAP, :].rearrange("(p g) k -> p (g k)", g=10),
                in_=initu[:, :]), 16)

        if upto < 2:
            continue
        @block.gpsimd
        def _(gpsimd, e_i=e_i):
            st.wait(gpsimd)
            for f in range(128):
                inst = gpsimd.indirect_dma_start(
                    out=gidx_dram, out_offset=IndirectOffsetOnAxis(ap=offu[:, f:f + 1], axis=0),
                    in_=iotau[:, f * 8:(f + 1) * 8], in_offset=None)
                st.inc(inst, 16)

        with (
            nc.sbuf_tensor("xe_sb%d" % e_i, [128, 10 * D], BF16) as xe_sb,
            nc.sbuf_tensor("xeT_sb%d" % e_i, [128, 8 * CAP], BF16) as xeT_sb,
            nc.sbuf_tensor("hT_sb%d" % e_i, [128, 16 * CAP], BF16) as hT_sb,
            nc.sbuf_tensor("w1_sb%d" % e_i, [128, 2 * 8 * 128], BF16) as w1_sb,
            nc.sbuf_tensor("w2_sb%d" % e_i, [128, 16 * 1024], BF16) as w2_sb,
            nc.sbuf_tensor("b1_sb%d" % e_i, [128, 16], F32) as b1_sb,
            nc.sbuf_tensor("b2b_sb%d" % e_i, [128, 1024], F32) as b2b_sb,
            nc.sbuf_tensor("b2r_sb%d" % e_i, [1, 1024], F32) as b2r_sb,
            nc.sbuf_tensor("mo_sb%d" % e_i, [128, 10 * D], BF16) as mo_sb,
        ):
            @block.sync
            def _(sync, e_i=e_i):
                st.wait(sync)
                ld.wait(sync)
                ld.inc(sync.dma_start(out=gidx_sb[:, :],
                                      in_=gidx_dram[0:CAP, :].rearrange("(p g) k -> p (g k)", g=10)), 16)
                ld.inc(sync.dma_start(out=b1_sb[:, :], in_=t["b1s"][e_i]), 16)
                ld.inc(sync.dma_start(out=b2r_sb[:, :], in_=t["b2r"][e_i]), 16)

            @block.gpsimd
            def _(gpsimd, e_i=e_i):
                ld.wait(gpsimd)
                for g in range(10):
                    inst = gpsimd.indirect_dma_start(
                        out=xe_sb[:, g * D:(g + 1) * D], out_offset=None,
                        in_=t["xn1tok"],
                        in_offset=IndirectOffsetOnAxis(ap=gidx_sb[:, g * 8:g * 8 + 1], axis=0))
                    st.inc(inst, 16)
                T["gp"].inc(gpsimd.dma_start(
                    out=w2_sb.rearrange("p (c n) -> p c n", n=1024),
                    in_=t["w2blk"][e_i].transpose([1, 0, 2])), 16)

            if upto < 3:
                @block.gpsimd
                def _(gpsimd, e_i=e_i, xe_sb=xe_sb):
                    st.wait(gpsimd)
                    mo3 = t["moe_out"][e_i].rearrange("(p g) d -> p g d", g=10)
                    for g in range(10):
                        st.inc(gpsimd.dma_start(
                            out=mo3[:, g:g + 1, :].squeeze(1),
                            in_=xe_sb[:, g * D:(g + 1) * D]), 16)
                continue
            tp_tn = {}
            tp_sc = {}
            for u in range(80):
                g, c = divmod(u, 8)

                @block.tensor
                def _(tensor, u=u, g=g, c=c, xe_sb=xe_sb):
                    if u == 0:
                        st.wait(tensor)
                    if u >= 2:
                        sc.wait(tensor, tp_sc[u - 2])
                    mm = tensor.transpose(pst[:, (u % 2) * 128:(u % 2) * 128 + 128],
                                          xe_sb[:, g * D + c * 128: g * D + (c + 1) * 128],
                                          ident[:, :])
                    tn.inc(mm)
                    tp_tn[u] = tn.n

                @block.scalar
                def _(scalar, u=u, g=g, c=c, xeT_sb=xeT_sb):
                    tn.wait(scalar, tp_tn[u])
                    sc.inc(scalar.copy(
                        out=xeT_sb[:, c * CAP + g * 128: c * CAP + (g + 1) * 128],
                        in_=pst[:, (u % 2) * 128:(u % 2) * 128 + 128]))
                    tp_sc[u] = sc.n

            if upto == 25:
                @block.gpsimd
                def _(gpsimd, e_i=e_i, xeT_sb=xeT_sb):
                    sc.wait(gpsimd)
                    mo3 = t["moe_out"][e_i].rearrange("(p g) d -> p g d", g=10)
                    for g in range(10):
                        st.inc(gpsimd.dma_start(
                            out=mo3[:, g:g + 1, :].squeeze(1),
                            in_=xe_sb[:, g * D:(g + 1) * D]), 16)
                continue
            xeT_t = lambda c: xeT_sb[:, c * CAP:(c + 1) * CAP]
            hT_t = lambda j: hT_sb[:, j * CAP:(j + 1) * CAP]
            NT3 = [(0, 512), (512, 512), (1024, 256)]
            wl_after = {}

            wlt = [T["wl"], T["wl2"]]

            @block.gpsimd
            def _(gpsimd, e_i=e_i):
                sc.wait(gpsimd)
                for oc in range(2):
                    wlt[oc % 2].inc(gpsimd.dma_start(
                        out=w1_sb[:, (oc % 2) * 1024:((oc % 2) + 1) * 1024],
                        in_=t["w1blk"][e_i, oc].rearrange("p ic o -> p (ic o)")), 16)
                    wl_after[oc] = wlt[oc % 2].n

            h_tn = {}
            h_sc = {}
            for oc in range(16):
                for ti, (t0, tw) in enumerate(NT3):
                    u = oc * 3 + ti

                    @block.tensor
                    def _(tensor, oc=oc, ti=ti, t0=t0, tw=tw, u=u, xeT_t=xeT_t):
                        if ti == 0:
                            wlt[oc % 2].wait(tensor, wl_after[oc])
                        slot_ps = u % 2
                        if u >= 2:
                            sc.wait(tensor, h_sc[u - 2])
                        for ic in range(8):
                            mm = tensor.matmul(
                                psw[:, slot_ps * 512: slot_ps * 512 + tw],
                                w1_sb[:, (oc % 2) * 1024 + ic * 128:(oc % 2) * 1024 + (ic + 1) * 128],
                                xeT_t(ic)[:, t0:t0 + tw],
                                start=(ic == 0), stop=(ic == 7))
                        tn.inc(mm)
                        h_tn[u] = tn.n

                    @block.scalar
                    def _(scalar, oc=oc, ti=ti, t0=t0, tw=tw, u=u, hT_t=hT_t):
                        if u == 0:
                            ld.wait(scalar)
                        slot_ps = u % 2
                        tn.wait(scalar, h_tn[u])
                        sc.inc(scalar.activation(
                            out=hT_t(oc)[:, t0:t0 + tw],
                            in_=psw[:, slot_ps * 512: slot_ps * 512 + tw],
                            func=AF.Gelu_apprx_tanh, bias=b1_sb[:, oc:oc + 1]))
                        h_sc[u] = sc.n

                nxt = oc + 2
                if nxt < 16:
                    @block.gpsimd
                    def _(gpsimd, oc=oc, nxt=nxt, e_i=e_i):
                        tn.wait(gpsimd, h_tn[oc * 3 + 2])
                        wlt[nxt % 2].inc(gpsimd.dma_start(
                            out=w1_sb[:, (nxt % 2) * 1024:((nxt % 2) + 1) * 1024],
                            in_=t["w1blk"][e_i, nxt].rearrange("p ic o -> p (ic o)")), 16)
                    wl_after[nxt] = wlt[nxt % 2].n

            @block.tensor
            def _(tensor):
                ld.wait(tensor)
                ve.wait(tensor)
                tensor.matmul(psq[:, 0:512], onesrow[0:1, :], b2r_sb[0:1, 0:512],
                              start=True, stop=True)
                tn.inc(tensor.matmul(psq[:, 512:1024], onesrow[0:1, :], b2r_sb[0:1, 512:1024],
                                     start=True, stop=True))
            b2bc = tn.n

            @block.vector
            def _(vector):
                tn.wait(vector, b2bc)
                ve.inc(vector.tensor_copy(b2b_sb[:, :], psq[:, 0:1024]))
            b2done = ve.n

            o_tn = {}
            o_ve = {}
            for mt in range(10):
                for nn in range(2):
                    u = mt * 2 + nn

                    @block.tensor
                    def _(tensor, mt=mt, nn=nn, u=u, hT_t=hT_t):
                        if u == 0:
                            T["gp"].wait(tensor)
                            sc.wait(tensor)
                        slot_ps = u % 2
                        if u >= 2:
                            ve.wait(tensor, o_ve[u - 2])
                        for hc in range(16):
                            mm = tensor.matmul(
                                psw[:, slot_ps * 512: slot_ps * 512 + 512],
                                hT_t(hc)[:, mt * 128:(mt + 1) * 128],
                                w2_sb[:, hc * 1024 + nn * 512: hc * 1024 + (nn + 1) * 512],
                                start=(hc == 0), stop=(hc == 15))
                        tn.inc(mm)
                        o_tn[u] = tn.n

                    @block.vector
                    def _(vector, mt=mt, nn=nn, u=u):
                        if u == 0:
                            ve.wait(vector, b2done)
                        slot_ps = u % 2
                        tn.wait(vector, o_tn[u])
                        ve.inc(vector.tensor_tensor(
                            mo_sb[:, mt * D + nn * 512: mt * D + (nn + 1) * 512],
                            psw[:, slot_ps * 512: slot_ps * 512 + 512],
                            b2b_sb[:, nn * 512:(nn + 1) * 512], ALU.add))
                        o_ve[u] = ve.n

            @block.gpsimd
            def _(gpsimd, e_i=e_i):
                ve.wait(gpsimd)
                mo3 = t["moe_out"][e_i].rearrange("(p g) d -> p g d", g=10)
                for g in range(10):
                    st.inc(gpsimd.dma_start(
                        out=mo3[:, g:g + 1, :].squeeze(1),
                        in_=mo_sb[:, g * D:(g + 1) * D]), 16)

    @block.gpsimd
    def _(gpsimd):
        st.wait(gpsimd)


def build_stageD2(upto=3):
    nc = bass.Bass("TRN2", target_bir_lowering=False, debug=False)
    t = {}
    t["pe_all"] = nc.dram_tensor("pe_all", [128, 256], F32, kind="ExternalInput").ap()
    t["xn1tok"] = nc.dram_tensor("xn1tok", [16384, D], BF16, kind="ExternalInput").ap()
    t["expid"] = nc.dram_tensor("expid", [128, 2], F32, kind="ExternalInput").ap()
    t["w1blk"] = nc.dram_tensor("w1blk", [2, 16, 128, 8, 128], F32, kind="ExternalInput").ap()
    t["w2blk"] = nc.dram_tensor("w2blk", [2, 16, 128, 1024], F32, kind="ExternalInput").ap()
    t["b1s"] = nc.dram_tensor("b1s", [2, 128, 16], F32, kind="ExternalInput").ap()
    t["b2r"] = nc.dram_tensor("b2r", [2, 1, 1024], F32, kind="ExternalInput").ap()
    t["iotatok"] = nc.dram_tensor("iotatok", [128, 128], F32, kind="ExternalInput").ap()
    t["slmat"] = nc.dram_tensor("slmat", [128, 128], F32, kind="ExternalInput").ap()
    t["iotaf"] = nc.dram_tensor("iotaf", [128, 128], F32, kind="ExternalInput").ap()
    t["consts"] = nc.dram_tensor("consts", [128, 2], F32, kind="ExternalInput").ap()
    t["identbf"] = nc.dram_tensor("identbf", [128, 128], BF16, kind="ExternalInput").ap()
    t["moe_out"] = nc.dram_tensor("moe_out", [2, CAP, D], BF16, kind="ExternalOutput").ap()
    t["keptslot"] = nc.dram_tensor("keptslot", [2, 2, 128, 128], F32, kind="ExternalOutput").ap()
    with nc.Block() as block:
        T = make_tracks(nc)
        emit_stageD2(nc, t, block, T, upto=upto)
    return nc


def host_inputs_stageD2(d, core, p_vec, e_vec, xn1tok_full):
    import ml_dtypes
    n0 = 2 * core
    W1 = np.asarray(d["L1_ffn_W1"])
    W2 = np.asarray(d["L1_ffn_W2"])
    b1 = np.asarray(d["L1_ffn_b1"])
    b2 = np.asarray(d["L1_ffn_b2"])
    pe = np.concatenate([p_vec.reshape(128, 128), e_vec.reshape(128, 128).astype(np.float32)], 1)
    return {
        "pe_all": pe.astype(np.float32),
        "xn1tok": xn1tok_full.astype(ml_dtypes.bfloat16),
        "expid": np.stack([np.full(128, n0, np.float32), np.full(128, n0 + 1, np.float32)], 1),
        "w1blk": np.stack([blk_lhsT(W1[n0]), blk_lhsT(W1[n0 + 1])]),
        "w2blk": np.stack([W2[n0].reshape(16, 128, 1024), W2[n0 + 1].reshape(16, 128, 1024)]),
        "b1s": np.stack([blk_scales(b1[n0]), blk_scales(b1[n0 + 1])]),
        "b2r": np.stack([b2[n0].reshape(1, 1024), b2[n0 + 1].reshape(1, 1024)]),
        "iotatok": (np.arange(128)[:, None] * 128 + np.arange(128)[None, :]).astype(np.float32),
        "slmat": (np.arange(128)[:, None] < np.arange(128)[None, :]).astype(np.float32),
        "iotaf": np.broadcast_to(np.arange(128, dtype=np.float32), (128, 128)).copy(),
        "consts": np.stack([np.ones(128, np.float32), np.full(128, 1 / 1024, np.float32)], 1),
        "identbf": np.eye(128, dtype=np.float32).astype(ml_dtypes.bfloat16),
    }








def emit_stageE(nc, t, block, T, upto=4):
    ld, tn, sc, ve, st = (T[k] for k in ["ld", "tn", "sc", "ve", "st"])

    o1_sb = nc.alloc_sbuf_tensor("o1_sb", [128, DC * TOK], BF16)
    s1_sb = nc.alloc_sbuf_tensor("s1p_sb", [128, DC * TOK], BF16)
    g1T_sb = nc.alloc_sbuf_tensor("g1T_sb", [128, DC * TOK], BF16)
    xn_tok = nc.alloc_sbuf_tensor("xntok_sb", [128, 16 * D], BF16)
    ks_sb = nc.alloc_sbuf_tensor("ks_sb", [128, 2 * 16 * 16], F32)
    io_sb = nc.alloc_sbuf_tensor("io1280_sb", [128, 16], F32)
    ident = nc.alloc_sbuf_tensor("identE", [128, 128], BF16)
    misc = nc.alloc_sbuf_tensor("miscE", [128, 8], F32)
    offu = nc.alloc_sbuf_tensor("offuE", [128, 16], U32)
    gtok = nc.alloc_sbuf_tensor("gtok", [128, 2 * D], F32)
    gbf = nc.alloc_sbuf_tensor("gbf", [128, 2 * D], BF16)
    g1tok = nc.alloc_sbuf_tensor("g1tok", [128, 2 * D], BF16)
    wbufE = nc.alloc_sbuf_tensor("wbufE", [128, 2 * DC * 128], BF16)
    rt = nc.alloc_sbuf_tensor("rtE", [128, 8], F32)

    psA = nc.alloc_psum_tensor("psAE", [128, 1024], F32)
    psC = nc.alloc_psum_tensor("psCE", [128, 1024], F32)
    pst = nc.alloc_psum_tensor("pstE", [128, 512], BF16)

    psums = [psA[:, 0:512], psA[:, 512:1024], psC[:, 0:512], psC[:, 512:1024]]
    psum_free = {}

    o1_t = lambda c: o1_sb[:, c * TOK:(c + 1) * TOK]
    kept_my = ks_sb[:, 0:256]
    slot_my = ks_sb[:, 256:512]

    @block.sync
    def _(sync):
        for c in range(DC):
            ld.inc(sync.dma_start(out=o1_t(c), in_=t["o1T"][c * 128:(c + 1) * 128, :]), 16)
        ld.inc(sync.dma_start(out=ks_sb[:, 0:256],
                              in_=t["kept_my"].rearrange("p a e -> p (a e)")), 16)
        ld.inc(sync.dma_start(out=ks_sb[:, 256:512],
                              in_=t["slot_my"].rearrange("p a e -> p (a e)")), 16)
        ld.inc(sync.dma_start(out=io_sb[:, :], in_=t["iota1280"]), 16)
        ld.inc(sync.dma_start(out=ident[:, :], in_=t["identbf"]), 16)
        ld.inc(sync.dma_start(out=misc[:, 0:2], in_=t["consts"]), 16)
        for s in range(16):
            ld.inc(sync.dma_start(out=xn_tok[:, s * D:(s + 1) * D],
                                  in_=t["xn1tok"][s * 128:(s + 1) * 128, :]), 16)

    def evac(eng, oc, tt, ps):
        return eng.tensor_copy(s1_sb[:, oc * TOK + tt * 512: oc * TOK + (tt + 1) * 512], ps)

    emit_proj(nc, block, T, t["wproj"], DC, DC, o1_t, wbufE, psums, psum_free,
              evac, "vector", ve, extra_mm_wait=(ld,))

    if upto < 2:
        @block.gpsimd
        def _(gpsimd):
            ve.wait(gpsimd)
            for c in range(DC):
                st.inc(gpsimd.dma_start(out=t["yT"][c * 128:(c + 1) * 128, :],
                                        in_=s1_sb[:, c * TOK:(c + 1) * TOK]), 16)
            st.wait(gpsimd)
        return

    gath_st = {}
    sel_ve = {}
    silu_sc = {}
    for s in range(16):
        par = s % 2

        @block.vector
        def _(vector, s=s, par=par):
            if s == 0:
                ld.wait(vector)
            if s >= 2:
                sc.wait(vector, silu_sc[s - 2])
            km = kept_my[:, s * 16:(s + 1) * 16]
            sm = slot_my[:, s * 16:(s + 1) * 16]
            ve.inc(vector.tensor_tensor(gtok[:, par * D: par * D + 16], sm, io_sb[:, :], ALU.add))
            ve.wait(vector)
            ve.inc(vector.tensor_tensor(gtok[:, par * D: par * D + 16],
                                        gtok[:, par * D: par * D + 16], km, ALU.mult))
            ve.wait(vector)
            ve.inc(vector.reduce_sum(rt[:, 0:1], gtok[:, par * D: par * D + 16], axis=AX.X))
            ve.wait(vector)
            ve.inc(vector.reduce_sum(rt[:, 1:2], km, axis=AX.X))
            ve.wait(vector)
            ve.inc(vector.tensor_copy(offu[:, s:s + 1], rt[:, 0:1]))
            sel_ve[("off", s)] = ve.n

        gpt = [T["gp"], T["gp2"]]

        @block.gpsimd
        def _(gpsimd, s=s, par=par):
            ve.wait(gpsimd, sel_ve[("off", s)])
            gpt[par].inc(gpsimd.indirect_dma_start(
                out=gbf[:, par * D:(par + 1) * D], out_offset=None,
                in_=t["moe_all"],
                in_offset=IndirectOffsetOnAxis(ap=offu[:, s:s + 1], axis=0)), 16)
            gath_st[s] = gpt[par].n

        @block.vector
        def _(vector, s=s, par=par):
            gpt[par].wait(vector, gath_st[s])
            ve.inc(vector.tensor_tensor(gtok[:, par * D:(par + 1) * D],
                                        gbf[:, par * D:(par + 1) * D],
                                        xn_tok[:, s * D:(s + 1) * D], ALU.subtract))
            ve.wait(vector)
            ve.inc(vector.tensor_scalar(out=gtok[:, par * D:(par + 1) * D],
                                        in0=gtok[:, par * D:(par + 1) * D],
                                        scalar1=rt[:, 1:2], scalar2=None, op0=ALU.mult))
            ve.wait(vector)
            ve.inc(vector.tensor_tensor(gtok[:, par * D:(par + 1) * D],
                                        gtok[:, par * D:(par + 1) * D],
                                        xn_tok[:, s * D:(s + 1) * D], ALU.add))
            sel_ve[s] = ve.n

        @block.scalar
        def _(scalar, s=s, par=par):
            ve.wait(scalar, sel_ve[s])
            sc.inc(scalar.activation(out=g1tok[:, par * D:(par + 1) * D],
                                     in_=gtok[:, par * D:(par + 1) * D], func=AF.Silu))
            silu_sc[s] = sc.n

        import os as _os
        if upto < 3 or s >= int(_os.environ.get("TLIM", "16")):
            continue
        for c in range(DC):
            u = s * DC + c

            @block.tensor
            def _(tensor, s=s, c=c, u=u, par=par):
                sc.wait(tensor, silu_sc[s])
                if u >= 2:
                    g_trk, g_n = sel_ve[("tp", u - 2)]
                    sc.wait(tensor, g_n)
                mm = tensor.transpose(pst[:, (u % 2) * 128:(u % 2) * 128 + 128],
                                      g1tok[:, par * D + c * 128: par * D + (c + 1) * 128],
                                      ident[:, :])
                tn.inc(mm)
                sel_ve[("tptn", u)] = tn.n

            @block.scalar
            def _(scalar, s=s, c=c, u=u):
                tn.wait(scalar, sel_ve[("tptn", u)])
                sc.inc(scalar.copy(
                    out=g1T_sb[:, c * TOK + s * 128: c * TOK + (s + 1) * 128],
                    in_=pst[:, (u % 2) * 128:(u % 2) * 128 + 128]))
                sel_ve[("tp", u)] = ("sc", sc.n)

    if upto < 4:
        @block.gpsimd
        def _(gpsimd):
            ve.wait(gpsimd)
            sc.wait(gpsimd)
            for c in range(DC):
                st.inc(gpsimd.dma_start(out=t["yT"][c * 128:(c + 1) * 128, :],
                                        in_=g1T_sb[:, c * TOK:(c + 1) * TOK] if upto == 3 else s1_sb[:, c * TOK:(c + 1) * TOK]), 16)
            st.wait(gpsimd)
            T["gp"].wait(gpsimd)
            T["gp2"].wait(gpsimd)
        return

    with (
        nc.sbuf_tensor("x1stg", [128, 2 * TOK], F32) as x1stg,
        nc.sbuf_tensor("fstg", [128, 2 * 2 * TOK], F32) as fstg,
    ):
        ldt = [T["ld"], T["ld2"]]
        ld_after = {}

        @block.sync
        def _(sync):
            ldt[0].wait(sync)
            ldt[1].wait(sync)
            for c in range(2):
                ldt[c % 2].inc(sync.dma_start(out=x1stg[:, (c % 2) * TOK:((c % 2) + 1) * TOK],
                                              in_=t["x1T"][c * 128:(c + 1) * 128, :]), 16)
                ld_after[c] = ldt[c % 2].n

        fin_ve = {}
        for c in range(DC):
            par = c % 2
            sf = fstg[:, par * 2 * TOK: par * 2 * TOK + TOK]
            gf = fstg[:, par * 2 * TOK + TOK: par * 2 * TOK + 2 * TOK]

            @block.scalar
            def _(scalar, c=c, par=par, sf=sf, gf=gf):
                if c < 2:
                    ve.wait(scalar)
                    sc.wait(scalar)
                ldt[par].wait(scalar, ld_after[c])
                scalar.copy(out=sf, in_=s1_sb[:, c * TOK:(c + 1) * TOK])
                sc.inc(scalar.copy(out=gf, in_=g1T_sb[:, c * TOK:(c + 1) * TOK]))
                fin_ve[("sc", c)] = sc.n

            @block.vector
            def _(vector, c=c, par=par, sf=sf, gf=gf):
                sc.wait(vector, fin_ve[("sc", c)])
                ve.inc(vector.tensor_tensor(sf, sf, gf, ALU.mult))
                ve.wait(vector)
                ve.inc(vector.tensor_tensor(x1stg[:, par * TOK:(par + 1) * TOK],
                                            x1stg[:, par * TOK:(par + 1) * TOK], sf, ALU.add))
                fin_ve[c] = ve.n

            stt = [T["st"], T["st2"]]

            @block.gpsimd
            def _(gpsimd, c=c, par=par):
                ve.wait(gpsimd, fin_ve[c])
                stt[par].inc(gpsimd.dma_start(out=t["yT"][c * 128:(c + 1) * 128, :],
                                              in_=x1stg[:, par * TOK:(par + 1) * TOK]), 16)
                fin_ve[("st", c)] = stt[par].n

            nxt = c + 2
            if nxt < DC:
                @block.sync
                def _(sync, nxt=nxt, par=par, c=c):
                    stt[par].wait(sync, fin_ve[("st", c)])
                    ldt[par].inc(sync.dma_start(out=x1stg[:, par * TOK:(par + 1) * TOK],
                                                in_=t["x1T"][nxt * 128:(nxt + 1) * 128, :]), 16)
                    ld_after[nxt] = ldt[par].n

    @block.gpsimd
    def _(gpsimd):
        st.wait(gpsimd)
        T["st2"].wait(gpsimd)
        T["gp"].wait(gpsimd)
        T["gp2"].wait(gpsimd)


def build_stageE(upto=4):
    nc = bass.Bass("TRN2", target_bir_lowering=False, debug=False)
    t = {}
    t["x1T"] = nc.dram_tensor("x1T", [D, TOK], F32, kind="ExternalInput").ap()
    t["o1T"] = nc.dram_tensor("o1T", [D, TOK], BF16, kind="ExternalInput").ap()
    t["wproj"] = nc.dram_tensor("wproj", [DC, 128, DC, 128], F32, kind="ExternalInput").ap()
    t["moe_all"] = nc.dram_tensor("moe_all", [16 * 1280, D], BF16, kind="ExternalInput").ap()
    t["xn1tok"] = nc.dram_tensor("xn1tok", [TOK, D], BF16, kind="ExternalInput").ap()
    t["kept_my"] = nc.dram_tensor("kept_my", [128, 16, 16], F32, kind="ExternalInput").ap()
    t["slot_my"] = nc.dram_tensor("slot_my", [128, 16, 16], F32, kind="ExternalInput").ap()
    t["iota1280"] = nc.dram_tensor("iota1280", [128, 16], F32, kind="ExternalInput").ap()
    t["identbf"] = nc.dram_tensor("identbf", [128, 128], BF16, kind="ExternalInput").ap()
    t["consts"] = nc.dram_tensor("consts", [128, 2], F32, kind="ExternalInput").ap()
    t["yT"] = nc.dram_tensor("yT", [D, TOK], F32, kind="ExternalOutput").ap()
    with nc.Block() as block:
        T = make_tracks(nc)
        emit_stageE(nc, t, block, T, upto=upto)
    return nc






_PROGS = {}


def _progs():
    if not _PROGS:
        _PROGS["A"] = build_stageA()
        _PROGS["B"] = build_stageB()
        _PROGS["C"] = build_stageC()
        _PROGS["D1"] = build_stageD1()
        _PROGS["D2"] = build_stageD2()
        _PROGS["E"] = build_stageE()
    return _PROGS


def _launch(nc, in_maps):
    return run_bass_kernel_spmd(nc, in_maps, core_ids=list(range(8))).results


def kernel(**inputs):
    d = {k: np.asarray(v) for k, v in inputs.items()}
    P = _progs()
    NC = 8

    resA = _launch(P["A"], [host_inputs_stageA(d, c) for c in range(NC)])

    qT_full = np.stack([np.concatenate([resA[4 * b + i]["qT_o"] for i in range(4)], axis=1)
                        for b in range(2)])
    kT_full = np.stack([np.concatenate([resA[4 * b + i]["kT_o"] for i in range(4)], axis=1)
                        for b in range(2)])
    vv_full = np.stack([np.concatenate([resA[4 * b + i]["vv_o"] for i in range(4)], axis=0)
                        for b in range(2)])
    resB = _launch(P["B"], [host_inputs_stageB(c, qT_full, kT_full, vv_full)
                            for c in range(NC)])

    s0_full = np.empty((16384, 1024), np.float32)
    for b in range(2):
        sT = np.concatenate([resB[4 * b + i]["s0h"].reshape(256, 8192)
                             for i in range(4)], axis=0)
        s0_full[b * 8192:(b + 1) * 8192] = sT.astype(np.float32).T
    g0_full = np.empty((16384, 1024), np.float32)
    for c in range(NC):
        g0_full[c * TOK:(c + 1) * TOK] = resA[c]["g0T"].T
    x0_full = d["L0_attn_x"].reshape(16384, 1024)
    resC = _launch(P["C"], [host_inputs_stageC(d, c, s0_full, g0_full, x0_full)
                            for c in range(NC)])

    q1_full = np.stack([np.concatenate([resC[4 * b + i]["q1T"] for i in range(4)], axis=1)
                        for b in range(2)])
    k1_full = np.stack([np.concatenate([resC[4 * b + i]["k1T"] for i in range(4)], axis=1)
                        for b in range(2)])
    v1_full = np.stack([np.concatenate([resC[4 * b + i]["vv1"] for i in range(4)], axis=0)
                        for b in range(2)])
    resD1 = _launch(P["D1"], [host_inputs_stageD1(c, q1_full, k1_full, v1_full)
                              for c in range(NC)])

    p_vec = np.concatenate([resC[c]["pe_o"][:, 0:16].T.reshape(-1) for c in range(NC)])
    e_vec = np.concatenate([resC[c]["pe_o"][:, 16:32].T.reshape(-1) for c in range(NC)])
    xn1tok_full = np.concatenate([resC[c]["xn1tok"] for c in range(NC)], axis=0)
    resD2 = _launch(P["D2"], [host_inputs_stageD2(d, c, p_vec, e_vec, xn1tok_full)
                              for c in range(NC)])

    moe_all = np.concatenate([resD2[c]["moe_out"].reshape(2 * 1280, 1024)
                              for c in range(NC)], axis=0)
    kept_full = np.empty((16, 16384), np.float32)
    slot_full = np.empty((16, 16384), np.float32)
    for c in range(NC):
        for j in range(2):
            kept_full[2 * c + j] = resD2[c]["keptslot"][j, 0].reshape(-1)
            slot_full[2 * c + j] = resD2[c]["keptslot"][j, 1].reshape(-1)
    o1_fullT = [np.concatenate([resD1[4 * b + i]["s1h"].reshape(256, 8192)
                                for i in range(4)], axis=0) for b in range(2)]
    wproj_blk = blk_lhsT(d["L1_attn_w_proj"].T)
    iota1280 = np.broadcast_to(np.arange(16, dtype=np.float32) * 1280, (128, 16)).copy()
    identbf = np.eye(128, dtype=np.float32).astype(ml_dtypes.bfloat16)
    consts = np.stack([np.ones(128, np.float32), np.full(128, 1 / 1024, np.float32)], 1)
    in_mapsE = []
    for c in range(NC):
        b = c // 4
        sl = slice(c * TOK, (c + 1) * TOK)
        km = kept_full[:, sl].T.reshape(16, 128, 16).transpose(1, 0, 2)
        sm = slot_full[:, sl].T.reshape(16, 128, 16).transpose(1, 0, 2)
        in_mapsE.append({
            "x1T": resC[c]["x1T"],
            "o1T": np.ascontiguousarray(o1_fullT[b][:, (c % 4) * TOK:(c % 4 + 1) * TOK]),
            "wproj": wproj_blk,
            "moe_all": moe_all,
            "xn1tok": resC[c]["xn1tok"],
            "kept_my": np.ascontiguousarray(km),
            "slot_my": np.ascontiguousarray(sm),
            "iota1280": iota1280,
            "identbf": identbf,
            "consts": consts,
        })
    resE = _launch(P["E"], in_mapsE)

    y = np.empty((16384, 1024), np.float32)
    for c in range(NC):
        y[c * TOK:(c + 1) * TOK] = resE[c]["yT"].T
    return y.reshape(2, 8192, 1024)
